# version 02
# baseline (speedup 1.0000x reference)
import numpy as np
from collections import defaultdict
from contextlib import ExitStack

import concourse.bass as bass
import concourse.mybir as mybir
from concourse.bass_utils import run_bass_kernel_spmd

ENGINES = ("t", "v", "s", "g", "y")


class Op:
    __slots__ = ("eng", "fn", "reads", "writes", "scalar_reads", "is_dma",
                 "idx", "dma_idx", "deps", "inc", "tag", "deferred", "is_cc")

    def __init__(self, eng, fn, reads, writes, scalar_reads, is_dma, tag,
                 deferred=False, is_cc=False):
        self.eng = eng
        self.fn = fn
        self.reads = reads
        self.writes = writes
        self.scalar_reads = scalar_reads
        self.is_dma = is_dma
        self.idx = -1
        self.dma_idx = -1
        self.deps = []
        self.inc = False
        self.tag = tag
        self.deferred = deferred
        self.is_cc = is_cc


class Prog:
    def __init__(self):
        self.small_res = set()
        self.ops = {e: [] for e in ENGINES}
        self.all_ops = []
        self.res = defaultdict(lambda: [None, []])
        self.ndma = {e: 0 for e in ENGINES}

    def op(self, eng, fn, reads=(), writes=(), scalar_reads=(), tag="",
           deferred=False, is_cc=False):
        if not deferred and (set(writes) & self.small_res):
            deferred = True
        o = Op(eng, fn, tuple(reads) + tuple(scalar_reads), tuple(writes),
               tuple(scalar_reads), False, tag, deferred, is_cc)
        self._record(o)
        return o

    def dma(self, eng, fn, reads=(), writes=(), tag=""):
        assert eng in ("g", "y", "s", "v")
        o = Op(eng, fn, tuple(reads), tuple(writes), (), True, tag)
        o.dma_idx = self.ndma[eng]
        self.ndma[eng] += 1
        self._record(o)
        return o

    def _record(self, o):
        o.idx = len(self.ops[o.eng])
        self.ops[o.eng].append(o)
        self.all_ops.append(o)
        deps = []
        for r in o.reads:
            w, _readers = self.res[r]
            if w is not None:
                deps.append(w)
            self.res[r][1].append(o)
        for w_ in o.writes:
            lw, readers = self.res[w_]
            if lw is not None:
                deps.append(lw)
            deps.extend(readers)
            self.res[w_][0] = o
            self.res[w_][1] = []
        seen = set()
        for d in deps:
            if d is o:
                continue
            if id(d) in seen:
                continue
            seen.add(id(d))
            if d.eng == o.eng and not d.is_dma and not d.is_cc:
                continue
            o.deps.append(d)

    def finalize(self, nc, engines, sems, dma_sems, verbose=False):
        for o in self.all_ops:
            if o.is_cc:
                o.inc = True
            for d in o.deps:
                if not d.is_dma:
                    d.inc = True
        inc_count = {}
        for e in ENGINES:
            c = 0
            for o in self.ops[e]:
                if o.inc and not o.is_dma:
                    c += 1
                inc_count[id(o)] = c
        plans = {e: [] for e in ENGINES}
        for e in ENGINES:
            last_wait = defaultdict(int)
            pend_w = set()
            pend_r = set()
            for o in self.ops[e]:
                waits = []
                for d in o.deps:
                    if d.is_dma:
                        key = ("d", d.eng, d.dma_idx % 8)
                        val = 16 * (d.dma_idx // 8 + 1)
                    else:
                        key = ("c", d.eng)
                        val = inc_count[id(d)]
                        assert val > 0
                    if val > last_wait[key]:
                        last_wait[key] = val
                        waits.append((key, val))
                rd, wr = set(o.reads), set(o.writes)
                need_gap = bool((rd & pend_w) or (wr & pend_w) or (wr & pend_r))
                if need_gap:
                    pend_w.clear()
                    pend_r.clear()
                pend_w.update(wr)
                pend_r.update(rd)
                plans[e].append((o, waits, need_gap))
        self._plans = plans
        self._inc_count = inc_count
        return plans

    def emit(self, e, eng_obj, sems, dma_sems):
        for o, waits, need_gap in self._plans[e]:
            for key, val in waits:
                if key[0] == "d":
                    eng_obj.wait_ge(dma_sems[key[1]][key[2]], val)
                else:
                    eng_obj.wait_ge(sems[key[1]], val)
            if need_gap:
                eng_obj.drain()
            if o.is_dma:
                slot = o.dma_idx % 8
                rnd = o.dma_idx // 8
                if rnd > 0:
                    eng_obj.wait_ge(dma_sems[e][slot], 16 * rnd)
                inst = o.fn(eng_obj)
                inst.then_inc(dma_sems[e][slot], 16)
            else:
                inst = o.fn(eng_obj)
                if o.inc:
                    if inst is None:
                        eng_obj.nop().then_inc(sems[e], 1)
                    else:
                        inst.then_inc(sems[e], 1)

    def stats(self):
        return {e: (len(self.ops[e]), sum(1 for o in self.ops[e] if o.inc),
                    self.ndma[e]) for e in ENGINES}


F32 = mybir.dt.float32
BF16 = mybir.dt.bfloat16
I16 = mybir.dt.int16
ALU = mybir.AluOpType
ACT = mybir.ActivationFunctionType
AX = mybir.AxisListType

D = 1024
NH = 16
HD = 64
C = 64
S = 2048
NSTEP = S // C
NSEQ = 2
IFF = 3072
NTOK = NSEQ * S
MAGIC = float(1.5 * 2 ** 23)
BIGNEG = -1.0e30
QMAX = 32767.0
EPS = 1e-5

def host_constants():
    cs = {}
    cs["c_id128f"] = np.eye(128, dtype=np.float32)
    cs["c_id128b"] = np.eye(128, dtype=np.float32)
    triu = np.triu(np.ones((C, C), np.float32))
    z = np.zeros((C, C), np.float32)
    cs["c_tril2"] = np.block([[triu, z], [z, triu]])
    sel = np.zeros((128, 128), np.float32)
    sel[C - 1, 0:C] = 1.0
    sel[127, C:128] = 1.0
    cs["c_selC"] = sel
    q = np.arange(128)
    s_in = q % C
    par_row = (q // C)
    t_idx = np.arange(C)
    dind = (s_in[:, None] == t_idx[None, :]).astype(np.float32)
    blkN = np.zeros((128, 2 * C), np.float32)
    blkN[:C, :C] = dind[:C]
    blkN[C:, C:] = dind[C:]
    cs["c_dindN"] = np.tile(blkN, (1, NH))
    cs["c_dindW"] = np.tile(dind, (1, NH))
    okN = np.zeros((128, 2 * C), np.float32) + BIGNEG
    okN[:C, :C] = np.where(s_in[:C, None] < t_idx[None, :], 0.0, BIGNEG)
    okN[C:, C:] = np.where(s_in[C:, None] < t_idx[None, :], 0.0, BIGNEG)
    cs["c_negN"] = np.tile(okN, (1, NH))
    okW = np.where(s_in[:, None] <= t_idx[None, :], 0.0, BIGNEG)
    cs["c_negW"] = np.tile(okW, (1, NH))
    ind = np.zeros((16, 8 * 128), np.float32)
    for ci in range(8):
        ind[2 * ci, ci * 128: ci * 128 + 64] = 1.0
        ind[2 * ci + 1, ci * 128 + 64: ci * 128 + 128] = 1.0
    cs["c_indh"] = ind
    cs["c_ones2"] = np.block([[np.ones((C, C), np.float32), z],
                              [z, np.ones((C, C), np.float32)]])
    okT = np.zeros((128, 2 * C), np.float32) + BIGNEG
    okT[:C, :C] = np.where(t_idx[None, :] < s_in[:C, None], 0.0, BIGNEG)
    okT[C:, C:] = np.where(t_idx[None, :] < s_in[C:, None], 0.0, BIGNEG)
    cs["c_negNT"] = np.tile(okT, (1, NH))
    i1 = np.zeros((16, 1024), np.float32)
    for ci in range(8):
        for pp in range(128):
            i1[2 * ci + pp // 64, ci * 128 + pp] = 1.0
    cs["c_indh1"] = i1
    s4 = np.zeros((128, 512), np.float32)
    for e in range(2):
        for sq in range(2):
            for pp in range(128):
                if pp // 64 == e:
                    s4[63 + 64 * sq, (e * 2 + sq) * 128 + pp] = 1.0
    cs["c_sel4"] = s4
    return cs


class KB:

    def __init__(self, nc, stage="full", dbg=()):
        self.nc = nc
        self.p = Prog()
        self.stage = stage
        self.dbg = set(dbg)
        self.es = ExitStack()
        self.scope = None
        self.T = {}
        self.dram = {}
        self.dbg_outs = {}

    def push_scope(self):
        self.scope = ExitStack()

    def pop_scope(self):
        self.scope.close()
        self.scope = None

    def barrier(self):
        p = self.p
        lasts = {}
        for e in ("t", "v", "s", "g", "y"):
            real = [op for op in p.ops[e] if not op.is_dma]
            lasts[e] = real[-1] if real else None
        dlasts = {}
        for e in ("t", "v", "s", "g", "y"):
            per_slot = {}
            for op in p.ops[e]:
                if op.is_dma:
                    per_slot[op.dma_idx % 8] = op
            dlasts[e] = list(per_slot.values())
        for e in ("t", "v", "s", "g", "y"):
            o = p.op(e, lambda eng: eng.nop(), tag="barr")
            for x in ("t", "v", "s", "g", "y"):
                if x != e and lasts[x] is not None:
                    o.deps.append(lasts[x])
                o.deps.extend(dlasts[x])

    def sb(self, name, shape, dt):
        es = self.scope if self.scope is not None else self.es
        self.T[name] = es.enter_context(self.nc.sbuf_tensor(name, shape, dt))
        import numpy as _np
        if int(_np.prod(shape[1:])) <= 64:
            self.p.small_res.add(name)
        return self.T[name]

    def ps(self, name, shape):
        self.T[name] = self.es.enter_context(self.nc.psum_tensor(name, shape, F32))
        return self.T[name]

    def din(self, name, shape, dt=F32):
        self.dram[name] = self.nc.dram_tensor(name, shape, dt, kind="ExternalInput")
        return self.dram[name]

    def dout(self, name, shape, dt=F32):
        self.dram[name] = self.nc.dram_tensor(name, shape, dt, kind="ExternalOutput")
        return self.dram[name]

    def dtmp(self, name, shape, dt=F32):
        self.dram[name] = self.nc.dram_tensor(name, shape, dt)
        return self.dram[name]

    def dbg_out(self, key, shape, dt=F32):
        if key in self.dbg:
            self.dbg_outs[key] = self.dout("dbg_" + key, shape, dt)
            return self.dram["dbg_" + key].ap()
        return None

    def dump(self, key, tile_ap, reads):
        if key in self.dbg_outs:
            d = self.dram["dbg_" + key].ap()
            self.p.dma("g", lambda g, d=d, t=tile_ap: g.dma_start(out=d, in_=t),
                       reads=reads, tag="dbg_" + key)

    def op(self, *a, **k):
        return self.p.op(*a, **k)

    def dma(self, *a, **k):
        return self.p.dma(*a, **k)


def build_attn(kb, ncores_note=""):
    nc, p, T = kb.nc, kb.p, kb.T
    x_d = kb.din("x", [NSEQ, S, D]).ap()
    Wnames = ["Wq", "Wk", "Wv", "Wg", "Wo"]
    for w in Wnames:
        kb.din(w, [D, D])
    kb.din("Wa", [D, NH]); kb.din("Wb", [D, NH])
    kb.din("i_ba_bc", [128, NH]); kb.din("i_bb_bc", [128, NH])
    kb.din("i_gnorm_bc", [128, D])
    for cname, cshape, cdt in [
        ("c_id128f", [128, 128], F32), ("c_id128b", [128, 128], F32),
        ("c_tril2", [128, 128], F32), ("c_selC", [128, 128], F32),
        ("c_dindN", [128, 2048], F32), ("c_dindW", [128, 1024], F32),
        ("c_negN", [128, 2048], F32), ("c_negW", [128, 1024], F32),
        ("c_indh1", [16, 1024], F32), ("c_ones2", [128, 128], F32),
        ("c_negNT", [128, 2048], F32), ("c_sel4", [128, 512], F32),
    ]:
        kb.din("i_" + cname, cshape, cdt)
    x2_d = kb.dtmp("x2_d", [NSEQ, S, D]).ap()
    x2ln_d = kb.dtmp("x2ln_d", [NSEQ, S, D]).ap()

    for cn in ["id128f"]:
        pass
    kb.sb("c_id128f", [128, 128], F32)
    kb.sb("c_id128b", [128, 128], BF16)
    for nm in ["sum1", "mn", "ssq", "sd", "rstd", "bias2", "m2", "r2", "b2",
               "m3", "r3", "b3", "mx1", "eps1", "magicC"]:
        kb.sb(nm, [128, 1], F32)
    kb.sb("mxacc", [128, 1], F32)
    kb.push_scope()
    for w in Wnames:
        kb.sb("w_" + w, [128, 8 * 1024], BF16)
    kb.sb("w_ab", [128, 8 * 32], BF16)
    kb.sb("ba_bc", [128, NH], F32); kb.sb("bb_bc", [128, NH], F32)
    kb.sb("gnorm_bc", [128, D], F32)
    consts_b = [("dindN", BF16), ("dindW", BF16),
                ("negN", BF16), ("negW", BF16), ("ones2", BF16), ("negNT", BF16)]
    for cn, cdt in consts_b:
        kb.sb("c_" + cn, [128, {"dindN": 2048, "negN": 2048, "negNT": 2048,
                                "dindW": 1024, "negW": 1024}.get(cn, 128)], cdt)
    for cn in ["tril2", "selC"]:
        kb.sb("c_" + cn, [128, 128], F32)
    kb.sb("c_indh1", [16, 1024], F32)
    kb.sb("c_sel4", [128, 512], F32)
    kb.sb("scT3", [16, 128], F32)
    kb.sb("dN2", [128, 2048], BF16)
    kb.sb("x2ln_c", [128, D], F32)
    kb.sb("e_lC_st", [128, 16], F32)

    kb.sb("xc", [128, D], F32)
    kb.sb("xln_tm", [128, D], BF16)
    kb.sb("xln_fm", [128, D], BF16)
    kb.sb("q_fm", [128, D], BF16); kb.sb("k_fm", [128, D], BF16)
    kb.sb("qh_fm", [128, D], BF16)
    kb.sb("q_tm", [128, D], BF16); kb.sb("k_tm", [128, D], BF16)
    kb.sb("v_tm", [128, D], BF16)
    kb.sb("g_tm", [128, D], F32)
    kb.sb("R", [128, 2048], BF16)
    kb.sb("kt_tm", [128, D], BF16)
    kb.sb("scrD", [128, D], F32)
    kb.sb("scrD2", [128, D], F32)
    kb.sb("dN", [128, 2048], BF16)
    kb.sb("nbar", [128, 2048], BF16)
    kb.sb("pbar", [128, 2048], BF16)
    kb.sb("dW", [128, 1024], BF16)
    kb.sb("wbar", [128, 1024], BF16)
    kb.sb("Xc", [128, 2048], BF16)
    kb.sb("xkT", [128, 1024], BF16)
    kb.sb("gt", [128, 1024], BF16)
    kb.sb("y_tm", [128, D], F32)
    kb.sb("yf_fm", [128, D], BF16)
    kb.sb("negS", [128, 1024], F32)
    kb.sb("negS_b", [128, 1024], BF16)
    for nm in ["zat", "zbt", "alpha", "beta", "loga", "logb", "ell", "ellprev",
               "glt", "ellC", "e_l", "e_lp", "e_ct", "e_lC", "d1",
               "lhi", "llo", "nlhi", "nllo", "ghi", "glo", "negell",
               "ssq_q", "ssq_k", "nq", "nk", "rn_q", "rn_k",
               "skh", "skh2", "sct", "sq2", "sq2n", "lnrnq", "lnrnk",
               "biasWN", "nlt", "wlt", "whi", "wlo", "bhi", "blo"]:
        kb.sb(nm, [128, NH], F32)
    kb.sb("hilo16", [128, NH], BF16)
    for nm in ["scTq", "scTqh", "scTk"]:
        kb.sb(nm, [16, 128], F32)
    kb.sb("sc3", [128, 64], F32)
    kb.ps("psA", [128, 1024])
    kb.ps("psB", [128, 1024])
    kb.ps("psC", [128, 2048])
    psum_init(kb)

    for w in Wnames:
        wd = kb.dram[w].ap().rearrange("(i p) o -> p i o", p=128)
        p.dma("g", lambda g, o=T["w_" + w], i=wd: g.dma_start(out=o[:].rearrange(
            "p (i o) -> p i o", i=8), in_=i), writes=["w_" + w], tag="ldw")
    wa = kb.dram["Wa"].ap().rearrange("(i p) o -> p i o", p=128)
    wb = kb.dram["Wb"].ap().rearrange("(i p) o -> p i o", p=128)
    p.dma("g", lambda g, i=wa: g.dma_start(
        out=T["w_ab"][:].rearrange("p (i o) -> p i o", i=8)[:, :, 0:16], in_=i),
        writes=["w_ab"], tag="ldab")
    p.dma("g", lambda g, i=wb: g.dma_start(
        out=T["w_ab"][:].rearrange("p (i o) -> p i o", i=8)[:, :, 16:32], in_=i),
        writes=["w_ab"], tag="ldab")
    for nm in ["ba_bc", "bb_bc", "gnorm_bc"]:
        p.dma("y", lambda y, o=T[nm], i=kb.dram["i_" + nm].ap(): y.dma_start(out=o[:], in_=i),
              writes=[nm], tag="ldc")
    for cn, cdt in consts_b + [("id128b", BF16)]:
        p.dma("g", lambda g, o=T["c_" + cn], i=kb.dram["i_c_" + cn].ap():
              g.dma_start(out=o[:], in_=i), writes=["c_" + cn], tag="ldc")
    for cn in ["id128f", "tril2", "selC", "indh1", "sel4"]:
        p.dma("y", lambda y, o=T["c_" + cn], i=kb.dram["i_c_" + cn].ap():
              y.dma_start(out=o[:], in_=i), writes=["c_" + cn], tag="ldc")
    p.op("v", lambda v: v.memset(T["eps1"][:], EPS), writes=["eps1"])
    p.op("v", lambda v: v.memset(T["magicC"][:], MAGIC), writes=["magicC"])
    p.op("v", lambda v: v.memset(T["negS"][:], 0.0), writes=["negS"])
    p.op("v", lambda v: v.memset(T["negS_b"][:], 0.0), writes=["negS_b"])
    p.op("v", lambda v: v.memset(T["mxacc"][:], 0.0), writes=["mxacc"])

    for key, shp, dt in [("qh_fm", [128, D], F32), ("zat", [128, NH], F32),
                         ("scT3", [16, 128], F32), ("sq2n", [128, NH], F32),
                         ("qfm", [128, D], F32),
                         ("alpha", [128, NH], F32), ("loga", [128, NH], F32),
                         ("xlntm", [128, D], F32), ("xlnfm", [128, D], F32),
                         ("mn", [128, 1], F32), ("sum1", [128, 1], F32),
                         ("rstd", [128, 1], F32),
                         ("ssqd", [128, 1], F32), ("xcc", [128, D], F32),
                         ("R", [128, 2048], F32), ("ell", [128, NH], F32),
                         ("glt", [128, NH], F32), ("kt", [128, D], F32),
                         ("nbar", [128, 2048], F32), ("wbar", [128, 1024], F32),
                         ("Xs", [128, 2048], F32), ("gt", [128, 1024], F32),
                         ("y0", [128, D], F32), ("y1", [128, D], F32),
                         ("x2_0", [128, D], F32), ("x2ln_0", [128, D], F32)]:
        kb.dbg_out(key, shp, dt)

    nsteps = 1 if kb.stage in ("A1", "B1", "M1", "S1", "S0", "Sg", "SZ", "SA", "SB") else NSTEP
    for step in range(nsteps):
        emit_attn_step(kb, step, x_d, x2_d, x2ln_d)
        emit_attn_step2(kb, step)
        if kb.stage in ("A",):
            continue
        emit_attn_step3(kb, step)
        if kb.stage in ("M1", "M"):
            continue
        emit_attn_step4(kb, step, x2_d, x2ln_d)
    kb.pop_scope()
    return x2_d, x2ln_d


def _ln_ops(kb, src_res, src_ap_fn, out_res, out_ap, mn_t, r_t, b_t,
            out_dt_note="", src_is_psum=False, tag="ln", scratch="scrD2",
            in_scale=None, in_scale_sq=None):
    p, T = kb.p, kb.T
    sumr = ["sum1"]
    p.op("v", lambda v: v.tensor_reduce(out=T["sum1"][:], in_=src_ap_fn(T),
         axis=AX.X, op=ALU.add), reads=[src_res], writes=["sum1"],
         tag=tag + "_sum", deferred=True)
    p.op("v", lambda v: v.tensor_scalar(out=T[mn_t][:], in0=T["sum1"][:],
         scalar1=-1.0 / D, scalar2=None, op0=ALU.mult),
         reads=["sum1"], writes=[mn_t], tag=tag + "_mn")
    p.op("s", lambda s: s.activation(out=T[scratch][:], in_=src_ap_fn(T),
         func=ACT.Square, bias=T[mn_t][:, 0:1], scale=1.0, accum_out=T["ssq"][:]),
         reads=[src_res], writes=[scratch, "ssq"], scalar_reads=[mn_t],
         tag=tag + "_sq", deferred=True)
    if in_scale is None:
        p.op("s", lambda s: s.activation(out=T["sd"][:], in_=T["ssq"][:],
             func=ACT.Sqrt, bias=T["eps1"][:, 0:1], scale=1.0 / D),
             reads=["ssq", "eps1"], writes=["sd"], tag=tag + "_sd")
    else:
        p.op("s", lambda s: s.activation(out=T["sd"][:], in_=T["ssq"][:],
             func=ACT.Sqrt, bias=T["eps1"][:, 0:1],
             scale=T[in_scale_sq][:, 0:1]),
             reads=["ssq", "eps1"], writes=["sd"],
             scalar_reads=[in_scale_sq], tag=tag + "_sd")
    p.op("v", lambda v: v.reciprocal(out=T[r_t][:], in_=T["sd"][:]),
         reads=["sd"], writes=[r_t], tag=tag + "_r", deferred=True)
    if in_scale is not None:
        p.op("v", lambda v: v.tensor_tensor(out=T[r_t][:], in0=T[r_t][:],
             in1=T[in_scale][:], op=ALU.mult), reads=[r_t, in_scale],
             writes=[r_t], tag=tag + "_rs", deferred=True)
    p.op("v", lambda v: v.tensor_tensor(out=T[b_t][:], in0=T[mn_t][:],
         in1=T[r_t][:], op=ALU.mult), reads=[mn_t, r_t], writes=[b_t],
         tag=tag + "_b", deferred=True)
    p.op("s", lambda s: s.activation(out=out_ap(T), in_=src_ap_fn(T),
         func=ACT.Identity, scale=T[r_t][:, 0:1], bias=T[b_t][:, 0:1]),
         reads=[src_res], writes=[out_res], scalar_reads=[r_t, b_t],
         tag=tag + "_app")


def emit_attn_step(kb, step, x_d, x2_d, x2ln_d):
    p, T = kb.p, kb.T
    r0, r1 = step * C, (step + 1) * C
    dbg = (step == 0)

    for sq in range(NSEQ):
        p.dma("y", lambda y, sq=sq: y.dma_start(
            out=T["xc"][sq * C:(sq + 1) * C, :], in_=x_d[sq, r0:r1, :]),
            writes=["xc"], tag=f"ldx{step}")

    _ln_ops(kb, "xc", lambda T: T["xc"][:], "xln_tm",
            lambda T: T["xln_tm"][:], "mn", "rstd", "bias2", tag=f"ln1_{step}")

    def tr8(t, src="xln_tm"):
        pb = T["psA"][:].bitcast(BF16)
        for ci in range(8):
            inst = t.transpose(pb[:, ci * 128:(ci + 1) * 128],
                               T[src][:, ci * 128:(ci + 1) * 128], T["c_id128b"][:])
        return inst
    p.op("t", lambda t: tr8(t), reads=["xln_tm", "c_id128b"],
         writes=["psA"], tag=f"trx{step}")
    p.op("v", lambda v: v.tensor_copy(out=T["xln_fm"][:],
         in_=T["psA"][:].bitcast(BF16)[:, 0:1024]),
         reads=["psA"], writes=["xln_fm"], tag=f"trxe{step}")

    def proj_fm(t, wname):
        for j in range(8):
            for i in range(8):
                inst = t.matmul(T["psB"][:, j * 128:(j + 1) * 128],
                                T["w_" + wname][:, i * 1024 + j * 128:i * 1024 + (j + 1) * 128],
                                T["xln_fm"][:, i * 128:(i + 1) * 128],
                                start=(i == 0), stop=(i == 7))
        return inst
    for wname, dst in [("Wq", "q_fm"), ("Wk", "k_fm")]:
        p.op("t", lambda t, w=wname: proj_fm(t, w), reads=["w_" + wname, "xln_fm"],
             writes=["psB"], tag=f"pfm{wname}{step}")
        p.op("s", lambda s, d=dst: s.copy(out=T[d][:], in_=T["psB"][:]),
             reads=["psB"], writes=[dst], tag=f"pfme{wname}{step}")

    def proj_tm(t, wname):
        for half in range(2):
            for i in range(8):
                inst = t.matmul(T["psB"][:, half * 512:(half + 1) * 512],
                                T["xln_fm"][:, i * 128:(i + 1) * 128],
                                T["w_" + wname][:, i * 1024 + half * 512:
                                                i * 1024 + (half + 1) * 512],
                                start=(i == 0), stop=(i == 7))
        return inst
    for wname, dst in [("Wq", "q_tm"), ("Wk", "k_tm"), ("Wv", "v_tm")]:
        p.op("t", lambda t, w=wname: proj_tm(t, w), reads=["w_" + wname, "xln_fm"],
             writes=["psB"], tag=f"ptm{wname}{step}")
        p.op("v", lambda v, d=dst: v.tensor_copy(out=T[d][:], in_=T["psB"][:]),
             reads=["psB"], writes=[dst], tag=f"ptme{wname}{step}")
    p.op("t", lambda t: proj_tm(t, "Wg"), reads=["w_Wg", "xln_fm"],
         writes=["psB"], tag=f"ptmWg{step}")
    p.op("s", lambda s: s.activation(out=T["g_tm"][:], in_=T["psB"][:],
         func=ACT.Silu), reads=["psB"], writes=["g_tm"], tag=f"silu{step}")

    def logits(t):
        for i in range(8):
            inst = t.matmul(T["psA"][:, 896:928], T["xln_fm"][:, i * 128:(i + 1) * 128],
                            T["w_ab"][:, i * 32:(i + 1) * 32],
                            start=(i == 0), stop=(i == 7))
        return inst
    p.op("t", logits, reads=["w_ab", "xln_fm"], writes=["psA"], tag=f"lg{step}")
    p.op("v", lambda v: v.tensor_tensor(out=T["zat"][:], in0=T["psA"][:, 896:912],
         in1=T["ba_bc"][:], op=ALU.add), reads=["psA", "ba_bc"], writes=["zat"])
    p.op("v", lambda v: v.tensor_tensor(out=T["zbt"][:], in0=T["psA"][:, 912:928],
         in1=T["bb_bc"][:], op=ALU.add), reads=["psA", "bb_bc"], writes=["zbt"])
    p.op("s", lambda s: s.activation(out=T["alpha"][:], in_=T["zat"][:],
         func=ACT.Sigmoid), reads=["zat"], writes=["alpha"])
    p.op("s", lambda s: s.activation(out=T["beta"][:], in_=T["zbt"][:],
         func=ACT.Sigmoid), reads=["zbt"], writes=["beta"])
    p.op("s", lambda s: s.activation(out=T["loga"][:], in_=T["alpha"][:],
         func=ACT.Ln), reads=["alpha"], writes=["loga"])
    p.op("s", lambda s: s.activation(out=T["logb"][:], in_=T["beta"][:],
         func=ACT.Ln), reads=["beta"], writes=["logb"])

    p.op("t", lambda t: t.matmul(T["psA"][:, 928:944], T["c_tril2"][:],
         T["loga"][:], start=True, stop=True), reads=["c_tril2", "loga"],
         writes=["psA"], tag=f"cum{step}")
    p.op("v", lambda v: v.tensor_copy(out=T["ell"][:], in_=T["psA"][:, 928:944]),
         reads=["psA"], writes=["ell"])
    p.op("t", lambda t: t.matmul(T["psA"][:, 928:944], T["c_selC"][:],
         T["ell"][:], start=True, stop=True), reads=["c_selC", "ell"],
         writes=["psA"], tag=f"selC{step}")
    p.op("v", lambda v: v.tensor_copy(out=T["ellC"][:], in_=T["psA"][:, 928:944]),
         reads=["psA"], writes=["ellC"])
    p.op("v", lambda v: v.tensor_tensor(out=T["ellprev"][:], in0=T["ell"][:],
         in1=T["loga"][:], op=ALU.subtract), reads=["ell", "loga"], writes=["ellprev"])
    p.op("v", lambda v: v.tensor_tensor(out=T["glt"][:], in0=T["ellprev"][:],
         in1=T["logb"][:], op=ALU.add), reads=["ellprev", "logb"], writes=["glt"])
    p.op("v", lambda v: v.tensor_tensor(out=T["d1"][:], in0=T["ellC"][:],
         in1=T["ell"][:], op=ALU.subtract), reads=["ellC", "ell"], writes=["d1"])
    for src, dst in [("ell", "e_l"), ("ellprev", "e_lp"), ("d1", "e_ct"),
                     ("ellC", "e_lC")]:
        p.op("s", lambda s, a=src, b=dst: s.activation(out=T[b][:], in_=T[a][:],
             func=ACT.Exp), reads=[src], writes=[dst])
    def selmm(t):
        for e in range(2):
            for sq in range(2):
                j = e * 2 + sq
                inst = t.matmul(T["psA"][:, 944 + sq * 8:944 + sq * 8 + 8],
                                T["c_sel4"][:, j * 128:(j + 1) * 128],
                                T["ell"][:, e:16:2],
                                start=(e == 0 and sq == 0), stop=(e == 1 and sq == 1))
        return inst
    p.op("t", selmm, reads=["c_sel4", "ell"], writes=["psA"], tag=f"selmm{step}")
    p.op("s", lambda s: s.activation(out=T["e_lC_st"][:], in_=T["psA"][:, 944:960],
         func=ACT.Exp), reads=["psA"], writes=["e_lC_st"], tag=f"elcst{step}")
    if dbg:
        kb.dump("ell", T["ell"][:], ["ell"])
        kb.dump("glt", T["glt"][:], ["glt"])
        kb.dump("zat", T["zat"][:], ["zat"])
        kb.dump("alpha", T["alpha"][:], ["alpha"])
        kb.dump("loga", T["loga"][:], ["loga"])
        kb.dump("xlntm", T["xln_tm"][:], ["xln_tm"])
        kb.dump("mn", T["mn"][:], ["mn"])
        kb.dump("sum1", T["sum1"][:], ["sum1"])
        kb.dump("rstd", T["rstd"][:], ["rstd"])
        kb.dump("ssqd", T["ssq"][:], ["ssq"])
        kb.dump("xcc", T["xc"][:], ["xc"])
        kb.dump("xlnfm", T["xln_fm"][:], ["xln_fm"])


def emit_attn_step2(kb, step):
    p, T = kb.p, kb.T
    dbg = (step == 0)
    for src, ssq in [("q_tm", "ssq_q"), ("k_tm", "ssq_k")]:
        p.op("s", lambda s, a=src: s.activation(out=T["scrD"][:], in_=T[a][:],
             func=ACT.Square), reads=[src], writes=["scrD"], tag=f"l2sq{step}")
        p.op("v", lambda v, o=ssq: v.tensor_reduce(
             out=T[o][:], in_=T["scrD"][:].rearrange("p (h e) -> p h e", h=NH),
             axis=AX.X, op=ALU.add), reads=["scrD"], writes=[ssq],
             tag=f"l2r{step}", deferred=True)
    for ssq, n, rn in [("ssq_q", "nq", "rn_q"), ("ssq_k", "nk", "rn_k")]:
        p.op("s", lambda s, a=ssq, b=n: s.activation(out=T[b][:], in_=T[a][:],
             func=ACT.Sqrt), reads=[ssq], writes=[n])
        p.op("v", lambda v, b=n: v.tensor_scalar_max(out=T[b][:], in0=T[b][:],
             scalar1=1e-12), reads=[n], writes=[n], deferred=True)
        p.op("v", lambda v, b=n, c=rn: v.reciprocal(out=T[c][:], in_=T[b][:]),
             reads=[n], writes=[rn], deferred=True)
    p.op("s", lambda s: s.activation(out=T["lnrnq"][:], in_=T["nq"][:],
         func=ACT.Ln), reads=["nq"], writes=["lnrnq"])
    p.op("s", lambda s: s.activation(out=T["lnrnk"][:], in_=T["nk"][:],
         func=ACT.Ln), reads=["nk"], writes=["lnrnk"])
    p.op("v", lambda v: v.tensor_scalar(out=T["lnrnq"][:], in0=T["lnrnq"][:],
         scalar1=-1.0, scalar2=None, op0=ALU.mult), reads=["lnrnq"], writes=["lnrnq"])
    p.op("v", lambda v: v.tensor_scalar(out=T["lnrnk"][:], in0=T["lnrnk"][:],
         scalar1=-1.0, scalar2=None, op0=ALU.mult), reads=["lnrnk"], writes=["lnrnk"])
    p.op("v", lambda v: v.tensor_tensor(out=T["biasWN"][:], in0=T["lnrnk"][:],
         in1=T["ell"][:], op=ALU.subtract), reads=["lnrnk", "ell"], writes=["biasWN"])
    p.op("v", lambda v: v.tensor_tensor(out=T["nlt"][:], in0=T["glt"][:],
         in1=T["lnrnk"][:], op=ALU.add), reads=["glt", "lnrnk"], writes=["nlt"])
    p.op("v", lambda v: v.tensor_tensor(out=T["wlt"][:], in0=T["ell"][:],
         in1=T["lnrnq"][:], op=ALU.add), reads=["ell", "lnrnq"], writes=["wlt"])
    for srcn, hi, lo in [("wlt", "whi", "wlo"), ("nlt", "ghi", "glo"),
                         ("biasWN", "bhi", "blo")]:
        p.op("v", lambda v, a=srcn: v.tensor_copy(out=T["hilo16"][:], in_=T[a][:]),
             reads=[srcn], writes=["hilo16"], tag="hs")
        p.op("v", lambda v, h=hi: v.tensor_copy(out=T[h][:], in_=T["hilo16"][:]),
             reads=["hilo16"], writes=[hi], tag="hs2")
        p.op("v", lambda v, a=srcn, h=hi, l=lo: v.tensor_tensor(out=T[l][:],
             in0=T[a][:], in1=T[h][:], op=ALU.subtract), reads=[srcn, hi],
             writes=[lo], tag="hs3")
    p.op("v", lambda v: v.tensor_tensor(out=T["sq2"][:], in0=T["rn_q"][:],
         in1=T["e_l"][:], op=ALU.mult), reads=["rn_q", "e_l"], writes=["sq2"])
    p.op("v", lambda v: v.tensor_scalar(out=T["sq2n"][:], in0=T["sq2"][:],
         scalar1=-1.0, scalar2=None, op0=ALU.mult), reads=["sq2"], writes=["sq2n"])
    p.op("v", lambda v: v.memset(T["sc3"][:, 16:64], 0.0), writes=["sc3"])
    p.op("v", lambda v: v.tensor_copy(out=T["sc3"][:, 0:16], in_=T["sq2n"][:]),
         reads=["sq2n"], writes=["sc3"])
    p.op("t", lambda t: t.transpose(T["psA"][0:64, 896:1024], T["sc3"][:],
         T["c_id128f"][:]), reads=["sc3", "c_id128f"], writes=["psA"],
         tag=f"trsc{step}")
    p.op("v", lambda v: v.tensor_copy(out=T["scT3"][:], in_=T["psA"][0:16, 896:1024]),
         reads=["psA"], writes=["scT3"], tag=f"trsce{step}")
    def indmm(t):
        for ci in range(8):
            inst = t.matmul(T["psA"][:, ci * 128:(ci + 1) * 128],
                            T["c_indh1"][:, ci * 128:(ci + 1) * 128],
                            T["scT3"][:], start=True, stop=True)
        return inst
    p.op("t", indmm, reads=["c_indh1", "scT3"], writes=["psA"], tag=f"indmm{step}")
    p.op("v", lambda v: v.tensor_tensor(out=T["qh_fm"][:], in0=T["q_fm"][:],
         in1=T["psA"][:], op=ALU.mult), reads=["q_fm", "psA"], writes=["qh_fm"],
         tag=f"scap{step}")
    p.op("v", lambda v: v.tensor_tensor(out=T["skh"][:], in0=T["beta"][:],
         in1=T["e_lp"][:], op=ALU.mult), reads=["beta", "e_lp"], writes=["skh"])
    p.op("v", lambda v: v.tensor_tensor(out=T["skh2"][:], in0=T["skh"][:],
         in1=T["rn_k"][:], op=ALU.mult), reads=["skh", "rn_k"], writes=["skh2"])
    p.op("v", lambda v: v.tensor_tensor(out=T["sct"][:], in0=T["e_ct"][:],
         in1=T["rn_k"][:], op=ALU.mult), reads=["e_ct", "rn_k"], writes=["sct"])
    R3 = lambda: T["R"][:].rearrange("p (h c) -> p h c", h=NH)
    v3 = lambda nm: T[nm][:].rearrange("p (h e) -> p h e", h=NH)
    bc = lambda nm: T[nm][:].unsqueeze(2).broadcast_to([128, NH, HD])
    p.op("v", lambda v: v.tensor_tensor(out=R3()[:, :, 0:64], in0=v3("v_tm"),
         in1=bc("beta"), op=ALU.mult), reads=["v_tm", "beta"], writes=["R"],
         tag=f"Rv{step}")
    p.op("v", lambda v: v.tensor_tensor(out=R3()[:, :, 64:128], in0=v3("k_tm"),
         in1=bc("skh2"), op=ALU.mult), reads=["k_tm", "skh2"], writes=["R"],
         tag=f"Rk{step}")
    p.op("v", lambda v: v.tensor_tensor(out=v3("kt_tm"), in0=v3("k_tm"),
         in1=bc("sct"), op=ALU.mult), reads=["k_tm", "sct"], writes=["kt_tm"],
         tag=f"kt{step}")
    if dbg:
        kb.dump("qh_fm", T["qh_fm"][:], ["qh_fm"])
        kb.dump("scT3", T["scT3"][:], ["scT3"])
        kb.dump("sq2n", T["sq2n"][:], ["sq2n"])
        kb.dump("qfm", T["q_fm"][:], ["q_fm"])
        kb.dump("R", T["R"][:], ["R"])
        kb.dump("kt", T["kt_tm"][:], ["kt_tm"])


def _halves(fn):
    return [(0, 1024), (1024, 2048)]


def emit_attn_step3(kb, step):
    p, T = kb.p, kb.T
    dbg = (step == 0)
    dv3 = lambda nm, nh: T[nm][:].rearrange("p (h c) -> p h c", h=nh)
    bc16 = lambda nm, w: T[nm][:].unsqueeze(2).broadcast_to([128, NH, w])

    if kb.stage == "SZ":
        return
    for hl in ["whi", "wlo"]:
        p.op("v", lambda v, a=hl: v.tensor_tensor(out=dv3("dW", NH),
             in0=dv3("c_dindW", NH), in1=bc16(a, 64), op=ALU.mult),
             reads=["c_dindW", hl], writes=["dW"], tag=f"dgW{hl}{step}")
        def mmW(t, first=(hl == "whi")):
            for q in range(2):
                inst = t.matmul(T["psB"][:, q * 512:(q + 1) * 512],
                                T["c_ones2"][:], T["dW"][:, q * 512:(q + 1) * 512],
                                start=first, stop=False)
            return inst
        p.op("t", mmW, reads=["c_ones2", "dW"], writes=["psB"], tag=f"mmW{hl}{step}")
        if kb.stage == "SA" and hl == "whi":
            return
    def mmWneg(t):
        for q in range(2):
            inst = t.matmul(T["psB"][:, q * 512:(q + 1) * 512], T["c_id128b"][:],
                            T["c_negW"][:, q * 512:(q + 1) * 512],
                            start=False, stop=True)
        return inst
    p.op("t", mmWneg, reads=["c_id128b", "c_negW"], writes=["psB"],
         tag=f"mmWneg{step}")
    if kb.stage == "SB":
        return
    def expW(s):
        for h in range(NH):
            inst = s.activation(out=T["dW"][:, h * 64:(h + 1) * 64],
                                in_=T["psB"][:, h * 64:(h + 1) * 64], func=ACT.Exp,
                                scale=1.0, bias=T["biasWN"][:, h:h + 1])
        return inst
    p.op("s", expW, reads=["psB"], writes=["dW"], scalar_reads=["biasWN"],
         tag=f"expW{step}")
    if kb.stage == "S0":
        return
    def gramKQ(t):
        inst = None
        for parity in range(2):
            if parity == 1:
                t.drain()
            rlo = 64 * parity
            for h in range(parity, NH, 2):
                for par in range(2):
                    clo = (h // 2) * 128 + 64 * par
                    inst = t.matmul(
                        T["psB"][64 * par:64 * par + 64, h * 64:(h + 1) * 64],
                        T["k_fm"][rlo:rlo + 64, clo:clo + 64],
                        T["q_fm"][rlo:rlo + 64, clo:clo + 64],
                        start=True, stop=True, skip_group_check=True)
        return inst
    p.op("t", gramKQ, reads=["k_fm", "q_fm"], writes=["psB"], tag=f"gKQ{step}")
    p.op("v", lambda v: v.tensor_tensor(out=T["wbar"][:], in0=T["dW"][:],
         in1=T["psB"][:], op=ALU.mult), reads=["dW", "psB"], writes=["wbar"],
         tag=f"wbar{step}")
    if kb.stage == "Sg":
        return

    for hl in ["ghi", "glo"]:
        p.op("v", lambda v, a=hl: v.tensor_tensor(out=dv3("dN2", NH),
             in0=dv3("c_dindN", NH), in1=bc16(a, 128), op=ALU.mult),
             reads=["c_dindN", hl], writes=["dN2"], tag=f"dgN{hl}{step}")
        def mmN(t, first=(hl == "ghi")):
            for q in range(4):
                inst = t.matmul(T["psC"][:, q * 512:(q + 1) * 512], T["c_ones2"][:],
                                T["dN2"][:, q * 512:(q + 1) * 512],
                                start=first, stop=False)
            return inst
        p.op("t", mmN, reads=["c_ones2", "dN2"], writes=["psC"],
             tag=f"mmN{hl}{step}")
    def mmNneg(t):
        for q in range(4):
            inst = t.matmul(T["psC"][:, q * 512:(q + 1) * 512], T["c_id128b"][:],
                            T["c_negN"][:, q * 512:(q + 1) * 512],
                            start=False, stop=True)
        return inst
    p.op("t", mmNneg, reads=["c_id128b", "c_negN"], writes=["psC"],
         tag=f"mmNneg{step}")
    def expN(s):
        for h in range(NH):
            inst = s.activation(out=T["dN"][:, h * 128:(h + 1) * 128],
                                in_=T["psC"][:, h * 128:(h + 1) * 128],
                                func=ACT.Exp, scale=1.0,
                                bias=T["biasWN"][:, h:h + 1])
        return inst
    p.op("s", expN, reads=["psC"], writes=["dN"], scalar_reads=["biasWN"],
         tag=f"expN{step}")

    for hl in ["bhi", "blo"]:
        p.op("v", lambda v, a=hl: v.tensor_tensor(out=dv3("dN2", NH),
             in0=dv3("c_dindN", NH), in1=bc16(a, 128), op=ALU.mult),
             reads=["c_dindN", hl], writes=["dN2"], tag=f"dgT{hl}{step}")
        def mmT(t, first=(hl == "bhi")):
            for q in range(4):
                inst = t.matmul(T["psC"][:, q * 512:(q + 1) * 512], T["c_ones2"][:],
                                T["dN2"][:, q * 512:(q + 1) * 512],
                                start=first, stop=False)
            return inst
        p.op("t", mmT, reads=["c_ones2", "dN2"], writes=["psC"],
             tag=f"mmT{hl}{step}")
    def mmTneg(t):
        for q in range(4):
            inst = t.matmul(T["psC"][:, q * 512:(q + 1) * 512], T["c_id128b"][:],
                            T["c_negNT"][:, q * 512:(q + 1) * 512],
                            start=False, stop=True)
        return inst
    p.op("t", mmTneg, reads=["c_id128b", "c_negNT"], writes=["psC"],
         tag=f"mmTneg{step}")
    def expT(s):
        for h in range(NH):
            inst = s.activation(out=T["dN2"][:, h * 128:(h + 1) * 128],
                                in_=T["psC"][:, h * 128:(h + 1) * 128],
                                func=ACT.Exp, scale=1.0, bias=T["nlt"][:, h:h + 1])
        return inst
    p.op("s", expT, reads=["psC"], writes=["dN2"], scalar_reads=["nlt"],
         tag=f"expT{step}")

    def gramKK(t):
        inst = None
        for parity in range(2):
            if parity == 1:
                t.drain()
            rlo = 64 * parity
            for h in range(parity, NH, 2):
                for par in range(2):
                    clo = (h // 2) * 128 + 64 * par
                    inst = t.matmul(
                        T["psC"][64 * par:64 * par + 64,
                                 h * 128 + 64 * par:h * 128 + 64 * par + 64],
                        T["k_fm"][rlo:rlo + 64, clo:clo + 64],
                        T["k_fm"][rlo:rlo + 64, clo:clo + 64],
                        start=True, stop=True, skip_group_check=True)
        return inst
    p.op("t", gramKK, reads=["k_fm"], writes=["psC"], tag=f"gKK{step}")
    p.op("v", lambda v: v.scalar_tensor_tensor(out=T["nbar"][:], in0=T["dN"][:],
         scalar=-1.0, in1=T["psC"][:], op0=ALU.mult, op1=ALU.mult),
         reads=["dN", "psC"], writes=["nbar"], tag=f"nbar{step}")
    p.op("v", lambda v: v.scalar_tensor_tensor(out=T["pbar"][:], in0=T["dN2"][:],
         scalar=-1.0, in1=T["psC"][:], op0=ALU.mult, op1=ALU.mult),
         reads=["dN2", "psC"], writes=["pbar"], tag=f"pbar{step}")
    if dbg:
        kb.dump("nbar", T["nbar"][:], ["nbar"])
        kb.dump("wbar", T["wbar"][:], ["wbar"])
    if kb.stage in ("S1",):
        return

    Bname, Pname = "nbar", "pbar"
    Balt, Palt = "dN", "dN2"
    Xsrc = "R"
    for lvl in range(6):
        def apply(t, B=Bname, X=Xsrc):
            for h in range(NH):
                sl = slice(h * 128, (h + 1) * 128)
                t.matmul(T["psC"][:, sl], T[B][:, sl], T[X][:, sl],
                         start=(h % 4 == 0), stop=False)
                inst = t.matmul(T["psC"][:, sl], T["c_id128b"][:], T[X][:, sl],
                                start=False, stop=(h % 4 == 3))
            return inst
        p.op("t", lambda t, f=apply: f(t), reads=[Bname, Xsrc, "c_id128b"],
             writes=["psC"], tag=f"ap{lvl}_{step}")
        p.op("v", lambda v: v.tensor_copy(out=T["Xc"][:, 0:1024],
             in_=T["psC"][:, 0:1024]), reads=["psC"], writes=["Xc"],
             tag=f"apev{lvl}_{step}")
        p.op("s", lambda s: s.copy(out=T["Xc"][:, 1024:2048],
             in_=T["psC"][:, 1024:2048]), reads=["psC"], writes=["Xc"],
             tag=f"apes{lvl}_{step}")
        Xsrc = "Xc"
        if lvl < 5:
            def sqB(t, B=Bname, P=Pname):
                for h in range(NH):
                    sl = slice(h * 128, (h + 1) * 128)
                    inst = t.matmul(T["psC"][:, sl], T[P][:, sl], T[B][:, sl],
                                    start=(h % 4 == 0), stop=(h % 4 == 3))
                return inst
            p.op("t", lambda t, f=sqB: f(t), reads=[Bname, Pname],
                 writes=["psC"], tag=f"sqB{lvl}_{step}")
            p.op("v", lambda v, d=Balt: v.tensor_copy(out=T[d][:], in_=T["psC"][:]),
                 reads=["psC"], writes=[Balt], tag=f"sqBe{lvl}_{step}")
            if lvl < 4:
                def sqP(t, B=Bname, P=Pname):
                    for h in range(NH):
                        sl = slice(h * 128, (h + 1) * 128)
                        inst = t.matmul(T["psC"][:, sl], T[B][:, sl], T[P][:, sl],
                                        start=(h % 4 == 0), stop=(h % 4 == 3))
                    return inst
                p.op("t", lambda t, f=sqP: f(t), reads=[Bname, Pname],
                     writes=["psC"], tag=f"sqP{lvl}_{step}")
                p.op("s", lambda s, d=Palt: s.copy(out=T[d][:], in_=T["psC"][:]),
                     reads=["psC"], writes=[Palt], tag=f"sqPe{lvl}_{step}")
                Pname, Palt = Palt, Pname
            Bname, Balt = Balt, Bname
    if dbg:
        kb.dump("Xs", T["Xc"][:], ["Xc"])


def emit_attn_step4(kb, step, x2_d, x2ln_d):
    p, T = kb.p, kb.T
    dbg = (step == 0)
    r0, r1 = step * C, (step + 1) * C

    def xkt(t):
        pb = T["psA"][:].bitcast(BF16)
        for h in range(NH):
            rlo = 64 * (h % 2)
            inst = t.transpose(pb[rlo:rlo + 64, (h // 2) * 128:(h // 2) * 128 + 128],
                               T["Xc"][:, h * 128 + 64:(h + 1) * 128],
                               T["c_id128b"][:], tile_position=(0, rlo))
        return inst
    p.op("t", xkt, reads=["Xc", "c_id128b"], writes=["psA"], tag=f"xkt{step}")
    p.op("v", lambda v: v.tensor_copy(out=T["xkT"][:],
         in_=T["psA"][:].bitcast(BF16)[:, 0:1024]),
         reads=["psA"], writes=["xkT"], tag=f"xkte{step}")

    def gmm(t):
        inst = None
        for h in range(NH):
            cl = slice(h * 64, (h + 1) * 64)
            for par in range(2):
                rlo = 64 * (h % 2)
                t.matmul(
                    T["psB"][64 * par:64 * par + 64, cl],
                    T["xkT"][rlo:rlo + 64, (h // 2) * 128 + 64 * par:
                             (h // 2) * 128 + 64 * par + 64],
                    T["negS_b"][rlo:rlo + 64,
                                (par * 8 + h // 2) * 64:(par * 8 + h // 2) * 64 + 64],
                    start=True, stop=False, skip_group_check=True)
            inst = t.matmul(T["psB"][:, cl], T["c_id128b"][:],
                            T["Xc"][:, h * 128:h * 128 + 64],
                            start=False, stop=True, skip_group_check=True)
        return inst
    p.op("t", gmm, reads=["Xc", "xkT", "negS_b", "c_id128b"], writes=["psB"],
         tag=f"gmm{step}")
    p.op("v", lambda v: v.tensor_copy(out=T["gt"][:], in_=T["psB"][:]),
         reads=["psB"], writes=["gt"], tag=f"gte{step}")
    if dbg:
        kb.dump("gt", T["gt"][:], ["gt"])

    def omm1(t):
        inst = None
        for parity in range(2):
            if parity == 1:
                t.drain()
            rlo = 64 * parity
            for h in range(parity, NH, 2):
                cl = slice(h * 64, (h + 1) * 64)
                for par in range(2):
                    inst = t.matmul(
                        T["psC"][64 * par:64 * par + 64, cl],
                        T["qh_fm"][rlo:rlo + 64, (h // 2) * 128 + 64 * par:
                                   (h // 2) * 128 + 64 * par + 64],
                        T["negS_b"][rlo:rlo + 64, (par * 8 + h // 2) * 64:
                                    (par * 8 + h // 2) * 64 + 64],
                        start=True, stop=True, skip_group_check=True)
        return inst

    def omm2(t):
        inst = None
        for par in (1, 0):
            if par == 0:
                t.drain()
            for h in range(NH):
                cl = slice(h * 64, (h + 1) * 64)
                inst = t.matmul(
                    T["psB"][64 * par:64 * par + 64, cl],
                    T["wbar"][64 * par:64 * par + 64, cl],
                    T["gt"][64 * par:64 * par + 64, cl],
                    start=True, stop=True, skip_group_check=True)
        return inst
    p.op("t", omm1, reads=["qh_fm", "negS_b"], writes=["psC"],
         tag=f"omm1{step}")
    p.op("t", omm2, reads=["wbar", "gt"], writes=["psB"],
         tag=f"omm2{step}")
    p.op("s", lambda s: s.copy(out=T["y_tm"][:], in_=T["psC"][:, 0:1024]),
         reads=["psC"], writes=["y_tm"], tag=f"ye1{step}")
    p.op("v", lambda v: v.tensor_tensor(out=T["y_tm"][:], in0=T["y_tm"][:],
         in1=T["psB"][:, 0:1024], op=ALU.add), reads=["y_tm", "psB"],
         writes=["y_tm"], tag=f"ye2{step}")
    if dbg:
        kb.dump("y0", T["y_tm"][:], ["y_tm"])
    if step == 1 and "y1" in kb.dbg_outs:
        kb.dump("y1", T["y_tm"][:], ["y_tm"])

    def smm(t):
        inst = None
        for par in range(2):
            if par == 1:
                t.drain()
            for h in range(NH):
                rlo = 64 * (h % 2)
                inst = t.matmul(
                    T["psA"][rlo:rlo + 64,
                             (par * 8 + h // 2) * 64:(par * 8 + h // 2) * 64 + 64],
                    T["kt_tm"][64 * par:64 * par + 64, h * 64:(h + 1) * 64],
                    T["gt"][64 * par:64 * par + 64, h * 64:(h + 1) * 64],
                    start=True, stop=True, skip_group_check=True)
        return inst
    p.op("t", smm, reads=["kt_tm", "gt"], writes=["psA"], tag=f"smm{step}")
    st3 = lambda: T["negS"][:].rearrange("p (c e) -> p c e", c=16)
    ebc = lambda: T["e_lC_st"][:].unsqueeze(2).broadcast_to([128, 16, HD])
    p.op("v", lambda v: v.tensor_tensor(out=T["scrD"][:].rearrange(
         "p (c e) -> p c e", c=16), in0=st3(), in1=ebc(), op=ALU.mult),
         reads=["negS", "e_lC_st"], writes=["scrD"], tag=f"sdec{step}")
    p.op("v", lambda v: v.tensor_tensor(out=T["negS"][:], in0=T["scrD"][:],
         in1=T["psA"][:], op=ALU.subtract), reads=["scrD", "psA"],
         writes=["negS"], tag=f"supd{step}")
    p.op("s", lambda s: s.copy(out=T["negS_b"][:], in_=T["negS"][:]),
         reads=["negS"], writes=["negS_b"], tag=f"supb{step}")

    p.op("s", lambda s: s.activation(out=T["scrD2"][:], in_=T["y_tm"][:],
         func=ACT.Square), reads=["y_tm"], writes=["scrD2"], tag=f"ysq{step}")
    p.op("v", lambda v: v.tensor_reduce(out=T["ssq"][:], in_=T["scrD2"][:],
         axis=AX.X, op=ALU.add), reads=["scrD2"], writes=["ssq"],
         tag=f"yssq{step}", deferred=True)
    p.op("s", lambda s: s.activation(out=T["sd"][:], in_=T["ssq"][:],
         func=ACT.Sqrt, bias=T["eps1"][:, 0:1], scale=1.0 / D),
         reads=["ssq", "eps1"], writes=["sd"], tag=f"ysd{step}")
    p.op("v", lambda v: v.reciprocal(out=T["rstd"][:], in_=T["sd"][:]),
         reads=["sd"], writes=["rstd"], tag=f"yr{step}", deferred=True)
    p.op("s", lambda s: s.activation(out=T["scrD2"][:], in_=T["y_tm"][:],
         func=ACT.Identity, scale=T["rstd"][:, 0:1]), reads=["y_tm"],
         writes=["scrD2"], scalar_reads=["rstd"], tag=f"yn{step}")
    p.op("v", lambda v: v.tensor_tensor(out=T["scrD2"][:], in0=T["scrD2"][:],
         in1=T["gnorm_bc"][:], op=ALU.mult), reads=["scrD2", "gnorm_bc"],
         writes=["scrD2"], tag=f"yg{step}")
    p.op("v", lambda v: v.tensor_tensor(out=T["xln_tm"][:], in0=T["scrD2"][:],
         in1=T["g_tm"][:], op=ALU.mult), reads=["scrD2", "g_tm"],
         writes=["xln_tm"], tag=f"ygg{step}")
    def try8(t):
        pb = T["psA"][:].bitcast(BF16)
        for ci in range(8):
            inst = t.transpose(pb[:, ci * 128:(ci + 1) * 128],
                               T["xln_tm"][:, ci * 128:(ci + 1) * 128],
                               T["c_id128b"][:])
        return inst
    p.op("t", try8, reads=["xln_tm", "c_id128b"], writes=["psA"], tag=f"try{step}")
    p.op("v", lambda v: v.tensor_copy(out=T["yf_fm"][:],
         in_=T["psA"][:].bitcast(BF16)[:, 0:1024]),
         reads=["psA"], writes=["yf_fm"], tag=f"trye{step}")
    def womm(t):
        for half in range(2):
            for i in range(8):
                inst = t.matmul(T["psB"][:, half * 512:(half + 1) * 512],
                                T["yf_fm"][:, i * 128:(i + 1) * 128],
                                T["w_Wo"][:, i * 1024 + half * 512:
                                          i * 1024 + (half + 1) * 512],
                                start=(i == 0), stop=(i == 7))
        return inst
    p.op("t", womm, reads=["yf_fm", "w_Wo"], writes=["psB"], tag=f"womm{step}")

    _ln_ops(kb, "psB", lambda T: T["psB"][:], "scrD",
            lambda T: T["scrD"][:], "m2", "r2", "b2", tag=f"ln2_{step}")
    p.op("v", lambda v: v.tensor_tensor(out=T["xc"][:], in0=T["xc"][:],
         in1=T["scrD"][:], op=ALU.add), reads=["xc", "scrD"], writes=["xc"],
         tag=f"res1{step}")
    for sq in range(NSEQ):
        p.dma("g", lambda g, sq=sq: g.dma_start(
            out=x2_d[sq, r0:r1, :], in_=T["xc"][sq * C:(sq + 1) * C, :]),
            reads=["xc"], tag=f"stx2_{step}")
    if dbg:
        kb.dump("x2_0", T["xc"][:], ["xc"])

    _ln_ops(kb, "xc", lambda T: T["xc"][:], "x2ln_c",
            lambda T: T["x2ln_c"][:], "m3", "r3", "b3", tag=f"ln3_{step}")
    for sq in range(NSEQ):
        p.dma("g", lambda g, sq=sq: g.dma_start(
            out=x2ln_d[sq, r0:r1, :], in_=T["x2ln_c"][sq * C:(sq + 1) * C, :]),
            reads=["x2ln_c"], tag=f"stxln_{step}")
    p.op("v", lambda v: v.tensor_reduce(out=T["mx1"][:], in_=T["x2ln_c"][:],
         axis=AX.X, op=ALU.max, apply_absolute_value=True),
         reads=["x2ln_c"], writes=["mx1"], tag=f"mx{step}", deferred=True)
    p.op("v", lambda v: v.tensor_tensor(out=T["mxacc"][:], in0=T["mxacc"][:],
         in1=T["mx1"][:], op=ALU.max), reads=["mxacc", "mx1"], writes=["mxacc"],
         tag=f"mxa{step}")
    if dbg:
        kb.dump("x2ln_0", T["x2ln_c"][:], ["x2ln_c"])


def lower(kb):
    nc, p = kb.nc, kb.p
    T = kb.T
    es = kb.es
    sems = {}
    dsems = {}
    for e in ("t", "v", "s", "g", "y"):
        sems[e] = es.enter_context(nc.semaphore("sem_" + e))
    for e in ("g", "y", "s", "v"):
        dsems[e] = [es.enter_context(nc.semaphore(f"dsem_{e}{i}")) for i in range(8)]
    block = es.enter_context(nc.Block())
    p.finalize(nc, None, sems, dsems)

    @block.tensor
    def _(t):
        p.emit("t", t, sems, dsems)

    @block.vector
    def _(v):
        p.emit("v", v, sems, dsems)

    @block.scalar
    def _(s):
        p.emit("s", s, sems, dsems)

    @block.gpsimd
    def _(g):
        p.emit("g", g, sems, dsems)

    @block.sync
    def _(y):
        p.emit("y", y, sems, dsems)

    kb.es.close()
    return nc


def psum_init(kb):
    p, T = kb.p, kb.T
    for ps, res in [("psA", "psA"), ("psB", "psB"), ("psC", "psC")]:
        p.op("v", lambda v, a=ps: v.memset(T[a][:], 0.0), writes=[res],
             tag="psz")


def final_barrier(kb):
    p = kb.p
    for e in ("g", "y", "s", "v"):
        n = p.ndma[e]
        if n:
            def mk(eng_obj, e=e, n=n):
                return None
            o = p.op(e, lambda eng: eng.nop(), tag="final_wait")
            last = [op for op in p.ops[e] if op.is_dma]
            if last:
                o.deps.append(last[-1])


def _gmax_collective(kb, srcs, dsts, tagn):
    p, T, nc = kb.p, kb.T, kb.nc
    n = len(srcs)
    cin = kb.dtmp(f"cc_in_{tagn}", [128, n])
    cout = kb.dtmp(f"cc_out_{tagn}", [128, n])
    sd = kb.dtmp(f"cc_s_{tagn}", [n])
    for i, s in enumerate(srcs):
        p.dma("g", lambda g, i=i, s=s: g.dma_start(
            out=cin.ap()[:, i:i + 1], in_=T[s][:],
            allow_slow_non_contiguous=True), reads=[s],
            writes=[f"ccin_{tagn}"], tag=f"cc{tagn}a")
    p.op("g", lambda g: g.collective_compute(
        "AllReduce", ALU.max, replica_groups=[list(range(8))],
        ins=[cin.ap()], outs=[cout.ap()]),
        reads=[f"ccin_{tagn}"], writes=[f"ccout_{tagn}"], tag=f"cc{tagn}b",
        is_cc=True)
    p.dma("g", lambda g: g.dma_start(out=T["ccrow"][0:128, 0:n], in_=cout.ap()),
          reads=[f"ccout_{tagn}"], writes=["ccrow"], tag=f"cc{tagn}c")
    p.op("t", lambda t: t.transpose(T["psA"][0:n, 896:1024],
         T["ccrow"][:, 0:n], T["c_id128f"][:]), reads=["ccrow", "c_id128f"],
         writes=["psA"], tag=f"cc{tagn}t")
    p.op("v", lambda v: v.tensor_reduce(out=T["ccred"][0:n, 0:1],
         in_=T["psA"][0:n, 896:1024], axis=AX.X, op=ALU.max),
         reads=["psA"], writes=["ccred"], tag=f"cc{tagn}r")
    p.dma("g", lambda g: g.dma_start(out=sd.ap(), in_=T["ccred"][0:n, 0:1]),
          reads=["ccred"], writes=[f"ccs_{tagn}"], tag=f"cc{tagn}d")
    for i, d in enumerate(dsts):
        p.dma("y", lambda y, i=i, d=d: y.dma_start(
            out=T[d][:], in_=sd.ap()[i:i + 1].rearrange("(a b) -> a b", a=1)
            .broadcast_to([128, 1])), reads=[f"ccs_{tagn}"], writes=[d],
            tag=f"cc{tagn}e")


def _fix_cc_deps(kb, tagn):
    pass


def emit_ffn_head(kb, x2_d, x2ln_d):
    p, T, nc = kb.p, kb.T, kb.nc
    NB = IFF // 32
    kb.din("w_gate_d", [IFF, D]); kb.din("w_up_d", [IFF, D])
    kb.din("w_down_d", [D, IFF])
    uraw_d = kb.dtmp("uraw_d", [IFF, NTOK]).ap()
    graw_d = kb.dtmp("graw_d", [IFF, NTOK]).ap()
    kb.sb("ccrow", [128, 8], F32)
    kb.sb("ccred", [8, 1], F32)
    kb.sb("sgx", [128, 1], F32)
    kb.sb("rsgx", [128, 1], F32)
    kb.sb("Mxg", [128, 1], F32)
    kb.sb("xt1", [128, D], F32)
    kb.sb("xq_tm", [128, D], BF16)
    kb.sb("mxu", [128, 1], F32); kb.sb("mxg", [128, 1], F32)
    kb.sb("mxp", [128, 1], F32); kb.sb("mxo", [128, 1], F32)
    kb.sb("Mug", [128, 1], F32); kb.sb("Mgg", [128, 1], F32)
    kb.sb("Mpg", [128, 1], F32); kb.sb("Mog", [128, 1], F32)
    kb.sb("cu", [128, 1], F32); kb.sb("cg", [128, 1], F32)
    kb.sb("sgu", [128, 1], F32); kb.sb("sgg", [128, 1], F32)
    kb.sb("c2", [128, 1], F32); kb.sb("co", [128, 1], F32)
    kb.sb("sgo", [128, 1], F32)
    kb.push_scope()
    kb.sb("wu_l", [128, 8 * IFF], BF16)
    kb.sb("wg_l", [128, 8 * IFF], BF16)
    kb.sb("xq_fm", [128, 8 * NTOK], BF16)
    kb.sb("wst", [128, IFF], F32)
    kb.sb("bmax", [128, 96], F32)
    kb.sb("bsc", [128, 96], F32)
    kb.sb("brs", [128, 96], F32)
    kb.sb("dq1", [128, IFF], F32)
    kb.sb("dqo", [128, IFF], BF16)
    kb.sb("u_st", [128, 1024], F32)
    kb.sb("g_st", [128, 1024], F32)

    _gmax_collective(kb, ["mxacc"], ["Mxg"], "x")
    p.op("v", lambda v: v.tensor_scalar(out=T["sgx"][:], in0=T["Mxg"][:],
         scalar1=1.0 / QMAX, scalar2=1e-12, op0=ALU.mult, op1=ALU.max),
         reads=["Mxg"], writes=["sgx"], tag="sgx")
    p.op("v", lambda v: v.reciprocal(out=T["rsgx"][:], in_=T["sgx"][:]),
         reads=["sgx"], writes=["rsgx"], tag="rsgx")

    for wsrc, wdst in [("w_up_d", "wu_l"), ("w_gate_d", "wg_l")]:
        for ot in range(24):
            wd = kb.dram[wsrc].ap()
            p.dma("y", lambda y, ot=ot, wd=wd: y.dma_start(
                out=T["wst"][:, 0:1024], in_=wd[ot * 128:(ot + 1) * 128, :]),
                writes=["wst"], tag="wld")
            emit_dequant_tile(kb, 1024, wsrc + str(ot))
            def wtr(t, ot=ot):
                pb = T["psA"][:].bitcast(BF16)
                for ci in range(8):
                    inst = t.transpose(pb[:, ci * 128:(ci + 1) * 128],
                                       T["dqo"][:, ci * 128:(ci + 1) * 128],
                                       T["c_id128b"][:])
                return inst
            p.op("t", wtr, reads=["dqo", "c_id128b"], writes=["psA"], tag="wtr")
            p.op("v", lambda v, wdst=wdst, ot=ot: v.tensor_copy(
                out=T[wdst][:].rearrange("p (c o) -> p c o", c=8)[:, :, ot * 128:(ot + 1) * 128],
                in_=T["psA"][:].bitcast(BF16)[:, 0:1024].rearrange(
                    "p (c o) -> p c o", c=8)),
                reads=["psA"], writes=[wdst], tag="wtre")

    for ti in range(NTOK // 128):
        sq, r0 = divmod(ti * 128, S)
        p.dma("y", lambda y, sq=sq, r0=r0: y.dma_start(
            out=T["xt1"][:], in_=x2ln_d[sq, r0:r0 + 128, :]), writes=["xt1"],
            tag=f"xqld{ti}")
        p.op("s", lambda s: s.activation(out=T["xt1"][:], in_=T["xt1"][:],
             func=ACT.Identity, scale=T["rsgx"][:, 0:1], bias=T["magicC"][:, 0:1]),
             reads=["xt1"], writes=["xt1"], scalar_reads=["rsgx", "magicC"],
             tag=f"xqr{ti}")
        p.op("v", lambda v: v.tensor_scalar(out=T["xq_tm"][:], in0=T["xt1"][:],
             scalar1=MAGIC, scalar2=None, op0=ALU.subtract),
             reads=["xt1"], writes=["xq_tm"], tag=f"xqs{ti}")
        def xtr(t):
            pb = T["psA"][:].bitcast(BF16)
            for ci in range(8):
                inst = t.transpose(pb[:, ci * 128:(ci + 1) * 128],
                                   T["xq_tm"][:, ci * 128:(ci + 1) * 128],
                                   T["c_id128b"][:])
            return inst
        p.op("t", xtr, reads=["xq_tm", "c_id128b"], writes=["psA"], tag=f"xqt{ti}")
        p.op("v", lambda v, ti=ti: v.tensor_copy(
             out=T["xq_fm"][:].rearrange("p (c t) -> p c t", c=8)[:, :, ti * 128:(ti + 1) * 128],
             in_=T["psA"][:].bitcast(BF16)[:, 0:1024].rearrange("p (c t) -> p c t", c=8)),
             reads=["psA"], writes=["xq_fm"], tag=f"xqe{ti}")
    return uraw_d, graw_d


def emit_dequant_tile(kb, width, tagn, sfx=""):
    p, T = kb.p, kb.T
    nb = width // 32
    W, BM, BS, BR, D1, DQ = ("wst" + sfx, "bmax" + sfx, "bsc" + sfx,
                             "brs" + sfx, "dq1" + sfx, "dqo" + sfx)
    v3 = lambda nm, w=width: T[nm][:, 0:w].rearrange("p (b k) -> p b k", k=32)
    bcb = lambda nm, c=nb: T[nm][:, 0:c].unsqueeze(2).broadcast_to([128, c, 32])
    p.op("v", lambda v: v.tensor_reduce(out=T[BM][:, 0:nb], in_=v3(W),
         axis=AX.X, op=ALU.max, apply_absolute_value=True),
         reads=[W], writes=[BM], tag="dq_bm", deferred=True)
    p.op("v", lambda v: v.tensor_scalar(out=T[BS][:, 0:nb], in0=T[BM][:, 0:nb],
         scalar1=1.0 / 7.0, scalar2=1e-12, op0=ALU.mult, op1=ALU.max),
         reads=[BM], writes=[BS], tag="dq_bs", deferred=True)
    p.op("v", lambda v: v.reciprocal(out=T[BR][:, 0:nb], in_=T[BS][:, 0:nb]),
         reads=[BS], writes=[BR], tag="dq_rs", deferred=True)
    p.op("v", lambda v: v.tensor_tensor(out=v3(D1), in0=v3(W),
         in1=bcb(BR), op=ALU.mult), reads=[W, BR], writes=[D1],
         tag="dq_m1")
    p.op("v", lambda v: v.tensor_scalar(out=T[D1][:, 0:width], in0=T[D1][:, 0:width],
         scalar1=MAGIC, scalar2=MAGIC, op0=ALU.add, op1=ALU.subtract),
         reads=[D1], writes=[D1], tag="dq_rnd")
    p.op("v", lambda v: v.tensor_tensor(out=v3(DQ), in0=v3(D1),
         in1=bcb(BS), op=ALU.mult), reads=[D1, BS], writes=[DQ],
         tag="dq_m2")


def emit_ffn_f1(kb, uraw_d, graw_d):
    p, T = kb.p, kb.T
    NSLAB = NTOK // 512
    p.op("v", lambda v: v.memset(T["mxu"][:], 0.0), writes=["mxu"])
    p.op("v", lambda v: v.memset(T["mxg"][:], 0.0), writes=["mxg"])
    xq3 = lambda: T["xq_fm"][:].rearrange("p (c t) -> p c t", c=8)
    for sl in range(NSLAB):
        for wnm, stg, mx, raw, ps in [("wu_l", "u_st", "mxu", uraw_d, "psA"),
                                      ("wg_l", "g_st", "mxg", graw_d, "psB")]:
            for oc in range(24):
                half = oc % 2
                def mm(t, wnm=wnm, oc=oc, sl=sl, half=half, ps=ps):
                    for i in range(8):
                        inst = t.matmul(
                            T[ps][:, half * 512:(half + 1) * 512],
                            T[wnm][:].rearrange("p (c o) -> p c o", c=8)[:, i, oc * 128:(oc + 1) * 128],
                            xq3()[:, i, sl * 512:(sl + 1) * 512],
                            start=(i == 0), stop=(i == 7))
                    return inst
                p.op("t", mm, reads=[wnm, "xq_fm"], writes=[ps], tag=f"f1mm{sl}_{oc}")
                p.op("s", lambda s, stg=stg, half=half, ps=ps: s.copy(
                     out=T[stg][:, half * 512:(half + 1) * 512],
                     in_=T[ps][:, half * 512:(half + 1) * 512]),
                     reads=[ps], writes=[stg], tag=f"f1ev{sl}_{oc}")
                p.op("v", lambda v, stg=stg, half=half: v.tensor_reduce(
                     out=T["mx1"][:], in_=T[stg][:, half * 512:(half + 1) * 512],
                     axis=AX.X, op=ALU.max, apply_absolute_value=True),
                     reads=[stg], writes=["mx1"], tag=f"f1mx{sl}_{oc}", deferred=True)
                p.op("v", lambda v, mx=mx: v.tensor_tensor(out=T[mx][:],
                     in0=T[mx][:], in1=T["mx1"][:], op=ALU.max),
                     reads=[mx, "mx1"], writes=[mx], tag=f"f1mxa{sl}_{oc}")
                p.dma("g", lambda g, raw=raw, oc=oc, sl=sl, stg=stg, half=half:
                      g.dma_start(out=raw[oc * 128:(oc + 1) * 128,
                                          sl * 512:(sl + 1) * 512],
                                  in_=T[stg][:, half * 512:(half + 1) * 512]),
                      reads=[stg], tag=f"f1st{sl}_{oc}")


def _scal(kb, fn, reads, writes, tag):
    kb.p.op("v", fn, reads=reads, writes=writes, tag=tag, deferred=True)


def emit_ffn_rest(kb, x2_d, x2ln_d, out_d, uraw_d, graw_d):
    p, T, nc = kb.p, kb.T, kb.nc
    _gmax_collective(kb, ["mxu", "mxg"], ["Mug", "Mgg"], "ug")
    for Mg, sg, cc in [("Mug", "sgu", "cu"), ("Mgg", "sgg", "cg")]:
        _scal(kb, lambda v, Mg=Mg, sg=sg: v.tensor_tensor(out=T[sg][:],
              in0=T[Mg][:], in1=T["sgx"][:], op=ALU.mult), [Mg, "sgx"], [sg], "a2a")
        _scal(kb, lambda v, sg=sg: v.tensor_scalar(out=T[sg][:], in0=T[sg][:],
              scalar1=1.0 / QMAX, scalar2=1e-12, op0=ALU.mult, op1=ALU.max),
              [sg], [sg], "a2b")
        _scal(kb, lambda v, sg=sg, cc=cc: v.reciprocal(out=T[cc][:], in_=T[sg][:]),
              [sg], [cc], "a2c")
        _scal(kb, lambda v, cc=cc: v.tensor_tensor(out=T[cc][:], in0=T[cc][:],
              in1=T["sgx"][:], op=ALU.mult), [cc, "sgx"], [cc], "a2d")

    kb.pop_scope()
    kb.barrier()
    kb.push_scope()
    kb.sb("wst2", [128, IFF], F32)
    kb.sb("bmax2", [128, 96], F32); kb.sb("bsc2", [128, 96], F32)
    kb.sb("brs2", [128, 96], F32)
    kb.sb("dq12", [128, IFF], F32); kb.sb("dqo2", [128, IFF], BF16)
    kb.sb("swg", [128, 4096], F32)
    kb.sb("swu", [128, 4096], F32)
    kb.sb("sw1", [128, 4096], F32)
    kb.sb("sw2", [128, 4096], F32)
    kb.sb("oq_sl", [128, 24 * 512], BF16)
    kb.sb("wd_c0", [128, 1024], BF16); kb.sb("wd_c1", [128, 1024], BF16)
    kb.sb("xr", [128, D], F32)
    kb.sb("of", [128, D], F32)
    kb.sb("scrF", [128, D], F32)
    wdT_d = kb.dtmp("wdT_d", [IFF, 1024], BF16).ap()

    wd = kb.dram["w_down_d"].ap()
    for ot in range(8):
        p.dma("y", lambda y, ot=ot: y.dma_start(
            out=T["wst2"][:], in_=wd[ot * 128:(ot + 1) * 128, :]),
            writes=["wst2"], tag="wdld")
        emit_dequant_tile(kb, IFF, "wd" + str(ot), sfx="2")
        for grp in range(3):
            def wtr(t, grp=grp):
                pb = T["psA"][:].bitcast(BF16)
                for ci in range(8):
                    inst = t.transpose(pb[:, ci * 128:(ci + 1) * 128],
                                       T["dqo2"][:, (grp * 8 + ci) * 128:(grp * 8 + ci + 1) * 128],
                                       T["c_id128b"][:])
                return inst
            p.op("t", wtr, reads=["dqo2", "c_id128b"], writes=["psA"], tag="wdtr")
            p.op("v", lambda v: v.tensor_copy(
                 out=T["dq12"][:, 0:512].bitcast(BF16),
                 in_=T["psA"][:].bitcast(BF16)[:, 0:1024]), reads=["psA"],
                 writes=["dq12"], tag="wdtre")
            for ci in range(8):
                p.dma("g", lambda g, ot=ot, grp=grp, ci=ci: g.dma_start(
                    out=wdT_d[(grp * 8 + ci) * 128:(grp * 8 + ci + 1) * 128,
                              ot * 128:(ot + 1) * 128],
                    in_=T["dq12"][:, 0:512].bitcast(BF16)[:, ci * 128:(ci + 1) * 128]),
                    reads=["dq12"], tag="wdst")

    p.op("v", lambda v: v.memset(T["mxp"][:], 0.0), writes=["mxp"])
    for ot in range(24):
        p.dma("y", lambda y, ot=ot: y.dma_start(out=T["swg"][:],
              in_=graw_d[ot * 128:(ot + 1) * 128, :]), writes=["swg"], tag="mpld")
        emit_gq_chain(kb, "swg", "sw1", "sw2", p_out=True)
        p.op("v", lambda v: v.tensor_reduce(out=T["mx1"][:], in_=T["sw2"][:],
             axis=AX.X, op=ALU.max, apply_absolute_value=True),
             reads=["sw2"], writes=["mx1"], tag="mpmx", deferred=True)
        p.op("v", lambda v: v.tensor_tensor(out=T["mxp"][:], in0=T["mxp"][:],
             in1=T["mx1"][:], op=ALU.max), reads=["mxp", "mx1"], writes=["mxp"],
             tag="mpmxa")
    _gmax_collective(kb, ["mxp"], ["Mpg"], "p")
    _scal(kb, lambda v: v.tensor_tensor(out=T["c2"][:], in0=T["Mpg"][:],
          in1=T["sgg"][:], op=ALU.mult), ["Mpg", "sgg"], ["c2"], "a3a")
    _scal(kb, lambda v: v.tensor_scalar(out=T["c2"][:], in0=T["c2"][:],
          scalar1=1.0 / (65536.0 * QMAX), scalar2=1e-12, op0=ALU.mult,
          op1=ALU.max), ["c2"], ["c2"], "a3b")
    _scal(kb, lambda v: v.reciprocal(out=T["co"][:], in_=T["c2"][:]),
          ["c2"], ["co"], "a3c")
    _scal(kb, lambda v: v.tensor_tensor(out=T["co"][:], in0=T["co"][:],
          in1=T["sgg"][:], op=ALU.mult), ["co", "sgg"], ["co"], "a3d")
    _scal(kb, lambda v: v.tensor_scalar(out=T["co"][:], in0=T["co"][:],
          scalar1=1.0 / 65536.0, scalar2=None, op0=ALU.mult), ["co"], ["co"],
          "a3e")
    p.op("v", lambda v: v.memset(T["mxo"][:], 0.0), writes=["mxo"])
    for ot in range(24):
        p.dma("y", lambda y, ot=ot: y.dma_start(out=T["swg"][:],
              in_=graw_d[ot * 128:(ot + 1) * 128, :]), writes=["swg"], tag="mold")
        p.dma("y", lambda y, ot=ot: y.dma_start(out=T["swu"][:],
              in_=uraw_d[ot * 128:(ot + 1) * 128, :]), writes=["swu"], tag="mold2")
        emit_o_chain(kb, 4096)
        p.op("v", lambda v: v.tensor_reduce(out=T["mx1"][:], in_=T["sw2"][:],
             axis=AX.X, op=ALU.max, apply_absolute_value=True),
             reads=["sw2"], writes=["mx1"], tag="momx", deferred=True)
        p.op("v", lambda v: v.tensor_tensor(out=T["mxo"][:], in0=T["mxo"][:],
             in1=T["mx1"][:], op=ALU.max), reads=["mxo", "mx1"], writes=["mxo"],
             tag="momxa")
    _gmax_collective(kb, ["mxo"], ["Mog"], "o")
    _scal(kb, lambda v: v.tensor_tensor(out=T["sgo"][:], in0=T["c2"][:],
          in1=T["sgu"][:], op=ALU.mult), ["c2", "sgu"], ["sgo"], "a4a")
    _scal(kb, lambda v: v.tensor_tensor(out=T["Mog"][:], in0=T["Mog"][:],
          in1=T["sgo"][:], op=ALU.mult), ["Mog", "sgo"], ["Mog"], "a4b")
    _scal(kb, lambda v: v.tensor_scalar(out=T["Mog"][:], in0=T["Mog"][:],
          scalar1=1.0 / QMAX, scalar2=1e-12, op0=ALU.mult, op1=ALU.max),
          ["Mog"], ["Mog"], "a4c")
    _scal(kb, lambda v: v.reciprocal(out=T["mx1"][:], in_=T["Mog"][:]),
          ["Mog"], ["mx1"], "a4d")
    _scal(kb, lambda v: v.tensor_tensor(out=T["sgo"][:], in0=T["sgo"][:],
          in1=T["mx1"][:], op=ALU.mult), ["sgo", "mx1"], ["sgo"], "a4e")
    kb.sb("co2", [128, 1], F32)
    _scal(kb, lambda v: v.tensor_tensor(out=T["co2"][:], in0=T["Mog"][:],
          in1=T["Mog"][:], op=ALU.mult), ["Mog"], ["co2"], "a4f")
    _scal(kb, lambda v: v.tensor_scalar(out=T["co2"][:], in0=T["co2"][:],
          scalar1=1.0 / D, scalar2=None, op0=ALU.mult), ["co2"], ["co2"], "a4g")

    emit_ffn_final(kb, x2_d, out_d, uraw_d, graw_d, wdT_d)
    kb.pop_scope()


def emit_gq_chain(kb, src, t1, t2, p_out, width=4096):
    p, T = kb.p, kb.T
    w = width
    p.op("s", lambda s: s.activation(out=T[t1][:, 0:w], in_=T[src][:, 0:w],
         func=ACT.Identity, scale=T["cg"][:, 0:1], bias=T["magicC"][:, 0:1]),
         reads=[src], writes=[t1], scalar_reads=["cg", "magicC"], tag="gq1")
    p.op("v", lambda v: v.tensor_scalar(out=T[t1][:, 0:w], in0=T[t1][:, 0:w],
         scalar1=MAGIC, scalar2=None, op0=ALU.subtract), reads=[t1],
         writes=[t1], tag="gq2")
    p.op("s", lambda s: s.activation(out=T[t2][:, 0:w], in_=T[t1][:, 0:w],
         func=ACT.Sigmoid, scale=T["sgg"][:, 0:1]), reads=[t1], writes=[t2],
         scalar_reads=["sgg"], tag="gq3")
    p.op("s", lambda s: s.activation(out=T[t2][:, 0:w], in_=T[t2][:, 0:w],
         func=ACT.Identity, scale=65536.0, bias=T["magicC"][:, 0:1]),
         reads=[t2], writes=[t2], scalar_reads=["magicC"], tag="gq4")
    p.op("v", lambda v: v.tensor_scalar(out=T[t2][:, 0:w], in0=T[t2][:, 0:w],
         scalar1=MAGIC, scalar2=65535.0, op0=ALU.subtract, op1=ALU.min),
         reads=[t2], writes=[t2], tag="gq5")
    if p_out:
        p.op("v", lambda v: v.tensor_tensor(out=T[t2][:, 0:w], in0=T[t1][:, 0:w],
             in1=T[t2][:, 0:w], op=ALU.mult), reads=[t1, t2], writes=[t2],
             tag="gq6")


def emit_o_chain(kb, width):
    p, T = kb.p, kb.T
    w = width
    emit_gq_chain(kb, "swg", "sw1", "sw2", p_out=True, width=w)
    p.op("s", lambda s: s.activation(out=T["sw2"][:, 0:w], in_=T["sw2"][:, 0:w],
         func=ACT.Identity, scale=T["co"][:, 0:1], bias=T["magicC"][:, 0:1]),
         reads=["sw2"], writes=["sw2"], scalar_reads=["co", "magicC"], tag="oc1")
    p.op("v", lambda v: v.tensor_scalar(out=T["sw2"][:, 0:w], in0=T["sw2"][:, 0:w],
         scalar1=MAGIC, scalar2=None, op0=ALU.subtract), reads=["sw2"],
         writes=["sw2"], tag="oc2")
    p.op("s", lambda s: s.activation(out=T["sw1"][:, 0:w], in_=T["swu"][:, 0:w],
         func=ACT.Identity, scale=T["cu"][:, 0:1], bias=T["magicC"][:, 0:1]),
         reads=["swu"], writes=["sw1"], scalar_reads=["cu", "magicC"], tag="oc3")
    p.op("v", lambda v: v.tensor_scalar(out=T["sw1"][:, 0:w], in0=T["sw1"][:, 0:w],
         scalar1=MAGIC, scalar2=None, op0=ALU.subtract), reads=["sw1"],
         writes=["sw1"], tag="oc4")
    p.op("v", lambda v: v.tensor_tensor(out=T["sw2"][:, 0:w], in0=T["sw2"][:, 0:w],
         in1=T["sw1"][:, 0:w], op=ALU.mult), reads=["sw1", "sw2"], writes=["sw2"],
         tag="oc5")


def emit_ffn_final(kb, x2_d, out_d, uraw_d, graw_d, wdT_d):
    p, T = kb.p, kb.T
    NSLAB = NTOK // 512
    for sl in range(NSLAB):
        for ic in range(24):
            p.dma("y", lambda y, ic=ic, sl=sl: y.dma_start(
                out=T["swg"][:, 0:512],
                in_=graw_d[ic * 128:(ic + 1) * 128, sl * 512:(sl + 1) * 512]),
                writes=["swg"], tag=f"f4g{sl}_{ic}")
            p.dma("y", lambda y, ic=ic, sl=sl: y.dma_start(
                out=T["swu"][:, 0:512],
                in_=uraw_d[ic * 128:(ic + 1) * 128, sl * 512:(sl + 1) * 512]),
                writes=["swu"], tag=f"f4u{sl}_{ic}")
            emit_o_chain(kb, 512)
            p.op("s", lambda s: s.activation(out=T["sw2"][:, 0:512],
                 in_=T["sw2"][:, 0:512], func=ACT.Identity,
                 scale=T["sgo"][:, 0:1], bias=T["magicC"][:, 0:1]),
                 reads=["sw2"], writes=["sw2"], scalar_reads=["sgo", "magicC"],
                 tag=f"f4r{sl}_{ic}")
            p.op("v", lambda v, ic=ic: v.tensor_scalar(
                 out=T["oq_sl"][:, ic * 512:(ic + 1) * 512], in0=T["sw2"][:, 0:512],
                 scalar1=MAGIC, scalar2=None, op0=ALU.subtract),
                 reads=["sw2"], writes=["oq_sl"], tag=f"f4q{sl}_{ic}")
        pmap = [("psB", 0), ("psC", 0), ("psC", 1024), ("psA", 0)]
        for ic in range(24):
            buf = "wd_c0" if ic % 2 == 0 else "wd_c1"
            p.dma("g", lambda g, ic=ic, buf=buf: g.dma_start(
                out=T[buf][:], in_=wdT_d[ic * 128:(ic + 1) * 128, :]),
                writes=[buf], tag=f"f4w{sl}_{ic}")
            def dmm(t, ic=ic, buf=buf):
                for tt in range(4):
                    ps, off = pmap[tt]
                    for q in range(2):
                        inst = t.matmul(
                            T[ps][:, off + q * 512:off + (q + 1) * 512],
                            T["oq_sl"][:, ic * 512 + tt * 128:ic * 512 + (tt + 1) * 128],
                            T[buf][:, q * 512:(q + 1) * 512],
                            start=(ic == 0), stop=(ic == 23))
                return inst
            p.op("t", dmm, reads=["oq_sl", buf], writes=["psA", "psB", "psC"],
                 tag=f"f4mm{sl}_{ic}")
        for tt in range(4):
            ps, off = pmap[tt]
            ti = sl * 4 + tt
            sq, r0 = divmod(ti * 128, S)
            p.dma("y", lambda y, sq=sq, r0=r0: y.dma_start(
                out=T["xr"][:], in_=x2_d[sq, r0:r0 + 128, :]), writes=["xr"],
                tag=f"f4x{sl}_{tt}")
            _ln_ops(kb, ps, lambda T, ps=ps, off=off: T[ps][:, off:off + 1024],
                    "of", lambda T: T["of"][:], "m2", "r2", "b2",
                    tag=f"ln4_{sl}_{tt}", scratch="scrF",
                    in_scale="Mog", in_scale_sq="co2")
            p.op("v", lambda v: v.tensor_tensor(out=T["of"][:], in0=T["of"][:],
                 in1=T["xr"][:], op=ALU.add), reads=["of", "xr"], writes=["of"],
                 tag=f"f4res{sl}_{tt}")
            p.dma("g", lambda g, sq=sq, r0=r0: g.dma_start(
                out=out_d[sq, r0:r0 + 128, :], in_=T["of"][:]), reads=["of"],
                tag=f"f4o{sl}_{tt}")


def build_full(kb):
    x2_d, x2ln_d = build_attn(kb)
    out_d = kb.dout("out", [NSEQ, S, D]).ap()
    if kb.stage in ("A", "A1", "B", "B1", "M", "M1", "S0", "S1", "SZ", "SA",
                    "SB", "Sg"):
        final_barrier(kb)
        return
    kb.barrier()
    uraw_d, graw_d = emit_ffn_head(kb, x2_d, x2ln_d)
    emit_ffn_f1(kb, uraw_d, graw_d)
    emit_ffn_rest(kb, x2_d, x2ln_d, out_d, uraw_d, graw_d)
    final_barrier(kb)


_CACHE = {}


def _build():
    if "nc" not in _CACHE:
        nc = bass.Bass("TRN2", target_bir_lowering=False, debug=False)
        kb = KB(nc, stage="full", dbg=[])
        build_full(kb)
        lower(kb)
        _CACHE["nc"] = nc
    return _CACHE["nc"]


def kernel(L0_attn_x, L0_attn_Wq, L0_attn_Wk, L0_attn_Wv, L0_attn_Wa,
           L0_attn_ba, L0_attn_Wb, L0_attn_bb, L0_attn_Wg, L0_attn_Wo,
           L0_attn_g_norm, L0_ffn_w_gate, L0_ffn_w_up, L0_ffn_w_down):
    nc = _build()
    cs = host_constants()
    x = np.ascontiguousarray(np.asarray(L0_attn_x, dtype=np.float32))
    common = {
        "Wq": np.ascontiguousarray(L0_attn_Wq, dtype=np.float32),
        "Wk": np.ascontiguousarray(L0_attn_Wk, dtype=np.float32),
        "Wv": np.ascontiguousarray(L0_attn_Wv, dtype=np.float32),
        "Wg": np.ascontiguousarray(L0_attn_Wg, dtype=np.float32),
        "Wo": np.ascontiguousarray(L0_attn_Wo, dtype=np.float32),
        "Wa": np.ascontiguousarray(L0_attn_Wa, dtype=np.float32),
        "Wb": np.ascontiguousarray(L0_attn_Wb, dtype=np.float32),
        "i_ba_bc": np.ascontiguousarray(
            np.broadcast_to(np.asarray(L0_attn_ba, np.float32), (128, NH))),
        "i_bb_bc": np.ascontiguousarray(
            np.broadcast_to(np.asarray(L0_attn_bb, np.float32), (128, NH))),
        "i_gnorm_bc": np.ascontiguousarray(
            np.broadcast_to(np.asarray(L0_attn_g_norm, np.float32), (128, D))),
        "w_gate_d": np.ascontiguousarray(L0_ffn_w_gate, dtype=np.float32),
        "w_up_d": np.ascontiguousarray(L0_ffn_w_up, dtype=np.float32),
        "w_down_d": np.ascontiguousarray(L0_ffn_w_down, dtype=np.float32),
    }
    for k, v in cs.items():
        if k != "c_indh":
            common["i_" + k] = v
    in_maps = [dict(common, x=np.ascontiguousarray(x[2 * c:2 * c + 2]))
               for c in range(8)]
    res = run_bass_kernel_spmd(nc, in_maps, core_ids=list(range(8)))
    out = np.concatenate(
        [np.asarray(res.results[c]["out"], dtype=np.float32).reshape(NSEQ, S, D)
         for c in range(8)], axis=0)
    return out


# version 03
# speedup vs baseline: 1.0162x; 1.0162x over previous
import numpy as np
from collections import defaultdict
from contextlib import ExitStack

import concourse.bass as bass
import concourse.mybir as mybir
from concourse.bass_utils import run_bass_kernel_spmd

ENGINES = ("t", "v", "s", "g", "y")


class Op:
    __slots__ = ("eng", "fn", "reads", "writes", "scalar_reads", "is_dma",
                 "idx", "dma_idx", "deps", "inc", "tag", "deferred", "is_cc")

    def __init__(self, eng, fn, reads, writes, scalar_reads, is_dma, tag,
                 deferred=False, is_cc=False):
        self.eng = eng
        self.fn = fn
        self.reads = reads
        self.writes = writes
        self.scalar_reads = scalar_reads
        self.is_dma = is_dma
        self.idx = -1
        self.dma_idx = -1
        self.deps = []
        self.inc = False
        self.tag = tag
        self.deferred = deferred
        self.is_cc = is_cc


class Prog:
    def __init__(self):
        self.small_res = set()
        self.ops = {e: [] for e in ENGINES}
        self.all_ops = []
        self.res = defaultdict(lambda: [None, []])
        self.ndma = {e: 0 for e in ENGINES}

    def op(self, eng, fn, reads=(), writes=(), scalar_reads=(), tag="",
           deferred=False, is_cc=False):
        if not deferred and (set(writes) & self.small_res):
            deferred = True
        o = Op(eng, fn, tuple(reads) + tuple(scalar_reads), tuple(writes),
               tuple(scalar_reads), False, tag, deferred, is_cc)
        self._record(o)
        return o

    def dma(self, eng, fn, reads=(), writes=(), tag=""):
        assert eng in ("g", "y", "s", "v")
        o = Op(eng, fn, tuple(reads), tuple(writes), (), True, tag)
        o.dma_idx = self.ndma[eng]
        self.ndma[eng] += 1
        self._record(o)
        return o

    def _record(self, o):
        o.idx = len(self.ops[o.eng])
        self.ops[o.eng].append(o)
        self.all_ops.append(o)
        deps = []
        for r in o.reads:
            w, _readers = self.res[r]
            if w is not None:
                deps.append(w)
            self.res[r][1].append(o)
        for w_ in o.writes:
            lw, readers = self.res[w_]
            if lw is not None:
                deps.append(lw)
            deps.extend(readers)
            self.res[w_][0] = o
            self.res[w_][1] = []
        seen = set()
        for d in deps:
            if d is o:
                continue
            if id(d) in seen:
                continue
            seen.add(id(d))
            if d.eng == o.eng and not d.is_dma and not d.is_cc:
                continue
            o.deps.append(d)

    def finalize(self, nc, engines, sems, dma_sems, verbose=False):
        for o in self.all_ops:
            if o.is_cc:
                o.inc = True
            for d in o.deps:
                if not d.is_dma:
                    d.inc = True
        inc_count = {}
        for e in ENGINES:
            c = 0
            for o in self.ops[e]:
                if o.inc and not o.is_dma:
                    c += 1
                inc_count[id(o)] = c
        plans = {e: [] for e in ENGINES}
        for e in ENGINES:
            last_wait = defaultdict(int)
            pend_w = set()
            pend_r = set()
            for o in self.ops[e]:
                waits = []
                for d in o.deps:
                    if d.is_dma:
                        key = ("d", d.eng, d.dma_idx % 8)
                        val = 16 * (d.dma_idx // 8 + 1)
                    else:
                        key = ("c", d.eng)
                        val = inc_count[id(d)]
                        assert val > 0
                    if val > last_wait[key]:
                        last_wait[key] = val
                        waits.append((key, val))
                rd, wr = set(o.reads), set(o.writes)
                need_gap = bool((rd & pend_w) or (wr & pend_w) or (wr & pend_r))
                if need_gap:
                    pend_w.clear()
                    pend_r.clear()
                pend_w.update(wr)
                pend_r.update(rd)
                plans[e].append((o, waits, need_gap))
        self._plans = plans
        self._inc_count = inc_count
        return plans

    def emit(self, e, eng_obj, sems, dma_sems):
        for o, waits, need_gap in self._plans[e]:
            for key, val in waits:
                if key[0] == "d":
                    eng_obj.wait_ge(dma_sems[key[1]][key[2]], val)
                else:
                    eng_obj.wait_ge(sems[key[1]], val)
            if need_gap:
                eng_obj.drain()
            if o.is_dma:
                slot = o.dma_idx % 8
                rnd = o.dma_idx // 8
                if rnd > 0:
                    eng_obj.wait_ge(dma_sems[e][slot], 16 * rnd)
                inst = o.fn(eng_obj)
                inst.then_inc(dma_sems[e][slot], 16)
            else:
                inst = o.fn(eng_obj)
                if o.inc:
                    if inst is None:
                        eng_obj.nop().then_inc(sems[e], 1)
                    else:
                        inst.then_inc(sems[e], 1)

    def stats(self):
        return {e: (len(self.ops[e]), sum(1 for o in self.ops[e] if o.inc),
                    self.ndma[e]) for e in ENGINES}


F32 = mybir.dt.float32
BF16 = mybir.dt.bfloat16
I16 = mybir.dt.int16
ALU = mybir.AluOpType
ACT = mybir.ActivationFunctionType
AX = mybir.AxisListType

D = 1024
NH = 16
HD = 64
C = 64
S = 2048
NSTEP = S // C
NSEQ = 2
IFF = 3072
NTOK = NSEQ * S
MAGIC = float(1.5 * 2 ** 23)
BIGNEG = -1.0e30
QMAX = 32767.0
EPS = 1e-5

def host_constants():
    cs = {}
    cs["c_id128f"] = np.eye(128, dtype=np.float32)
    cs["c_id128b"] = np.eye(128, dtype=np.float32)
    triu = np.triu(np.ones((C, C), np.float32))
    z = np.zeros((C, C), np.float32)
    cs["c_tril2"] = np.block([[triu, z], [z, triu]])
    sel = np.zeros((128, 128), np.float32)
    sel[C - 1, 0:C] = 1.0
    sel[127, C:128] = 1.0
    cs["c_selC"] = sel
    q = np.arange(128)
    s_in = q % C
    par_row = (q // C)
    t_idx = np.arange(C)
    dind = (s_in[:, None] == t_idx[None, :]).astype(np.float32)
    blkN = np.zeros((128, 2 * C), np.float32)
    blkN[:C, :C] = dind[:C]
    blkN[C:, C:] = dind[C:]
    cs["c_dindN"] = np.tile(blkN, (1, NH))
    cs["c_dindW"] = np.tile(dind, (1, NH))
    okN = np.zeros((128, 2 * C), np.float32) + BIGNEG
    okN[:C, :C] = np.where(s_in[:C, None] < t_idx[None, :], 0.0, BIGNEG)
    okN[C:, C:] = np.where(s_in[C:, None] < t_idx[None, :], 0.0, BIGNEG)
    cs["c_negN"] = np.tile(okN, (1, NH))
    okW = np.where(s_in[:, None] <= t_idx[None, :], 0.0, BIGNEG)
    cs["c_negW"] = np.tile(okW, (1, NH))
    ind = np.zeros((16, 8 * 128), np.float32)
    for ci in range(8):
        ind[2 * ci, ci * 128: ci * 128 + 64] = 1.0
        ind[2 * ci + 1, ci * 128 + 64: ci * 128 + 128] = 1.0
    cs["c_indh"] = ind
    cs["c_ones2"] = np.block([[np.ones((C, C), np.float32), z],
                              [z, np.ones((C, C), np.float32)]])
    okT = np.zeros((128, 2 * C), np.float32) + BIGNEG
    okT[:C, :C] = np.where(t_idx[None, :] < s_in[:C, None], 0.0, BIGNEG)
    okT[C:, C:] = np.where(t_idx[None, :] < s_in[C:, None], 0.0, BIGNEG)
    cs["c_negNT"] = np.tile(okT, (1, NH))
    i1 = np.zeros((16, 1024), np.float32)
    for ci in range(8):
        for pp in range(128):
            i1[2 * ci + pp // 64, ci * 128 + pp] = 1.0
    cs["c_indh1"] = i1
    s4 = np.zeros((128, 512), np.float32)
    for e in range(2):
        for sq in range(2):
            for pp in range(128):
                if pp // 64 == e:
                    s4[63 + 64 * sq, (e * 2 + sq) * 128 + pp] = 1.0
    cs["c_sel4"] = s4
    return cs


class KB:

    def __init__(self, nc, stage="full", dbg=()):
        self.nc = nc
        self.p = Prog()
        self.stage = stage
        self.dbg = set(dbg)
        self.es = ExitStack()
        self.scope = None
        self.T = {}
        self.dram = {}
        self.dbg_outs = {}

    def push_scope(self):
        self.scope = ExitStack()

    def pop_scope(self):
        self.scope.close()
        self.scope = None

    def barrier(self):
        p = self.p
        lasts = {}
        for e in ("t", "v", "s", "g", "y"):
            real = [op for op in p.ops[e] if not op.is_dma]
            lasts[e] = real[-1] if real else None
        dlasts = {}
        for e in ("t", "v", "s", "g", "y"):
            per_slot = {}
            for op in p.ops[e]:
                if op.is_dma:
                    per_slot[op.dma_idx % 8] = op
            dlasts[e] = list(per_slot.values())
        for e in ("t", "v", "s", "g", "y"):
            o = p.op(e, lambda eng: eng.nop(), tag="barr")
            for x in ("t", "v", "s", "g", "y"):
                if x != e and lasts[x] is not None:
                    o.deps.append(lasts[x])
                o.deps.extend(dlasts[x])

    def sb(self, name, shape, dt):
        es = self.scope if self.scope is not None else self.es
        self.T[name] = es.enter_context(self.nc.sbuf_tensor(name, shape, dt))
        import numpy as _np
        if int(_np.prod(shape[1:])) <= 64:
            self.p.small_res.add(name)
        return self.T[name]

    def ps(self, name, shape):
        self.T[name] = self.es.enter_context(self.nc.psum_tensor(name, shape, F32))
        return self.T[name]

    def din(self, name, shape, dt=F32):
        self.dram[name] = self.nc.dram_tensor(name, shape, dt, kind="ExternalInput")
        return self.dram[name]

    def dout(self, name, shape, dt=F32):
        self.dram[name] = self.nc.dram_tensor(name, shape, dt, kind="ExternalOutput")
        return self.dram[name]

    def dtmp(self, name, shape, dt=F32):
        self.dram[name] = self.nc.dram_tensor(name, shape, dt)
        return self.dram[name]

    def dbg_out(self, key, shape, dt=F32):
        if key in self.dbg:
            self.dbg_outs[key] = self.dout("dbg_" + key, shape, dt)
            return self.dram["dbg_" + key].ap()
        return None

    def dump(self, key, tile_ap, reads):
        if key in self.dbg_outs:
            d = self.dram["dbg_" + key].ap()
            self.p.dma("g", lambda g, d=d, t=tile_ap: g.dma_start(out=d, in_=t),
                       reads=reads, tag="dbg_" + key)

    def op(self, *a, **k):
        return self.p.op(*a, **k)

    def dma(self, *a, **k):
        return self.p.dma(*a, **k)


def build_attn(kb, ncores_note=""):
    nc, p, T = kb.nc, kb.p, kb.T
    x_d = kb.din("x", [NSEQ, S, D]).ap()
    Wnames = ["Wq", "Wk", "Wv", "Wg", "Wo"]
    for w in Wnames:
        kb.din(w, [D, D])
    kb.din("Wa", [D, NH]); kb.din("Wb", [D, NH])
    kb.din("i_ba_bc", [128, NH]); kb.din("i_bb_bc", [128, NH])
    kb.din("i_gnorm_bc", [128, D])
    for cname, cshape, cdt in [
        ("c_id128f", [128, 128], F32), ("c_id128b", [128, 128], F32),
        ("c_tril2", [128, 128], F32), ("c_selC", [128, 128], F32),
        ("c_dindN", [128, 2048], F32), ("c_dindW", [128, 1024], F32),
        ("c_negN", [128, 2048], F32), ("c_negW", [128, 1024], F32),
        ("c_indh1", [16, 1024], F32), ("c_ones2", [128, 128], F32),
        ("c_negNT", [128, 2048], F32), ("c_sel4", [128, 512], F32),
    ]:
        kb.din("i_" + cname, cshape, cdt)
    x2_d = kb.dtmp("x2_d", [NSEQ, S, D]).ap()
    x2ln_d = kb.dtmp("x2ln_d", [NSEQ, S, D]).ap()

    for cn in ["id128f"]:
        pass
    kb.sb("c_id128f", [128, 128], F32)
    kb.sb("c_id128b", [128, 128], BF16)
    for nm in ["sum1", "mn", "ssq", "sd", "rstd", "bias2", "m2", "r2", "b2",
               "m3", "r3", "b3", "mx1", "eps1", "magicC"]:
        kb.sb(nm, [128, 1], F32)
    kb.sb("mxacc", [128, 1], F32)
    kb.push_scope()
    for w in Wnames:
        kb.sb("w_" + w, [128, 8 * 1024], BF16)
    kb.sb("w_ab", [128, 8 * 32], BF16)
    kb.sb("ba_bc", [128, NH], F32); kb.sb("bb_bc", [128, NH], F32)
    kb.sb("gnorm_bc", [128, D], F32)
    consts_b = [("dindN", BF16), ("dindW", BF16),
                ("negN", BF16), ("negW", BF16), ("ones2", BF16), ("negNT", BF16)]
    for cn, cdt in consts_b:
        kb.sb("c_" + cn, [128, {"dindN": 2048, "negN": 2048, "negNT": 2048,
                                "dindW": 1024, "negW": 1024}.get(cn, 128)], cdt)
    for cn in ["tril2", "selC"]:
        kb.sb("c_" + cn, [128, 128], F32)
    kb.sb("c_indh1", [16, 1024], F32)
    kb.sb("c_sel4", [128, 512], F32)
    kb.sb("scT3", [16, 128], F32)
    kb.sb("dN2", [128, 2048], BF16)
    kb.sb("x2ln_c", [128, D], F32)
    kb.sb("e_lC_st", [128, 16], F32)

    kb.sb("xc", [128, D], F32)
    kb.sb("xln_tm", [128, D], BF16)
    kb.sb("xln_fm", [128, D], BF16)
    kb.sb("q_fm", [128, D], BF16); kb.sb("k_fm", [128, D], BF16)
    kb.sb("qh_fm", [128, D], BF16)
    kb.sb("q_tm", [128, D], BF16); kb.sb("k_tm", [128, D], BF16)
    kb.sb("v_tm", [128, D], BF16)
    kb.sb("g_tm", [128, D], F32)
    kb.sb("R", [128, 2048], BF16)
    kb.sb("kt_tm", [128, D], BF16)
    kb.sb("scrD", [128, D], F32)
    kb.sb("scrD2", [128, D], F32)
    kb.sb("dN", [128, 2048], BF16)
    kb.sb("nbar", [128, 2048], BF16)
    kb.sb("pbar", [128, 2048], BF16)
    kb.sb("dW", [128, 1024], BF16)
    kb.sb("wbar", [128, 1024], BF16)
    kb.sb("Xc", [128, 2048], BF16)
    kb.sb("xkT", [128, 1024], BF16)
    kb.sb("gt", [128, 1024], BF16)
    kb.sb("y_tm", [128, D], F32)
    kb.sb("yf_fm", [128, D], BF16)
    kb.sb("negS", [128, 1024], F32)
    kb.sb("negS_b", [128, 1024], BF16)
    for nm in ["zat", "zbt", "alpha", "beta", "loga", "logb", "ell", "ellprev",
               "glt", "ellC", "e_l", "e_lp", "e_ct", "e_lC", "d1",
               "lhi", "llo", "nlhi", "nllo", "ghi", "glo", "negell",
               "ssq_q", "ssq_k", "nq", "nk", "rn_q", "rn_k",
               "skh", "skh2", "sct", "sq2", "sq2n", "lnrnq", "lnrnk",
               "biasWN", "nlt", "wlt", "whi", "wlo", "bhi", "blo"]:
        kb.sb(nm, [128, NH], F32)
    kb.sb("hilo16", [128, NH], BF16)
    for nm in ["scTq", "scTqh", "scTk"]:
        kb.sb(nm, [16, 128], F32)
    kb.sb("sc3", [128, 64], F32)
    kb.ps("psA", [128, 1024])
    kb.ps("psB", [128, 1024])
    kb.ps("psC", [128, 2048])
    psum_init(kb)

    for w in Wnames:
        wd = kb.dram[w].ap().rearrange("(i p) o -> p i o", p=128)
        p.dma("g", lambda g, o=T["w_" + w], i=wd: g.dma_start(out=o[:].rearrange(
            "p (i o) -> p i o", i=8), in_=i), writes=["w_" + w], tag="ldw")
    wa = kb.dram["Wa"].ap().rearrange("(i p) o -> p i o", p=128)
    wb = kb.dram["Wb"].ap().rearrange("(i p) o -> p i o", p=128)
    p.dma("g", lambda g, i=wa: g.dma_start(
        out=T["w_ab"][:].rearrange("p (i o) -> p i o", i=8)[:, :, 0:16], in_=i),
        writes=["w_ab"], tag="ldab")
    p.dma("g", lambda g, i=wb: g.dma_start(
        out=T["w_ab"][:].rearrange("p (i o) -> p i o", i=8)[:, :, 16:32], in_=i),
        writes=["w_ab"], tag="ldab")
    for nm in ["ba_bc", "bb_bc", "gnorm_bc"]:
        p.dma("y", lambda y, o=T[nm], i=kb.dram["i_" + nm].ap(): y.dma_start(out=o[:], in_=i),
              writes=[nm], tag="ldc")
    for cn, cdt in consts_b + [("id128b", BF16)]:
        p.dma("g", lambda g, o=T["c_" + cn], i=kb.dram["i_c_" + cn].ap():
              g.dma_start(out=o[:], in_=i), writes=["c_" + cn], tag="ldc")
    for cn in ["id128f", "tril2", "selC", "indh1", "sel4"]:
        p.dma("y", lambda y, o=T["c_" + cn], i=kb.dram["i_c_" + cn].ap():
              y.dma_start(out=o[:], in_=i), writes=["c_" + cn], tag="ldc")
    p.op("v", lambda v: v.memset(T["eps1"][:], EPS), writes=["eps1"])
    p.op("v", lambda v: v.memset(T["magicC"][:], MAGIC), writes=["magicC"])
    p.op("v", lambda v: v.memset(T["negS"][:], 0.0), writes=["negS"])
    p.op("v", lambda v: v.memset(T["negS_b"][:], 0.0), writes=["negS_b"])
    p.op("v", lambda v: v.memset(T["mxacc"][:], 0.0), writes=["mxacc"])

    for key, shp, dt in [("qh_fm", [128, D], F32), ("zat", [128, NH], F32),
                         ("scT3", [16, 128], F32), ("sq2n", [128, NH], F32),
                         ("qfm", [128, D], F32),
                         ("alpha", [128, NH], F32), ("loga", [128, NH], F32),
                         ("xlntm", [128, D], F32), ("xlnfm", [128, D], F32),
                         ("mn", [128, 1], F32), ("sum1", [128, 1], F32),
                         ("rstd", [128, 1], F32),
                         ("ssqd", [128, 1], F32), ("xcc", [128, D], F32),
                         ("R", [128, 2048], F32), ("ell", [128, NH], F32),
                         ("glt", [128, NH], F32), ("kt", [128, D], F32),
                         ("nbar", [128, 2048], F32), ("wbar", [128, 1024], F32),
                         ("Xs", [128, 2048], F32), ("gt", [128, 1024], F32),
                         ("y0", [128, D], F32), ("y1", [128, D], F32),
                         ("x2_0", [128, D], F32), ("x2ln_0", [128, D], F32)]:
        kb.dbg_out(key, shp, dt)

    nsteps = 1 if kb.stage in ("A1", "B1", "M1", "S1", "S0", "Sg", "SZ", "SA", "SB") else NSTEP
    for step in range(nsteps):
        emit_attn_step(kb, step, x_d, x2_d, x2ln_d)
        emit_attn_step2(kb, step)
        if kb.stage in ("A",):
            continue
        emit_attn_step3(kb, step)
        if kb.stage in ("M1", "M"):
            continue
        emit_attn_step4(kb, step, x2_d, x2ln_d)
    kb.pop_scope()
    return x2_d, x2ln_d


def _ln_ops(kb, src_res, src_ap_fn, out_res, out_ap, mn_t, r_t, b_t,
            out_dt_note="", src_is_psum=False, tag="ln", scratch="scrD2",
            in_scale=None, in_scale_sq=None):
    p, T = kb.p, kb.T
    sumr = ["sum1"]
    p.op("v", lambda v: v.tensor_reduce(out=T["sum1"][:], in_=src_ap_fn(T),
         axis=AX.X, op=ALU.add), reads=[src_res], writes=["sum1"],
         tag=tag + "_sum", deferred=True)
    p.op("v", lambda v: v.tensor_scalar(out=T[mn_t][:], in0=T["sum1"][:],
         scalar1=-1.0 / D, scalar2=None, op0=ALU.mult),
         reads=["sum1"], writes=[mn_t], tag=tag + "_mn")
    p.op("s", lambda s: s.activation(out=T[scratch][:], in_=src_ap_fn(T),
         func=ACT.Square, bias=T[mn_t][:, 0:1], scale=1.0, accum_out=T["ssq"][:]),
         reads=[src_res], writes=[scratch, "ssq"], scalar_reads=[mn_t],
         tag=tag + "_sq", deferred=True)
    if in_scale is None:
        p.op("s", lambda s: s.activation(out=T["sd"][:], in_=T["ssq"][:],
             func=ACT.Sqrt, bias=T["eps1"][:, 0:1], scale=1.0 / D),
             reads=["ssq", "eps1"], writes=["sd"], tag=tag + "_sd")
    else:
        p.op("s", lambda s: s.activation(out=T["sd"][:], in_=T["ssq"][:],
             func=ACT.Sqrt, bias=T["eps1"][:, 0:1],
             scale=T[in_scale_sq][:, 0:1]),
             reads=["ssq", "eps1"], writes=["sd"],
             scalar_reads=[in_scale_sq], tag=tag + "_sd")
    p.op("v", lambda v: v.reciprocal(out=T[r_t][:], in_=T["sd"][:]),
         reads=["sd"], writes=[r_t], tag=tag + "_r", deferred=True)
    if in_scale is not None:
        p.op("v", lambda v: v.tensor_tensor(out=T[r_t][:], in0=T[r_t][:],
             in1=T[in_scale][:], op=ALU.mult), reads=[r_t, in_scale],
             writes=[r_t], tag=tag + "_rs", deferred=True)
    p.op("v", lambda v: v.tensor_tensor(out=T[b_t][:], in0=T[mn_t][:],
         in1=T[r_t][:], op=ALU.mult), reads=[mn_t, r_t], writes=[b_t],
         tag=tag + "_b", deferred=True)
    p.op("s", lambda s: s.activation(out=out_ap(T), in_=src_ap_fn(T),
         func=ACT.Identity, scale=T[r_t][:, 0:1], bias=T[b_t][:, 0:1]),
         reads=[src_res], writes=[out_res], scalar_reads=[r_t, b_t],
         tag=tag + "_app")


def emit_attn_step(kb, step, x_d, x2_d, x2ln_d):
    p, T = kb.p, kb.T
    r0, r1 = step * C, (step + 1) * C
    dbg = (step == 0)

    for sq in range(NSEQ):
        p.dma("y", lambda y, sq=sq: y.dma_start(
            out=T["xc"][sq * C:(sq + 1) * C, :], in_=x_d[sq, r0:r1, :]),
            writes=["xc"], tag=f"ldx{step}")

    _ln_ops(kb, "xc", lambda T: T["xc"][:], "xln_tm",
            lambda T: T["xln_tm"][:], "mn", "rstd", "bias2", tag=f"ln1_{step}")

    def tr8(t, src="xln_tm"):
        pb = T["psA"][:].bitcast(BF16)
        for ci in range(8):
            inst = t.transpose(pb[:, ci * 128:(ci + 1) * 128],
                               T[src][:, ci * 128:(ci + 1) * 128], T["c_id128b"][:])
        return inst
    p.op("t", lambda t: tr8(t), reads=["xln_tm", "c_id128b"],
         writes=["psA"], tag=f"trx{step}")
    p.op("v", lambda v: v.tensor_copy(out=T["xln_fm"][:],
         in_=T["psA"][:].bitcast(BF16)[:, 0:1024]),
         reads=["psA"], writes=["xln_fm"], tag=f"trxe{step}")

    def proj_fm(t, wname):
        for j in range(8):
            for i in range(8):
                inst = t.matmul(T["psB"][:, j * 128:(j + 1) * 128],
                                T["w_" + wname][:, i * 1024 + j * 128:i * 1024 + (j + 1) * 128],
                                T["xln_fm"][:, i * 128:(i + 1) * 128],
                                start=(i == 0), stop=(i == 7))
        return inst
    for wname, dst in [("Wq", "q_fm"), ("Wk", "k_fm")]:
        p.op("t", lambda t, w=wname: proj_fm(t, w), reads=["w_" + wname, "xln_fm"],
             writes=["psB"], tag=f"pfm{wname}{step}")
        p.op("s", lambda s, d=dst: s.copy(out=T[d][:], in_=T["psB"][:]),
             reads=["psB"], writes=[dst], tag=f"pfme{wname}{step}")

    def proj_tm(t, wname):
        for half in range(2):
            for i in range(8):
                inst = t.matmul(T["psB"][:, half * 512:(half + 1) * 512],
                                T["xln_fm"][:, i * 128:(i + 1) * 128],
                                T["w_" + wname][:, i * 1024 + half * 512:
                                                i * 1024 + (half + 1) * 512],
                                start=(i == 0), stop=(i == 7))
        return inst
    for wname, dst in [("Wq", "q_tm"), ("Wk", "k_tm"), ("Wv", "v_tm")]:
        p.op("t", lambda t, w=wname: proj_tm(t, w), reads=["w_" + wname, "xln_fm"],
             writes=["psB"], tag=f"ptm{wname}{step}")
        p.op("v", lambda v, d=dst: v.tensor_copy(out=T[d][:], in_=T["psB"][:]),
             reads=["psB"], writes=[dst], tag=f"ptme{wname}{step}")
    p.op("t", lambda t: proj_tm(t, "Wg"), reads=["w_Wg", "xln_fm"],
         writes=["psB"], tag=f"ptmWg{step}")
    p.op("s", lambda s: s.activation(out=T["g_tm"][:], in_=T["psB"][:],
         func=ACT.Silu), reads=["psB"], writes=["g_tm"], tag=f"silu{step}")

    def logits(t):
        for i in range(8):
            inst = t.matmul(T["psA"][:, 896:928], T["xln_fm"][:, i * 128:(i + 1) * 128],
                            T["w_ab"][:, i * 32:(i + 1) * 32],
                            start=(i == 0), stop=(i == 7))
        return inst
    p.op("t", logits, reads=["w_ab", "xln_fm"], writes=["psA"], tag=f"lg{step}")
    p.op("v", lambda v: v.tensor_tensor(out=T["zat"][:], in0=T["psA"][:, 896:912],
         in1=T["ba_bc"][:], op=ALU.add), reads=["psA", "ba_bc"], writes=["zat"])
    p.op("v", lambda v: v.tensor_tensor(out=T["zbt"][:], in0=T["psA"][:, 912:928],
         in1=T["bb_bc"][:], op=ALU.add), reads=["psA", "bb_bc"], writes=["zbt"])
    p.op("s", lambda s: s.activation(out=T["alpha"][:], in_=T["zat"][:],
         func=ACT.Sigmoid), reads=["zat"], writes=["alpha"])
    p.op("s", lambda s: s.activation(out=T["beta"][:], in_=T["zbt"][:],
         func=ACT.Sigmoid), reads=["zbt"], writes=["beta"])
    p.op("s", lambda s: s.activation(out=T["loga"][:], in_=T["alpha"][:],
         func=ACT.Ln), reads=["alpha"], writes=["loga"])
    p.op("s", lambda s: s.activation(out=T["logb"][:], in_=T["beta"][:],
         func=ACT.Ln), reads=["beta"], writes=["logb"])

    p.op("t", lambda t: t.matmul(T["psA"][:, 928:944], T["c_tril2"][:],
         T["loga"][:], start=True, stop=True), reads=["c_tril2", "loga"],
         writes=["psA"], tag=f"cum{step}")
    p.op("v", lambda v: v.tensor_copy(out=T["ell"][:], in_=T["psA"][:, 928:944]),
         reads=["psA"], writes=["ell"])
    p.op("t", lambda t: t.matmul(T["psA"][:, 928:944], T["c_selC"][:],
         T["ell"][:], start=True, stop=True), reads=["c_selC", "ell"],
         writes=["psA"], tag=f"selC{step}")
    p.op("v", lambda v: v.tensor_copy(out=T["ellC"][:], in_=T["psA"][:, 928:944]),
         reads=["psA"], writes=["ellC"])
    p.op("v", lambda v: v.tensor_tensor(out=T["ellprev"][:], in0=T["ell"][:],
         in1=T["loga"][:], op=ALU.subtract), reads=["ell", "loga"], writes=["ellprev"])
    p.op("v", lambda v: v.tensor_tensor(out=T["glt"][:], in0=T["ellprev"][:],
         in1=T["logb"][:], op=ALU.add), reads=["ellprev", "logb"], writes=["glt"])
    p.op("v", lambda v: v.tensor_tensor(out=T["d1"][:], in0=T["ellC"][:],
         in1=T["ell"][:], op=ALU.subtract), reads=["ellC", "ell"], writes=["d1"])
    for src, dst in [("ell", "e_l"), ("ellprev", "e_lp"), ("d1", "e_ct"),
                     ("ellC", "e_lC")]:
        p.op("s", lambda s, a=src, b=dst: s.activation(out=T[b][:], in_=T[a][:],
             func=ACT.Exp), reads=[src], writes=[dst])
    def selmm(t):
        for e in range(2):
            for sq in range(2):
                j = e * 2 + sq
                inst = t.matmul(T["psA"][:, 944 + sq * 8:944 + sq * 8 + 8],
                                T["c_sel4"][:, j * 128:(j + 1) * 128],
                                T["ell"][:, e:16:2],
                                start=(e == 0 and sq == 0), stop=(e == 1 and sq == 1))
        return inst
    p.op("t", selmm, reads=["c_sel4", "ell"], writes=["psA"], tag=f"selmm{step}")
    p.op("s", lambda s: s.activation(out=T["e_lC_st"][:], in_=T["psA"][:, 944:960],
         func=ACT.Exp), reads=["psA"], writes=["e_lC_st"], tag=f"elcst{step}")
    if dbg:
        kb.dump("ell", T["ell"][:], ["ell"])
        kb.dump("glt", T["glt"][:], ["glt"])
        kb.dump("zat", T["zat"][:], ["zat"])
        kb.dump("alpha", T["alpha"][:], ["alpha"])
        kb.dump("loga", T["loga"][:], ["loga"])
        kb.dump("xlntm", T["xln_tm"][:], ["xln_tm"])
        kb.dump("mn", T["mn"][:], ["mn"])
        kb.dump("sum1", T["sum1"][:], ["sum1"])
        kb.dump("rstd", T["rstd"][:], ["rstd"])
        kb.dump("ssqd", T["ssq"][:], ["ssq"])
        kb.dump("xcc", T["xc"][:], ["xc"])
        kb.dump("xlnfm", T["xln_fm"][:], ["xln_fm"])


def emit_attn_step2(kb, step):
    p, T = kb.p, kb.T
    dbg = (step == 0)
    for src, ssq in [("q_tm", "ssq_q"), ("k_tm", "ssq_k")]:
        p.op("s", lambda s, a=src: s.activation(out=T["scrD"][:], in_=T[a][:],
             func=ACT.Square), reads=[src], writes=["scrD"], tag=f"l2sq{step}")
        p.op("v", lambda v, o=ssq: v.tensor_reduce(
             out=T[o][:], in_=T["scrD"][:].rearrange("p (h e) -> p h e", h=NH),
             axis=AX.X, op=ALU.add), reads=["scrD"], writes=[ssq],
             tag=f"l2r{step}", deferred=True)
    for ssq, n, rn in [("ssq_q", "nq", "rn_q"), ("ssq_k", "nk", "rn_k")]:
        p.op("s", lambda s, a=ssq, b=n: s.activation(out=T[b][:], in_=T[a][:],
             func=ACT.Sqrt), reads=[ssq], writes=[n])
        p.op("v", lambda v, b=n: v.tensor_scalar_max(out=T[b][:], in0=T[b][:],
             scalar1=1e-12), reads=[n], writes=[n], deferred=True)
        p.op("v", lambda v, b=n, c=rn: v.reciprocal(out=T[c][:], in_=T[b][:]),
             reads=[n], writes=[rn], deferred=True)
    p.op("s", lambda s: s.activation(out=T["lnrnq"][:], in_=T["nq"][:],
         func=ACT.Ln), reads=["nq"], writes=["lnrnq"])
    p.op("s", lambda s: s.activation(out=T["lnrnk"][:], in_=T["nk"][:],
         func=ACT.Ln), reads=["nk"], writes=["lnrnk"])
    p.op("v", lambda v: v.tensor_scalar(out=T["lnrnq"][:], in0=T["lnrnq"][:],
         scalar1=-1.0, scalar2=None, op0=ALU.mult), reads=["lnrnq"], writes=["lnrnq"])
    p.op("v", lambda v: v.tensor_scalar(out=T["lnrnk"][:], in0=T["lnrnk"][:],
         scalar1=-1.0, scalar2=None, op0=ALU.mult), reads=["lnrnk"], writes=["lnrnk"])
    p.op("v", lambda v: v.tensor_tensor(out=T["biasWN"][:], in0=T["lnrnk"][:],
         in1=T["ell"][:], op=ALU.subtract), reads=["lnrnk", "ell"], writes=["biasWN"])
    p.op("v", lambda v: v.tensor_tensor(out=T["nlt"][:], in0=T["glt"][:],
         in1=T["lnrnk"][:], op=ALU.add), reads=["glt", "lnrnk"], writes=["nlt"])
    p.op("v", lambda v: v.tensor_tensor(out=T["wlt"][:], in0=T["ell"][:],
         in1=T["lnrnq"][:], op=ALU.add), reads=["ell", "lnrnq"], writes=["wlt"])
    for srcn, hi, lo in [("wlt", "whi", "wlo"), ("nlt", "ghi", "glo"),
                         ("biasWN", "bhi", "blo")]:
        p.op("v", lambda v, a=srcn: v.tensor_copy(out=T["hilo16"][:], in_=T[a][:]),
             reads=[srcn], writes=["hilo16"], tag="hs")
        p.op("v", lambda v, h=hi: v.tensor_copy(out=T[h][:], in_=T["hilo16"][:]),
             reads=["hilo16"], writes=[hi], tag="hs2")
        p.op("v", lambda v, a=srcn, h=hi, l=lo: v.tensor_tensor(out=T[l][:],
             in0=T[a][:], in1=T[h][:], op=ALU.subtract), reads=[srcn, hi],
             writes=[lo], tag="hs3")
    p.op("v", lambda v: v.tensor_tensor(out=T["sq2"][:], in0=T["rn_q"][:],
         in1=T["e_l"][:], op=ALU.mult), reads=["rn_q", "e_l"], writes=["sq2"])
    p.op("v", lambda v: v.tensor_scalar(out=T["sq2n"][:], in0=T["sq2"][:],
         scalar1=-1.0, scalar2=None, op0=ALU.mult), reads=["sq2"], writes=["sq2n"])
    p.op("v", lambda v: v.memset(T["sc3"][:, 16:64], 0.0), writes=["sc3"])
    p.op("v", lambda v: v.tensor_copy(out=T["sc3"][:, 0:16], in_=T["sq2n"][:]),
         reads=["sq2n"], writes=["sc3"])
    p.op("t", lambda t: t.transpose(T["psA"][0:64, 896:1024], T["sc3"][:],
         T["c_id128f"][:]), reads=["sc3", "c_id128f"], writes=["psA"],
         tag=f"trsc{step}")
    p.op("v", lambda v: v.tensor_copy(out=T["scT3"][:], in_=T["psA"][0:16, 896:1024]),
         reads=["psA"], writes=["scT3"], tag=f"trsce{step}")
    def indmm(t):
        for ci in range(8):
            inst = t.matmul(T["psA"][:, ci * 128:(ci + 1) * 128],
                            T["c_indh1"][:, ci * 128:(ci + 1) * 128],
                            T["scT3"][:], start=True, stop=True)
        return inst
    p.op("t", indmm, reads=["c_indh1", "scT3"], writes=["psA"], tag=f"indmm{step}")
    p.op("v", lambda v: v.tensor_tensor(out=T["qh_fm"][:], in0=T["q_fm"][:],
         in1=T["psA"][:], op=ALU.mult), reads=["q_fm", "psA"], writes=["qh_fm"],
         tag=f"scap{step}")
    p.op("v", lambda v: v.tensor_tensor(out=T["skh"][:], in0=T["beta"][:],
         in1=T["e_lp"][:], op=ALU.mult), reads=["beta", "e_lp"], writes=["skh"])
    p.op("v", lambda v: v.tensor_tensor(out=T["skh2"][:], in0=T["skh"][:],
         in1=T["rn_k"][:], op=ALU.mult), reads=["skh", "rn_k"], writes=["skh2"])
    p.op("v", lambda v: v.tensor_tensor(out=T["sct"][:], in0=T["e_ct"][:],
         in1=T["rn_k"][:], op=ALU.mult), reads=["e_ct", "rn_k"], writes=["sct"])
    R3 = lambda: T["R"][:].rearrange("p (h c) -> p h c", h=NH)
    v3 = lambda nm: T[nm][:].rearrange("p (h e) -> p h e", h=NH)
    bc = lambda nm: T[nm][:].unsqueeze(2).broadcast_to([128, NH, HD])
    p.op("v", lambda v: v.tensor_tensor(out=R3()[:, :, 0:64], in0=v3("v_tm"),
         in1=bc("beta"), op=ALU.mult), reads=["v_tm", "beta"], writes=["R"],
         tag=f"Rv{step}")
    p.op("v", lambda v: v.tensor_tensor(out=R3()[:, :, 64:128], in0=v3("k_tm"),
         in1=bc("skh2"), op=ALU.mult), reads=["k_tm", "skh2"], writes=["R"],
         tag=f"Rk{step}")
    p.op("v", lambda v: v.tensor_tensor(out=v3("kt_tm"), in0=v3("k_tm"),
         in1=bc("sct"), op=ALU.mult), reads=["k_tm", "sct"], writes=["kt_tm"],
         tag=f"kt{step}")
    if dbg:
        kb.dump("qh_fm", T["qh_fm"][:], ["qh_fm"])
        kb.dump("scT3", T["scT3"][:], ["scT3"])
        kb.dump("sq2n", T["sq2n"][:], ["sq2n"])
        kb.dump("qfm", T["q_fm"][:], ["q_fm"])
        kb.dump("R", T["R"][:], ["R"])
        kb.dump("kt", T["kt_tm"][:], ["kt_tm"])


def _halves(fn):
    return [(0, 1024), (1024, 2048)]


def emit_attn_step3(kb, step):
    p, T = kb.p, kb.T
    dbg = (step == 0)
    dv3 = lambda nm, nh: T[nm][:].rearrange("p (h c) -> p h c", h=nh)
    bc16 = lambda nm, w: T[nm][:].unsqueeze(2).broadcast_to([128, NH, w])

    if kb.stage == "SZ":
        return
    for hl in ["whi", "wlo"]:
        p.op("v", lambda v, a=hl: v.tensor_tensor(out=dv3("dW", NH),
             in0=dv3("c_dindW", NH), in1=bc16(a, 64), op=ALU.mult),
             reads=["c_dindW", hl], writes=["dW"], tag=f"dgW{hl}{step}")
        def mmW(t, first=(hl == "whi")):
            for q in range(2):
                inst = t.matmul(T["psB"][:, q * 512:(q + 1) * 512],
                                T["c_ones2"][:], T["dW"][:, q * 512:(q + 1) * 512],
                                start=first, stop=False)
            return inst
        p.op("t", mmW, reads=["c_ones2", "dW"], writes=["psB"], tag=f"mmW{hl}{step}")
        if kb.stage == "SA" and hl == "whi":
            return
    def mmWneg(t):
        for q in range(2):
            inst = t.matmul(T["psB"][:, q * 512:(q + 1) * 512], T["c_id128b"][:],
                            T["c_negW"][:, q * 512:(q + 1) * 512],
                            start=False, stop=True)
        return inst
    p.op("t", mmWneg, reads=["c_id128b", "c_negW"], writes=["psB"],
         tag=f"mmWneg{step}")
    if kb.stage == "SB":
        return
    def expW(s):
        for h in range(NH):
            inst = s.activation(out=T["dW"][:, h * 64:(h + 1) * 64],
                                in_=T["psB"][:, h * 64:(h + 1) * 64], func=ACT.Exp,
                                scale=1.0, bias=T["biasWN"][:, h:h + 1])
        return inst
    p.op("s", expW, reads=["psB"], writes=["dW"], scalar_reads=["biasWN"],
         tag=f"expW{step}")
    if kb.stage == "S0":
        return
    def gramKQ(t):
        inst = None
        for parity in range(2):
            if parity == 1:
                t.drain()
            rlo = 64 * parity
            for h in range(parity, NH, 2):
                for par in range(2):
                    clo = (h // 2) * 128 + 64 * par
                    inst = t.matmul(
                        T["psB"][64 * par:64 * par + 64, h * 64:(h + 1) * 64],
                        T["k_fm"][rlo:rlo + 64, clo:clo + 64],
                        T["q_fm"][rlo:rlo + 64, clo:clo + 64],
                        start=True, stop=True, skip_group_check=True)
        return inst
    p.op("t", gramKQ, reads=["k_fm", "q_fm"], writes=["psB"], tag=f"gKQ{step}")
    p.op("v", lambda v: v.tensor_tensor(out=T["wbar"][:], in0=T["dW"][:],
         in1=T["psB"][:], op=ALU.mult), reads=["dW", "psB"], writes=["wbar"],
         tag=f"wbar{step}")
    if kb.stage == "Sg":
        return

    for hl in ["ghi", "glo"]:
        p.op("v", lambda v, a=hl: v.tensor_tensor(out=dv3("dN2", NH),
             in0=dv3("c_dindN", NH), in1=bc16(a, 128), op=ALU.mult),
             reads=["c_dindN", hl], writes=["dN2"], tag=f"dgN{hl}{step}")
        def mmN(t, first=(hl == "ghi")):
            for q in range(4):
                inst = t.matmul(T["psC"][:, q * 512:(q + 1) * 512], T["c_ones2"][:],
                                T["dN2"][:, q * 512:(q + 1) * 512],
                                start=first, stop=False)
            return inst
        p.op("t", mmN, reads=["c_ones2", "dN2"], writes=["psC"],
             tag=f"mmN{hl}{step}")
    def mmNneg(t):
        for q in range(4):
            inst = t.matmul(T["psC"][:, q * 512:(q + 1) * 512], T["c_id128b"][:],
                            T["c_negN"][:, q * 512:(q + 1) * 512],
                            start=False, stop=True)
        return inst
    p.op("t", mmNneg, reads=["c_id128b", "c_negN"], writes=["psC"],
         tag=f"mmNneg{step}")
    def expN(s):
        for h in range(NH):
            inst = s.activation(out=T["dN"][:, h * 128:(h + 1) * 128],
                                in_=T["psC"][:, h * 128:(h + 1) * 128],
                                func=ACT.Exp, scale=1.0,
                                bias=T["biasWN"][:, h:h + 1])
        return inst
    p.op("s", expN, reads=["psC"], writes=["dN"], scalar_reads=["biasWN"],
         tag=f"expN{step}")

    for hl in ["bhi", "blo"]:
        p.op("v", lambda v, a=hl: v.tensor_tensor(out=dv3("dN2", NH),
             in0=dv3("c_dindN", NH), in1=bc16(a, 128), op=ALU.mult),
             reads=["c_dindN", hl], writes=["dN2"], tag=f"dgT{hl}{step}")
        def mmT(t, first=(hl == "bhi")):
            for q in range(4):
                inst = t.matmul(T["psC"][:, q * 512:(q + 1) * 512], T["c_ones2"][:],
                                T["dN2"][:, q * 512:(q + 1) * 512],
                                start=first, stop=False)
            return inst
        p.op("t", mmT, reads=["c_ones2", "dN2"], writes=["psC"],
             tag=f"mmT{hl}{step}")
    def mmTneg(t):
        for q in range(4):
            inst = t.matmul(T["psC"][:, q * 512:(q + 1) * 512], T["c_id128b"][:],
                            T["c_negNT"][:, q * 512:(q + 1) * 512],
                            start=False, stop=True)
        return inst
    p.op("t", mmTneg, reads=["c_id128b", "c_negNT"], writes=["psC"],
         tag=f"mmTneg{step}")
    def expT(s):
        for h in range(NH):
            inst = s.activation(out=T["dN2"][:, h * 128:(h + 1) * 128],
                                in_=T["psC"][:, h * 128:(h + 1) * 128],
                                func=ACT.Exp, scale=1.0, bias=T["nlt"][:, h:h + 1])
        return inst
    p.op("s", expT, reads=["psC"], writes=["dN2"], scalar_reads=["nlt"],
         tag=f"expT{step}")

    def gramKK(t):
        inst = None
        for parity in range(2):
            if parity == 1:
                t.drain()
            rlo = 64 * parity
            for h in range(parity, NH, 2):
                for par in range(2):
                    clo = (h // 2) * 128 + 64 * par
                    inst = t.matmul(
                        T["psC"][64 * par:64 * par + 64,
                                 h * 128 + 64 * par:h * 128 + 64 * par + 64],
                        T["k_fm"][rlo:rlo + 64, clo:clo + 64],
                        T["k_fm"][rlo:rlo + 64, clo:clo + 64],
                        start=True, stop=True, skip_group_check=True)
        return inst
    p.op("t", gramKK, reads=["k_fm"], writes=["psC"], tag=f"gKK{step}")
    p.op("v", lambda v: v.scalar_tensor_tensor(out=T["nbar"][:], in0=T["dN"][:],
         scalar=-1.0, in1=T["psC"][:], op0=ALU.mult, op1=ALU.mult),
         reads=["dN", "psC"], writes=["nbar"], tag=f"nbar{step}")
    p.op("v", lambda v: v.scalar_tensor_tensor(out=T["pbar"][:], in0=T["dN2"][:],
         scalar=-1.0, in1=T["psC"][:], op0=ALU.mult, op1=ALU.mult),
         reads=["dN2", "psC"], writes=["pbar"], tag=f"pbar{step}")
    if dbg:
        kb.dump("nbar", T["nbar"][:], ["nbar"])
        kb.dump("wbar", T["wbar"][:], ["wbar"])
    if kb.stage in ("S1",):
        return

    Bname, Pname = "nbar", "pbar"
    Balt, Palt = "dN", "dN2"
    Xsrc = "R"
    for lvl in range(6):
        def apply(t, B=Bname, X=Xsrc):
            for h in range(NH):
                sl = slice(h * 128, (h + 1) * 128)
                t.matmul(T["psC"][:, sl], T[B][:, sl], T[X][:, sl],
                         start=(h % 4 == 0), stop=False)
                inst = t.matmul(T["psC"][:, sl], T["c_id128b"][:], T[X][:, sl],
                                start=False, stop=(h % 4 == 3))
            return inst
        p.op("t", lambda t, f=apply: f(t), reads=[Bname, Xsrc, "c_id128b"],
             writes=["psC"], tag=f"ap{lvl}_{step}")
        p.op("v", lambda v: v.tensor_copy(out=T["Xc"][:, 0:1024],
             in_=T["psC"][:, 0:1024]), reads=["psC"], writes=["Xc"],
             tag=f"apev{lvl}_{step}")
        p.op("s", lambda s: s.copy(out=T["Xc"][:, 1024:2048],
             in_=T["psC"][:, 1024:2048]), reads=["psC"], writes=["Xc"],
             tag=f"apes{lvl}_{step}")
        Xsrc = "Xc"
        if lvl < 5:
            def sqB(t, B=Bname, P=Pname):
                for h in range(NH):
                    sl = slice(h * 128, (h + 1) * 128)
                    inst = t.matmul(T["psC"][:, sl], T[P][:, sl], T[B][:, sl],
                                    start=(h % 4 == 0), stop=(h % 4 == 3))
                return inst
            p.op("t", lambda t, f=sqB: f(t), reads=[Bname, Pname],
                 writes=["psC"], tag=f"sqB{lvl}_{step}")
            p.op("v", lambda v, d=Balt: v.tensor_copy(out=T[d][:], in_=T["psC"][:]),
                 reads=["psC"], writes=[Balt], tag=f"sqBe{lvl}_{step}")
            if lvl < 4:
                def sqP(t, B=Bname, P=Pname):
                    for h in range(NH):
                        sl = slice(h * 128, (h + 1) * 128)
                        inst = t.matmul(T["psC"][:, sl], T[B][:, sl], T[P][:, sl],
                                        start=(h % 4 == 0), stop=(h % 4 == 3))
                    return inst
                p.op("t", lambda t, f=sqP: f(t), reads=[Bname, Pname],
                     writes=["psC"], tag=f"sqP{lvl}_{step}")
                p.op("s", lambda s, d=Palt: s.copy(out=T[d][:], in_=T["psC"][:]),
                     reads=["psC"], writes=[Palt], tag=f"sqPe{lvl}_{step}")
                Pname, Palt = Palt, Pname
            Bname, Balt = Balt, Bname
    if dbg:
        kb.dump("Xs", T["Xc"][:], ["Xc"])


def emit_attn_step4(kb, step, x2_d, x2ln_d):
    p, T = kb.p, kb.T
    dbg = (step == 0)
    r0, r1 = step * C, (step + 1) * C

    def xkt(t):
        pb = T["psA"][:].bitcast(BF16)
        for h in range(NH):
            rlo = 64 * (h % 2)
            inst = t.transpose(pb[rlo:rlo + 64, (h // 2) * 128:(h // 2) * 128 + 128],
                               T["Xc"][:, h * 128 + 64:(h + 1) * 128],
                               T["c_id128b"][:], tile_position=(0, rlo))
        return inst
    p.op("t", xkt, reads=["Xc", "c_id128b"], writes=["psA"], tag=f"xkt{step}")
    p.op("v", lambda v: v.tensor_copy(out=T["xkT"][:],
         in_=T["psA"][:].bitcast(BF16)[:, 0:1024]),
         reads=["psA"], writes=["xkT"], tag=f"xkte{step}")

    def gmm(t):
        inst = None
        for h in range(NH):
            cl = slice(h * 64, (h + 1) * 64)
            for par in range(2):
                rlo = 64 * (h % 2)
                t.matmul(
                    T["psB"][64 * par:64 * par + 64, cl],
                    T["xkT"][rlo:rlo + 64, (h // 2) * 128 + 64 * par:
                             (h // 2) * 128 + 64 * par + 64],
                    T["negS_b"][rlo:rlo + 64,
                                (par * 8 + h // 2) * 64:(par * 8 + h // 2) * 64 + 64],
                    start=True, stop=False, skip_group_check=True)
            inst = t.matmul(T["psB"][:, cl], T["c_id128b"][:],
                            T["Xc"][:, h * 128:h * 128 + 64],
                            start=False, stop=True, skip_group_check=True)
        return inst
    p.op("t", gmm, reads=["Xc", "xkT", "negS_b", "c_id128b"], writes=["psB"],
         tag=f"gmm{step}")
    p.op("v", lambda v: v.tensor_copy(out=T["gt"][:], in_=T["psB"][:]),
         reads=["psB"], writes=["gt"], tag=f"gte{step}")
    if dbg:
        kb.dump("gt", T["gt"][:], ["gt"])

    def omm1(t):
        inst = None
        for parity in range(2):
            if parity == 1:
                t.drain()
            rlo = 64 * parity
            for h in range(parity, NH, 2):
                cl = slice(h * 64, (h + 1) * 64)
                for par in range(2):
                    inst = t.matmul(
                        T["psC"][64 * par:64 * par + 64, cl],
                        T["qh_fm"][rlo:rlo + 64, (h // 2) * 128 + 64 * par:
                                   (h // 2) * 128 + 64 * par + 64],
                        T["negS_b"][rlo:rlo + 64, (par * 8 + h // 2) * 64:
                                    (par * 8 + h // 2) * 64 + 64],
                        start=True, stop=True, skip_group_check=True)
        return inst

    def omm2(t):
        inst = None
        for par in (1, 0):
            if par == 0:
                t.drain()
            for h in range(NH):
                cl = slice(h * 64, (h + 1) * 64)
                inst = t.matmul(
                    T["psB"][64 * par:64 * par + 64, cl],
                    T["wbar"][64 * par:64 * par + 64, cl],
                    T["gt"][64 * par:64 * par + 64, cl],
                    start=True, stop=True, skip_group_check=True)
        return inst
    p.op("t", omm1, reads=["qh_fm", "negS_b"], writes=["psC"],
         tag=f"omm1{step}")
    p.op("t", omm2, reads=["wbar", "gt"], writes=["psB"],
         tag=f"omm2{step}")
    p.op("s", lambda s: s.copy(out=T["y_tm"][:], in_=T["psC"][:, 0:1024]),
         reads=["psC"], writes=["y_tm"], tag=f"ye1{step}")
    p.op("v", lambda v: v.tensor_tensor(out=T["y_tm"][:], in0=T["y_tm"][:],
         in1=T["psB"][:, 0:1024], op=ALU.add), reads=["y_tm", "psB"],
         writes=["y_tm"], tag=f"ye2{step}")
    if dbg:
        kb.dump("y0", T["y_tm"][:], ["y_tm"])
    if step == 1 and "y1" in kb.dbg_outs:
        kb.dump("y1", T["y_tm"][:], ["y_tm"])

    def smm(t):
        inst = None
        for par in range(2):
            if par == 1:
                t.drain()
            for h in range(NH):
                rlo = 64 * (h % 2)
                inst = t.matmul(
                    T["psA"][rlo:rlo + 64,
                             (par * 8 + h // 2) * 64:(par * 8 + h // 2) * 64 + 64],
                    T["kt_tm"][64 * par:64 * par + 64, h * 64:(h + 1) * 64],
                    T["gt"][64 * par:64 * par + 64, h * 64:(h + 1) * 64],
                    start=True, stop=True, skip_group_check=True)
        return inst
    p.op("t", smm, reads=["kt_tm", "gt"], writes=["psA"], tag=f"smm{step}")
    st3 = lambda: T["negS"][:].rearrange("p (c e) -> p c e", c=16)
    ebc = lambda: T["e_lC_st"][:].unsqueeze(2).broadcast_to([128, 16, HD])
    p.op("v", lambda v: v.tensor_tensor(out=T["scrD"][:].rearrange(
         "p (c e) -> p c e", c=16), in0=st3(), in1=ebc(), op=ALU.mult),
         reads=["negS", "e_lC_st"], writes=["scrD"], tag=f"sdec{step}")
    p.op("v", lambda v: v.tensor_tensor(out=T["negS"][:], in0=T["scrD"][:],
         in1=T["psA"][:], op=ALU.subtract), reads=["scrD", "psA"],
         writes=["negS"], tag=f"supd{step}")
    p.op("s", lambda s: s.copy(out=T["negS_b"][:], in_=T["negS"][:]),
         reads=["negS"], writes=["negS_b"], tag=f"supb{step}")

    p.op("s", lambda s: s.activation(out=T["scrD2"][:], in_=T["y_tm"][:],
         func=ACT.Square), reads=["y_tm"], writes=["scrD2"], tag=f"ysq{step}")
    p.op("v", lambda v: v.tensor_reduce(out=T["ssq"][:], in_=T["scrD2"][:],
         axis=AX.X, op=ALU.add), reads=["scrD2"], writes=["ssq"],
         tag=f"yssq{step}", deferred=True)
    p.op("s", lambda s: s.activation(out=T["sd"][:], in_=T["ssq"][:],
         func=ACT.Sqrt, bias=T["eps1"][:, 0:1], scale=1.0 / D),
         reads=["ssq", "eps1"], writes=["sd"], tag=f"ysd{step}")
    p.op("v", lambda v: v.reciprocal(out=T["rstd"][:], in_=T["sd"][:]),
         reads=["sd"], writes=["rstd"], tag=f"yr{step}", deferred=True)
    p.op("s", lambda s: s.activation(out=T["scrD2"][:], in_=T["y_tm"][:],
         func=ACT.Identity, scale=T["rstd"][:, 0:1]), reads=["y_tm"],
         writes=["scrD2"], scalar_reads=["rstd"], tag=f"yn{step}")
    p.op("v", lambda v: v.tensor_tensor(out=T["scrD2"][:], in0=T["scrD2"][:],
         in1=T["gnorm_bc"][:], op=ALU.mult), reads=["scrD2", "gnorm_bc"],
         writes=["scrD2"], tag=f"yg{step}")
    p.op("v", lambda v: v.tensor_tensor(out=T["xln_tm"][:], in0=T["scrD2"][:],
         in1=T["g_tm"][:], op=ALU.mult), reads=["scrD2", "g_tm"],
         writes=["xln_tm"], tag=f"ygg{step}")
    def try8(t):
        pb = T["psA"][:].bitcast(BF16)
        for ci in range(8):
            inst = t.transpose(pb[:, ci * 128:(ci + 1) * 128],
                               T["xln_tm"][:, ci * 128:(ci + 1) * 128],
                               T["c_id128b"][:])
        return inst
    p.op("t", try8, reads=["xln_tm", "c_id128b"], writes=["psA"], tag=f"try{step}")
    p.op("v", lambda v: v.tensor_copy(out=T["yf_fm"][:],
         in_=T["psA"][:].bitcast(BF16)[:, 0:1024]),
         reads=["psA"], writes=["yf_fm"], tag=f"trye{step}")
    def womm(t):
        for half in range(2):
            for i in range(8):
                inst = t.matmul(T["psB"][:, half * 512:(half + 1) * 512],
                                T["yf_fm"][:, i * 128:(i + 1) * 128],
                                T["w_Wo"][:, i * 1024 + half * 512:
                                          i * 1024 + (half + 1) * 512],
                                start=(i == 0), stop=(i == 7))
        return inst
    p.op("t", womm, reads=["yf_fm", "w_Wo"], writes=["psB"], tag=f"womm{step}")

    _ln_ops(kb, "psB", lambda T: T["psB"][:], "scrD",
            lambda T: T["scrD"][:], "m2", "r2", "b2", tag=f"ln2_{step}")
    p.op("v", lambda v: v.tensor_tensor(out=T["xc"][:], in0=T["xc"][:],
         in1=T["scrD"][:], op=ALU.add), reads=["xc", "scrD"], writes=["xc"],
         tag=f"res1{step}")
    for sq in range(NSEQ):
        p.dma("g", lambda g, sq=sq: g.dma_start(
            out=x2_d[sq, r0:r1, :], in_=T["xc"][sq * C:(sq + 1) * C, :]),
            reads=["xc"], tag=f"stx2_{step}")
    if dbg:
        kb.dump("x2_0", T["xc"][:], ["xc"])

    _ln_ops(kb, "xc", lambda T: T["xc"][:], "x2ln_c",
            lambda T: T["x2ln_c"][:], "m3", "r3", "b3", tag=f"ln3_{step}")
    for sq in range(NSEQ):
        p.dma("g", lambda g, sq=sq: g.dma_start(
            out=x2ln_d[sq, r0:r1, :], in_=T["x2ln_c"][sq * C:(sq + 1) * C, :]),
            reads=["x2ln_c"], tag=f"stxln_{step}")
    p.op("v", lambda v: v.tensor_reduce(out=T["mx1"][:], in_=T["x2ln_c"][:],
         axis=AX.X, op=ALU.max, apply_absolute_value=True),
         reads=["x2ln_c"], writes=["mx1"], tag=f"mx{step}", deferred=True)
    p.op("v", lambda v: v.tensor_tensor(out=T["mxacc"][:], in0=T["mxacc"][:],
         in1=T["mx1"][:], op=ALU.max), reads=["mxacc", "mx1"], writes=["mxacc"],
         tag=f"mxa{step}")
    if dbg:
        kb.dump("x2ln_0", T["x2ln_c"][:], ["x2ln_c"])


def lower(kb):
    nc, p = kb.nc, kb.p
    T = kb.T
    es = kb.es
    sems = {}
    dsems = {}
    for e in ("t", "v", "s", "g", "y"):
        sems[e] = es.enter_context(nc.semaphore("sem_" + e))
    for e in ("g", "y", "s", "v"):
        dsems[e] = [es.enter_context(nc.semaphore(f"dsem_{e}{i}")) for i in range(8)]
    block = es.enter_context(nc.Block())
    p.finalize(nc, None, sems, dsems)

    @block.tensor
    def _(t):
        p.emit("t", t, sems, dsems)

    @block.vector
    def _(v):
        p.emit("v", v, sems, dsems)

    @block.scalar
    def _(s):
        p.emit("s", s, sems, dsems)

    @block.gpsimd
    def _(g):
        p.emit("g", g, sems, dsems)

    @block.sync
    def _(y):
        p.emit("y", y, sems, dsems)

    kb.es.close()
    return nc


def psum_init(kb):
    p, T = kb.p, kb.T
    for ps, res in [("psA", "psA"), ("psB", "psB"), ("psC", "psC")]:
        p.op("v", lambda v, a=ps: v.memset(T[a][:], 0.0), writes=[res],
             tag="psz")


def final_barrier(kb):
    p = kb.p
    for e in ("g", "y", "s", "v"):
        n = p.ndma[e]
        if n:
            def mk(eng_obj, e=e, n=n):
                return None
            o = p.op(e, lambda eng: eng.nop(), tag="final_wait")
            last = [op for op in p.ops[e] if op.is_dma]
            if last:
                o.deps.append(last[-1])


def _gmax_collective(kb, srcs, dsts, tagn):
    p, T, nc = kb.p, kb.T, kb.nc
    n = len(srcs)
    cin = kb.dtmp(f"cc_in_{tagn}", [128, n])
    cout = kb.dtmp(f"cc_out_{tagn}", [128, n])
    sd = kb.dtmp(f"cc_s_{tagn}", [n])
    for i, s in enumerate(srcs):
        p.dma("g", lambda g, i=i, s=s: g.dma_start(
            out=cin.ap()[:, i:i + 1], in_=T[s][:],
            allow_slow_non_contiguous=True), reads=[s],
            writes=[f"ccin_{tagn}"], tag=f"cc{tagn}a")
    p.op("g", lambda g: g.collective_compute(
        "AllReduce", ALU.max, replica_groups=[list(range(8))],
        ins=[cin.ap()], outs=[cout.ap()]),
        reads=[f"ccin_{tagn}"], writes=[f"ccout_{tagn}"], tag=f"cc{tagn}b",
        is_cc=True)
    p.dma("g", lambda g: g.dma_start(out=T["ccrow"][0:128, 0:n], in_=cout.ap()),
          reads=[f"ccout_{tagn}"], writes=["ccrow"], tag=f"cc{tagn}c")
    p.op("t", lambda t: t.transpose(T["psA"][0:n, 896:1024],
         T["ccrow"][:, 0:n], T["c_id128f"][:]), reads=["ccrow", "c_id128f"],
         writes=["psA"], tag=f"cc{tagn}t")
    p.op("v", lambda v: v.tensor_reduce(out=T["ccred"][0:n, 0:1],
         in_=T["psA"][0:n, 896:1024], axis=AX.X, op=ALU.max),
         reads=["psA"], writes=["ccred"], tag=f"cc{tagn}r")
    p.dma("g", lambda g: g.dma_start(out=sd.ap(), in_=T["ccred"][0:n, 0:1]),
          reads=["ccred"], writes=[f"ccs_{tagn}"], tag=f"cc{tagn}d")
    for i, d in enumerate(dsts):
        p.dma("y", lambda y, i=i, d=d: y.dma_start(
            out=T[d][:], in_=sd.ap()[i:i + 1].rearrange("(a b) -> a b", a=1)
            .broadcast_to([128, 1])), reads=[f"ccs_{tagn}"], writes=[d],
            tag=f"cc{tagn}e")


def _fix_cc_deps(kb, tagn):
    pass


def emit_ffn_head(kb, x2_d, x2ln_d):
    p, T, nc = kb.p, kb.T, kb.nc
    NB = IFF // 32
    kb.din("w_gate_d", [IFF, D]); kb.din("w_up_d", [IFF, D])
    kb.din("w_down_d", [D, IFF])
    uraw_d = kb.dtmp("uraw_d", [IFF, NTOK]).ap()
    graw_d = kb.dtmp("graw_d", [IFF, NTOK]).ap()
    kb.sb("ccrow", [128, 8], F32)
    kb.sb("ccred", [8, 1], F32)
    kb.sb("sgx", [128, 1], F32)
    kb.sb("rsgx", [128, 1], F32)
    kb.sb("Mxg", [128, 1], F32)
    kb.sb("xt1", [128, D], F32)
    kb.sb("xq_tm", [128, D], BF16)
    kb.sb("mxu", [128, 1], F32); kb.sb("mxg", [128, 1], F32)
    kb.sb("mxp", [128, 1], F32); kb.sb("mxo", [128, 1], F32)
    kb.sb("Mug", [128, 1], F32); kb.sb("Mgg", [128, 1], F32)
    kb.sb("Mpg", [128, 1], F32); kb.sb("Mog", [128, 1], F32)
    kb.sb("cu", [128, 1], F32); kb.sb("cg", [128, 1], F32)
    kb.sb("sgu", [128, 1], F32); kb.sb("sgg", [128, 1], F32)
    kb.sb("c2", [128, 1], F32); kb.sb("co", [128, 1], F32)
    kb.sb("sgo", [128, 1], F32)
    kb.push_scope()
    kb.sb("wu_l", [128, 8 * IFF], BF16)
    kb.sb("wg_l", [128, 8 * IFF], BF16)
    kb.sb("xq_fm", [128, 8 * NTOK], BF16)
    kb.sb("wst", [128, IFF], F32)
    kb.sb("bmax", [128, 96], F32)
    kb.sb("bsc", [128, 96], F32)
    kb.sb("brs", [128, 96], F32)
    kb.sb("dq1", [128, IFF], F32)
    kb.sb("dqo", [128, IFF], BF16)
    kb.sb("u_st", [128, 1024], F32)
    kb.sb("g_st", [128, 1024], F32)

    _gmax_collective(kb, ["mxacc"], ["Mxg"], "x")
    p.op("v", lambda v: v.tensor_scalar(out=T["sgx"][:], in0=T["Mxg"][:],
         scalar1=1.0 / QMAX, scalar2=1e-12, op0=ALU.mult, op1=ALU.max),
         reads=["Mxg"], writes=["sgx"], tag="sgx")
    p.op("v", lambda v: v.reciprocal(out=T["rsgx"][:], in_=T["sgx"][:]),
         reads=["sgx"], writes=["rsgx"], tag="rsgx")

    for wsrc, wdst in [("w_up_d", "wu_l"), ("w_gate_d", "wg_l")]:
        for ot in range(24):
            wd = kb.dram[wsrc].ap()
            p.dma("y", lambda y, ot=ot, wd=wd: y.dma_start(
                out=T["wst"][:, 0:1024], in_=wd[ot * 128:(ot + 1) * 128, :]),
                writes=["wst"], tag="wld")
            emit_dequant_tile(kb, 1024, wsrc + str(ot))
            def wtr(t, ot=ot):
                pb = T["psA"][:].bitcast(BF16)
                for ci in range(8):
                    inst = t.transpose(pb[:, ci * 128:(ci + 1) * 128],
                                       T["dqo"][:, ci * 128:(ci + 1) * 128],
                                       T["c_id128b"][:])
                return inst
            p.op("t", wtr, reads=["dqo", "c_id128b"], writes=["psA"], tag="wtr")
            p.op("v", lambda v, wdst=wdst, ot=ot: v.tensor_copy(
                out=T[wdst][:].rearrange("p (c o) -> p c o", c=8)[:, :, ot * 128:(ot + 1) * 128],
                in_=T["psA"][:].bitcast(BF16)[:, 0:1024].rearrange(
                    "p (c o) -> p c o", c=8)),
                reads=["psA"], writes=[wdst], tag="wtre")

    for ti in range(NTOK // 128):
        sq, r0 = divmod(ti * 128, S)
        p.dma("y", lambda y, sq=sq, r0=r0: y.dma_start(
            out=T["xt1"][:], in_=x2ln_d[sq, r0:r0 + 128, :]), writes=["xt1"],
            tag=f"xqld{ti}")
        p.op("s", lambda s: s.activation(out=T["xt1"][:], in_=T["xt1"][:],
             func=ACT.Identity, scale=T["rsgx"][:, 0:1], bias=T["magicC"][:, 0:1]),
             reads=["xt1"], writes=["xt1"], scalar_reads=["rsgx", "magicC"],
             tag=f"xqr{ti}")
        p.op("v", lambda v: v.tensor_scalar(out=T["xq_tm"][:], in0=T["xt1"][:],
             scalar1=MAGIC, scalar2=None, op0=ALU.subtract),
             reads=["xt1"], writes=["xq_tm"], tag=f"xqs{ti}")
        def xtr(t):
            pb = T["psA"][:].bitcast(BF16)
            for ci in range(8):
                inst = t.transpose(pb[:, ci * 128:(ci + 1) * 128],
                                   T["xq_tm"][:, ci * 128:(ci + 1) * 128],
                                   T["c_id128b"][:])
            return inst
        p.op("t", xtr, reads=["xq_tm", "c_id128b"], writes=["psA"], tag=f"xqt{ti}")
        p.op("v", lambda v, ti=ti: v.tensor_copy(
             out=T["xq_fm"][:].rearrange("p (c t) -> p c t", c=8)[:, :, ti * 128:(ti + 1) * 128],
             in_=T["psA"][:].bitcast(BF16)[:, 0:1024].rearrange("p (c t) -> p c t", c=8)),
             reads=["psA"], writes=["xq_fm"], tag=f"xqe{ti}")
    return uraw_d, graw_d


def emit_dequant_tile(kb, width, tagn, sfx=""):
    p, T = kb.p, kb.T
    nb = width // 32
    W, BM, BS, BR, D1, DQ = ("wst" + sfx, "bmax" + sfx, "bsc" + sfx,
                             "brs" + sfx, "dq1" + sfx, "dqo" + sfx)
    v3 = lambda nm, w=width: T[nm][:, 0:w].rearrange("p (b k) -> p b k", k=32)
    bcb = lambda nm, c=nb: T[nm][:, 0:c].unsqueeze(2).broadcast_to([128, c, 32])
    p.op("v", lambda v: v.tensor_reduce(out=T[BM][:, 0:nb], in_=v3(W),
         axis=AX.X, op=ALU.max, apply_absolute_value=True),
         reads=[W], writes=[BM], tag="dq_bm", deferred=True)
    p.op("v", lambda v: v.tensor_scalar(out=T[BS][:, 0:nb], in0=T[BM][:, 0:nb],
         scalar1=1.0 / 7.0, scalar2=1e-12, op0=ALU.mult, op1=ALU.max),
         reads=[BM], writes=[BS], tag="dq_bs", deferred=True)
    p.op("v", lambda v: v.reciprocal(out=T[BR][:, 0:nb], in_=T[BS][:, 0:nb]),
         reads=[BS], writes=[BR], tag="dq_rs", deferred=True)
    p.op("v", lambda v: v.tensor_tensor(out=v3(D1), in0=v3(W),
         in1=bcb(BR), op=ALU.mult), reads=[W, BR], writes=[D1],
         tag="dq_m1")
    p.op("v", lambda v: v.tensor_scalar(out=T[D1][:, 0:width], in0=T[D1][:, 0:width],
         scalar1=MAGIC, scalar2=MAGIC, op0=ALU.add, op1=ALU.subtract),
         reads=[D1], writes=[D1], tag="dq_rnd")
    p.op("v", lambda v: v.tensor_tensor(out=v3(DQ), in0=v3(D1),
         in1=bcb(BS), op=ALU.mult), reads=[D1, BS], writes=[DQ],
         tag="dq_m2")


def emit_ffn_f1(kb, uraw_d, graw_d):
    p, T = kb.p, kb.T
    NSLAB = NTOK // 512
    p.op("v", lambda v: v.memset(T["mxu"][:], 0.0), writes=["mxu"])
    p.op("v", lambda v: v.memset(T["mxg"][:], 0.0), writes=["mxg"])
    xq3 = lambda: T["xq_fm"][:].rearrange("p (c t) -> p c t", c=8)
    for sl in range(NSLAB):
        for wnm, stg, mx, raw, ps in [("wu_l", "u_st", "mxu", uraw_d, "psA"),
                                      ("wg_l", "g_st", "mxg", graw_d, "psB")]:
            for oc in range(24):
                half = oc % 2
                def mm(t, wnm=wnm, oc=oc, sl=sl, half=half, ps=ps):
                    for i in range(8):
                        inst = t.matmul(
                            T[ps][:, half * 512:(half + 1) * 512],
                            T[wnm][:].rearrange("p (c o) -> p c o", c=8)[:, i, oc * 128:(oc + 1) * 128],
                            xq3()[:, i, sl * 512:(sl + 1) * 512],
                            start=(i == 0), stop=(i == 7))
                    return inst
                p.op("t", mm, reads=[wnm, "xq_fm"], writes=[ps], tag=f"f1mm{sl}_{oc}")
                p.op("s", lambda s, stg=stg, half=half, ps=ps: s.copy(
                     out=T[stg][:, half * 512:(half + 1) * 512],
                     in_=T[ps][:, half * 512:(half + 1) * 512]),
                     reads=[ps], writes=[stg], tag=f"f1ev{sl}_{oc}")
                p.op("v", lambda v, stg=stg, half=half: v.tensor_reduce(
                     out=T["mx1"][:], in_=T[stg][:, half * 512:(half + 1) * 512],
                     axis=AX.X, op=ALU.max, apply_absolute_value=True),
                     reads=[stg], writes=["mx1"], tag=f"f1mx{sl}_{oc}", deferred=True)
                p.op("v", lambda v, mx=mx: v.tensor_tensor(out=T[mx][:],
                     in0=T[mx][:], in1=T["mx1"][:], op=ALU.max),
                     reads=[mx, "mx1"], writes=[mx], tag=f"f1mxa{sl}_{oc}")
                p.dma("g", lambda g, raw=raw, oc=oc, sl=sl, stg=stg, half=half:
                      g.dma_start(out=raw[oc * 128:(oc + 1) * 128,
                                          sl * 512:(sl + 1) * 512],
                                  in_=T[stg][:, half * 512:(half + 1) * 512]),
                      reads=[stg], tag=f"f1st{sl}_{oc}")


def _scal(kb, fn, reads, writes, tag):
    kb.p.op("v", fn, reads=reads, writes=writes, tag=tag, deferred=True)


def emit_ffn_rest(kb, x2_d, x2ln_d, out_d, uraw_d, graw_d):
    p, T, nc = kb.p, kb.T, kb.nc
    _gmax_collective(kb, ["mxu", "mxg"], ["Mug", "Mgg"], "ug")
    for Mg, sg, cc in [("Mug", "sgu", "cu"), ("Mgg", "sgg", "cg")]:
        _scal(kb, lambda v, Mg=Mg, sg=sg: v.tensor_tensor(out=T[sg][:],
              in0=T[Mg][:], in1=T["sgx"][:], op=ALU.mult), [Mg, "sgx"], [sg], "a2a")
        _scal(kb, lambda v, sg=sg: v.tensor_scalar(out=T[sg][:], in0=T[sg][:],
              scalar1=1.0 / QMAX, scalar2=1e-12, op0=ALU.mult, op1=ALU.max),
              [sg], [sg], "a2b")
        _scal(kb, lambda v, sg=sg, cc=cc: v.reciprocal(out=T[cc][:], in_=T[sg][:]),
              [sg], [cc], "a2c")
        _scal(kb, lambda v, cc=cc: v.tensor_tensor(out=T[cc][:], in0=T[cc][:],
              in1=T["sgx"][:], op=ALU.mult), [cc, "sgx"], [cc], "a2d")

    kb.pop_scope()
    kb.barrier()
    kb.push_scope()
    kb.sb("wst2", [128, IFF], F32)
    kb.sb("bmax2", [128, 96], F32); kb.sb("bsc2", [128, 96], F32)
    kb.sb("brs2", [128, 96], F32)
    kb.sb("dq12", [128, IFF], F32); kb.sb("dqo2", [128, IFF], BF16)
    kb.sb("swg", [128, 4096], F32)
    kb.sb("swu", [128, 4096], F32)
    kb.sb("sw1", [128, 4096], F32)
    kb.sb("sw2", [128, 4096], F32)
    kb.sb("oq_sl", [128, 24 * 512], BF16)
    kb.sb("wd_c0", [128, 1024], BF16); kb.sb("wd_c1", [128, 1024], BF16)
    kb.sb("xr", [128, D], F32)
    kb.sb("of", [128, D], F32)
    kb.sb("scrF", [128, D], F32)
    wdT_d = kb.dtmp("wdT_d", [IFF, 1024], BF16).ap()

    wd = kb.dram["w_down_d"].ap()
    for ot in range(8):
        p.dma("y", lambda y, ot=ot: y.dma_start(
            out=T["wst2"][:], in_=wd[ot * 128:(ot + 1) * 128, :]),
            writes=["wst2"], tag="wdld")
        emit_dequant_tile(kb, IFF, "wd" + str(ot), sfx="2")
        for grp in range(3):
            def wtr(t, grp=grp):
                pb = T["psA"][:].bitcast(BF16)
                for ci in range(8):
                    inst = t.transpose(pb[:, ci * 128:(ci + 1) * 128],
                                       T["dqo2"][:, (grp * 8 + ci) * 128:(grp * 8 + ci + 1) * 128],
                                       T["c_id128b"][:])
                return inst
            p.op("t", wtr, reads=["dqo2", "c_id128b"], writes=["psA"], tag="wdtr")
            p.op("v", lambda v: v.tensor_copy(
                 out=T["dq12"][:, 0:512].bitcast(BF16),
                 in_=T["psA"][:].bitcast(BF16)[:, 0:1024]), reads=["psA"],
                 writes=["dq12"], tag="wdtre")
            for ci in range(8):
                p.dma("g", lambda g, ot=ot, grp=grp, ci=ci: g.dma_start(
                    out=wdT_d[(grp * 8 + ci) * 128:(grp * 8 + ci + 1) * 128,
                              ot * 128:(ot + 1) * 128],
                    in_=T["dq12"][:, 0:512].bitcast(BF16)[:, ci * 128:(ci + 1) * 128]),
                    reads=["dq12"], tag="wdst")

    p_d = kb.dtmp("p_d", [IFF, NTOK]).ap()
    p.op("v", lambda v: v.memset(T["mxp"][:], 0.0), writes=["mxp"])
    for ot in range(24):
        p.dma("y", lambda y, ot=ot: y.dma_start(out=T["swg"][:],
              in_=graw_d[ot * 128:(ot + 1) * 128, :]), writes=["swg"], tag="mpld")
        emit_gq_chain(kb, "swg", "sw1", "sw2", p_out=True)
        p.dma("g", lambda g, ot=ot: g.dma_start(
            out=p_d[ot * 128:(ot + 1) * 128, :], in_=T["sw2"][:]),
            reads=["sw2"], writes=[f"p_d{ot}"], tag="pst")
        p.op("v", lambda v: v.tensor_reduce(out=T["mx1"][:], in_=T["sw2"][:],
             axis=AX.X, op=ALU.max, apply_absolute_value=True),
             reads=["sw2"], writes=["mx1"], tag="mpmx", deferred=True)
        p.op("v", lambda v: v.tensor_tensor(out=T["mxp"][:], in0=T["mxp"][:],
             in1=T["mx1"][:], op=ALU.max), reads=["mxp", "mx1"], writes=["mxp"],
             tag="mpmxa")
    _gmax_collective(kb, ["mxp"], ["Mpg"], "p")
    _scal(kb, lambda v: v.tensor_tensor(out=T["c2"][:], in0=T["Mpg"][:],
          in1=T["sgg"][:], op=ALU.mult), ["Mpg", "sgg"], ["c2"], "a3a")
    _scal(kb, lambda v: v.tensor_scalar(out=T["c2"][:], in0=T["c2"][:],
          scalar1=1.0 / (65536.0 * QMAX), scalar2=1e-12, op0=ALU.mult,
          op1=ALU.max), ["c2"], ["c2"], "a3b")
    _scal(kb, lambda v: v.reciprocal(out=T["co"][:], in_=T["c2"][:]),
          ["c2"], ["co"], "a3c")
    _scal(kb, lambda v: v.tensor_tensor(out=T["co"][:], in0=T["co"][:],
          in1=T["sgg"][:], op=ALU.mult), ["co", "sgg"], ["co"], "a3d")
    _scal(kb, lambda v: v.tensor_scalar(out=T["co"][:], in0=T["co"][:],
          scalar1=1.0 / 65536.0, scalar2=None, op0=ALU.mult), ["co"], ["co"],
          "a3e")
    p.op("v", lambda v: v.memset(T["mxo"][:], 0.0), writes=["mxo"])
    for ot in range(24):
        p.dma("y", lambda y, ot=ot: y.dma_start(out=T["sw2"][:],
              in_=p_d[ot * 128:(ot + 1) * 128, :]),
              reads=[f"p_d{ot}"], writes=["sw2"], tag="mold")
        p.dma("y", lambda y, ot=ot: y.dma_start(out=T["swu"][:],
              in_=uraw_d[ot * 128:(ot + 1) * 128, :]), writes=["swu"], tag="mold2")
        emit_o_chain(kb, 4096, from_p=True)
        p.op("v", lambda v: v.tensor_reduce(out=T["mx1"][:], in_=T["sw2"][:],
             axis=AX.X, op=ALU.max, apply_absolute_value=True),
             reads=["sw2"], writes=["mx1"], tag="momx", deferred=True)
        p.op("v", lambda v: v.tensor_tensor(out=T["mxo"][:], in0=T["mxo"][:],
             in1=T["mx1"][:], op=ALU.max), reads=["mxo", "mx1"], writes=["mxo"],
             tag="momxa")
    _gmax_collective(kb, ["mxo"], ["Mog"], "o")
    _scal(kb, lambda v: v.tensor_tensor(out=T["sgo"][:], in0=T["c2"][:],
          in1=T["sgu"][:], op=ALU.mult), ["c2", "sgu"], ["sgo"], "a4a")
    _scal(kb, lambda v: v.tensor_tensor(out=T["Mog"][:], in0=T["Mog"][:],
          in1=T["sgo"][:], op=ALU.mult), ["Mog", "sgo"], ["Mog"], "a4b")
    _scal(kb, lambda v: v.tensor_scalar(out=T["Mog"][:], in0=T["Mog"][:],
          scalar1=1.0 / QMAX, scalar2=1e-12, op0=ALU.mult, op1=ALU.max),
          ["Mog"], ["Mog"], "a4c")
    _scal(kb, lambda v: v.reciprocal(out=T["mx1"][:], in_=T["Mog"][:]),
          ["Mog"], ["mx1"], "a4d")
    _scal(kb, lambda v: v.tensor_tensor(out=T["sgo"][:], in0=T["sgo"][:],
          in1=T["mx1"][:], op=ALU.mult), ["sgo", "mx1"], ["sgo"], "a4e")
    kb.sb("co2", [128, 1], F32)
    _scal(kb, lambda v: v.tensor_tensor(out=T["co2"][:], in0=T["Mog"][:],
          in1=T["Mog"][:], op=ALU.mult), ["Mog"], ["co2"], "a4f")
    _scal(kb, lambda v: v.tensor_scalar(out=T["co2"][:], in0=T["co2"][:],
          scalar1=1.0 / D, scalar2=None, op0=ALU.mult), ["co2"], ["co2"], "a4g")

    emit_ffn_final(kb, x2_d, out_d, uraw_d, graw_d, wdT_d, p_d)
    kb.pop_scope()


def emit_gq_chain(kb, src, t1, t2, p_out, width=4096):
    p, T = kb.p, kb.T
    w = width
    p.op("s", lambda s: s.activation(out=T[t1][:, 0:w], in_=T[src][:, 0:w],
         func=ACT.Identity, scale=T["cg"][:, 0:1], bias=T["magicC"][:, 0:1]),
         reads=[src], writes=[t1], scalar_reads=["cg", "magicC"], tag="gq1")
    p.op("v", lambda v: v.tensor_scalar(out=T[t1][:, 0:w], in0=T[t1][:, 0:w],
         scalar1=MAGIC, scalar2=None, op0=ALU.subtract), reads=[t1],
         writes=[t1], tag="gq2")
    p.op("s", lambda s: s.activation(out=T[t2][:, 0:w], in_=T[t1][:, 0:w],
         func=ACT.Sigmoid, scale=T["sgg"][:, 0:1]), reads=[t1], writes=[t2],
         scalar_reads=["sgg"], tag="gq3")
    p.op("s", lambda s: s.activation(out=T[t2][:, 0:w], in_=T[t2][:, 0:w],
         func=ACT.Identity, scale=65536.0, bias=T["magicC"][:, 0:1]),
         reads=[t2], writes=[t2], scalar_reads=["magicC"], tag="gq4")
    p.op("v", lambda v: v.tensor_scalar(out=T[t2][:, 0:w], in0=T[t2][:, 0:w],
         scalar1=MAGIC, scalar2=65535.0, op0=ALU.subtract, op1=ALU.min),
         reads=[t2], writes=[t2], tag="gq5")
    if p_out:
        p.op("v", lambda v: v.tensor_tensor(out=T[t2][:, 0:w], in0=T[t1][:, 0:w],
             in1=T[t2][:, 0:w], op=ALU.mult), reads=[t1, t2], writes=[t2],
             tag="gq6")


def emit_o_chain(kb, width, from_p=False):
    p, T = kb.p, kb.T
    w = width
    if not from_p:
        emit_gq_chain(kb, "swg", "sw1", "sw2", p_out=True, width=w)
    p.op("s", lambda s: s.activation(out=T["sw2"][:, 0:w], in_=T["sw2"][:, 0:w],
         func=ACT.Identity, scale=T["co"][:, 0:1], bias=T["magicC"][:, 0:1]),
         reads=["sw2"], writes=["sw2"], scalar_reads=["co", "magicC"], tag="oc1")
    p.op("v", lambda v: v.tensor_scalar(out=T["sw2"][:, 0:w], in0=T["sw2"][:, 0:w],
         scalar1=MAGIC, scalar2=None, op0=ALU.subtract), reads=["sw2"],
         writes=["sw2"], tag="oc2")
    p.op("s", lambda s: s.activation(out=T["sw1"][:, 0:w], in_=T["swu"][:, 0:w],
         func=ACT.Identity, scale=T["cu"][:, 0:1], bias=T["magicC"][:, 0:1]),
         reads=["swu"], writes=["sw1"], scalar_reads=["cu", "magicC"], tag="oc3")
    p.op("v", lambda v: v.tensor_scalar(out=T["sw1"][:, 0:w], in0=T["sw1"][:, 0:w],
         scalar1=MAGIC, scalar2=None, op0=ALU.subtract), reads=["sw1"],
         writes=["sw1"], tag="oc4")
    p.op("v", lambda v: v.tensor_tensor(out=T["sw2"][:, 0:w], in0=T["sw2"][:, 0:w],
         in1=T["sw1"][:, 0:w], op=ALU.mult), reads=["sw1", "sw2"], writes=["sw2"],
         tag="oc5")


def emit_ffn_final(kb, x2_d, out_d, uraw_d, graw_d, wdT_d, p_d):
    p, T = kb.p, kb.T
    NSLAB = NTOK // 512
    for sl in range(NSLAB):
        for ic in range(24):
            p.dma("y", lambda y, ic=ic, sl=sl: y.dma_start(
                out=T["sw2"][:, 0:512],
                in_=p_d[ic * 128:(ic + 1) * 128, sl * 512:(sl + 1) * 512]),
                reads=[f"p_d{ic}"], writes=["sw2"], tag=f"f4g{sl}_{ic}")
            p.dma("y", lambda y, ic=ic, sl=sl: y.dma_start(
                out=T["swu"][:, 0:512],
                in_=uraw_d[ic * 128:(ic + 1) * 128, sl * 512:(sl + 1) * 512]),
                writes=["swu"], tag=f"f4u{sl}_{ic}")
            emit_o_chain(kb, 512, from_p=True)
            p.op("s", lambda s: s.activation(out=T["sw2"][:, 0:512],
                 in_=T["sw2"][:, 0:512], func=ACT.Identity,
                 scale=T["sgo"][:, 0:1], bias=T["magicC"][:, 0:1]),
                 reads=["sw2"], writes=["sw2"], scalar_reads=["sgo", "magicC"],
                 tag=f"f4r{sl}_{ic}")
            p.op("v", lambda v, ic=ic: v.tensor_scalar(
                 out=T["oq_sl"][:, ic * 512:(ic + 1) * 512], in0=T["sw2"][:, 0:512],
                 scalar1=MAGIC, scalar2=None, op0=ALU.subtract),
                 reads=["sw2"], writes=["oq_sl"], tag=f"f4q{sl}_{ic}")
        pmap = [("psB", 0), ("psC", 0), ("psC", 1024), ("psA", 0)]
        for ic in range(24):
            buf = "wd_c0" if ic % 2 == 0 else "wd_c1"
            p.dma("g", lambda g, ic=ic, buf=buf: g.dma_start(
                out=T[buf][:], in_=wdT_d[ic * 128:(ic + 1) * 128, :]),
                writes=[buf], tag=f"f4w{sl}_{ic}")
            def dmm(t, ic=ic, buf=buf):
                for tt in range(4):
                    ps, off = pmap[tt]
                    for q in range(2):
                        inst = t.matmul(
                            T[ps][:, off + q * 512:off + (q + 1) * 512],
                            T["oq_sl"][:, ic * 512 + tt * 128:ic * 512 + (tt + 1) * 128],
                            T[buf][:, q * 512:(q + 1) * 512],
                            start=(ic == 0), stop=(ic == 23))
                return inst
            p.op("t", dmm, reads=["oq_sl", buf], writes=["psA", "psB", "psC"],
                 tag=f"f4mm{sl}_{ic}")
        for tt in range(4):
            ps, off = pmap[tt]
            ti = sl * 4 + tt
            sq, r0 = divmod(ti * 128, S)
            p.dma("y", lambda y, sq=sq, r0=r0: y.dma_start(
                out=T["xr"][:], in_=x2_d[sq, r0:r0 + 128, :]), writes=["xr"],
                tag=f"f4x{sl}_{tt}")
            _ln_ops(kb, ps, lambda T, ps=ps, off=off: T[ps][:, off:off + 1024],
                    "of", lambda T: T["of"][:], "m2", "r2", "b2",
                    tag=f"ln4_{sl}_{tt}", scratch="scrF",
                    in_scale="Mog", in_scale_sq="co2")
            p.op("v", lambda v: v.tensor_tensor(out=T["of"][:], in0=T["of"][:],
                 in1=T["xr"][:], op=ALU.add), reads=["of", "xr"], writes=["of"],
                 tag=f"f4res{sl}_{tt}")
            p.dma("g", lambda g, sq=sq, r0=r0: g.dma_start(
                out=out_d[sq, r0:r0 + 128, :], in_=T["of"][:]), reads=["of"],
                tag=f"f4o{sl}_{tt}")


def build_full(kb):
    x2_d, x2ln_d = build_attn(kb)
    out_d = kb.dout("out", [NSEQ, S, D]).ap()
    if kb.stage in ("A", "A1", "B", "B1", "M", "M1", "S0", "S1", "SZ", "SA",
                    "SB", "Sg"):
        final_barrier(kb)
        return
    kb.barrier()
    uraw_d, graw_d = emit_ffn_head(kb, x2_d, x2ln_d)
    emit_ffn_f1(kb, uraw_d, graw_d)
    emit_ffn_rest(kb, x2_d, x2ln_d, out_d, uraw_d, graw_d)
    final_barrier(kb)


_CACHE = {}


def _build():
    if "nc" not in _CACHE:
        nc = bass.Bass("TRN2", target_bir_lowering=False, debug=False)
        kb = KB(nc, stage="full", dbg=[])
        build_full(kb)
        lower(kb)
        _CACHE["nc"] = nc
    return _CACHE["nc"]


def kernel(L0_attn_x, L0_attn_Wq, L0_attn_Wk, L0_attn_Wv, L0_attn_Wa,
           L0_attn_ba, L0_attn_Wb, L0_attn_bb, L0_attn_Wg, L0_attn_Wo,
           L0_attn_g_norm, L0_ffn_w_gate, L0_ffn_w_up, L0_ffn_w_down):
    nc = _build()
    cs = host_constants()
    x = np.ascontiguousarray(np.asarray(L0_attn_x, dtype=np.float32))
    common = {
        "Wq": np.ascontiguousarray(L0_attn_Wq, dtype=np.float32),
        "Wk": np.ascontiguousarray(L0_attn_Wk, dtype=np.float32),
        "Wv": np.ascontiguousarray(L0_attn_Wv, dtype=np.float32),
        "Wg": np.ascontiguousarray(L0_attn_Wg, dtype=np.float32),
        "Wo": np.ascontiguousarray(L0_attn_Wo, dtype=np.float32),
        "Wa": np.ascontiguousarray(L0_attn_Wa, dtype=np.float32),
        "Wb": np.ascontiguousarray(L0_attn_Wb, dtype=np.float32),
        "i_ba_bc": np.ascontiguousarray(
            np.broadcast_to(np.asarray(L0_attn_ba, np.float32), (128, NH))),
        "i_bb_bc": np.ascontiguousarray(
            np.broadcast_to(np.asarray(L0_attn_bb, np.float32), (128, NH))),
        "i_gnorm_bc": np.ascontiguousarray(
            np.broadcast_to(np.asarray(L0_attn_g_norm, np.float32), (128, D))),
        "w_gate_d": np.ascontiguousarray(L0_ffn_w_gate, dtype=np.float32),
        "w_up_d": np.ascontiguousarray(L0_ffn_w_up, dtype=np.float32),
        "w_down_d": np.ascontiguousarray(L0_ffn_w_down, dtype=np.float32),
    }
    for k, v in cs.items():
        if k != "c_indh":
            common["i_" + k] = v
    in_maps = [dict(common, x=np.ascontiguousarray(x[2 * c:2 * c + 2]))
               for c in range(8)]
    res = run_bass_kernel_spmd(nc, in_maps, core_ids=list(range(8)))
    out = np.concatenate(
        [np.asarray(res.results[c]["out"], dtype=np.float32).reshape(NSEQ, S, D)
         for c in range(8)], axis=0)
    return out


# version 04
# speedup vs baseline: 1.0532x; 1.0364x over previous
import numpy as np
from collections import defaultdict
from contextlib import ExitStack

import concourse.bass as bass
import concourse.mybir as mybir
from concourse.bass_utils import run_bass_kernel_spmd

ENGINES = ("t", "v", "s", "g", "y")


class Op:
    __slots__ = ("eng", "fn", "reads", "writes", "scalar_reads", "is_dma",
                 "idx", "dma_idx", "deps", "inc", "tag", "deferred", "is_cc")

    def __init__(self, eng, fn, reads, writes, scalar_reads, is_dma, tag,
                 deferred=False, is_cc=False):
        self.eng = eng
        self.fn = fn
        self.reads = reads
        self.writes = writes
        self.scalar_reads = scalar_reads
        self.is_dma = is_dma
        self.idx = -1
        self.dma_idx = -1
        self.deps = []
        self.inc = False
        self.tag = tag
        self.deferred = deferred
        self.is_cc = is_cc


class Prog:
    def __init__(self):
        self.small_res = set()
        self.ops = {e: [] for e in ENGINES}
        self.all_ops = []
        self.res = defaultdict(lambda: [None, []])
        self.ndma = {e: 0 for e in ENGINES}

    def op(self, eng, fn, reads=(), writes=(), scalar_reads=(), tag="",
           deferred=False, is_cc=False):
        if not deferred and (set(writes) & self.small_res):
            deferred = True
        o = Op(eng, fn, tuple(reads) + tuple(scalar_reads), tuple(writes),
               tuple(scalar_reads), False, tag, deferred, is_cc)
        self._record(o)
        return o

    def dma(self, eng, fn, reads=(), writes=(), tag=""):
        assert eng in ("g", "y", "s", "v")
        o = Op(eng, fn, tuple(reads), tuple(writes), (), True, tag)
        o.dma_idx = self.ndma[eng]
        self.ndma[eng] += 1
        self._record(o)
        return o

    def _record(self, o):
        o.idx = len(self.ops[o.eng])
        self.ops[o.eng].append(o)
        self.all_ops.append(o)
        deps = []
        for r in o.reads:
            w, _readers = self.res[r]
            if w is not None:
                deps.append(w)
            self.res[r][1].append(o)
        for w_ in o.writes:
            lw, readers = self.res[w_]
            if lw is not None:
                deps.append(lw)
            deps.extend(readers)
            self.res[w_][0] = o
            self.res[w_][1] = []
        seen = set()
        for d in deps:
            if d is o:
                continue
            if id(d) in seen:
                continue
            seen.add(id(d))
            if d.eng == o.eng and not d.is_dma and not d.is_cc:
                continue
            o.deps.append(d)

    def finalize(self, nc, engines, sems, dma_sems, verbose=False):
        for o in self.all_ops:
            if o.is_cc:
                o.inc = True
            for d in o.deps:
                if not d.is_dma:
                    d.inc = True
        inc_count = {}
        for e in ENGINES:
            c = 0
            for o in self.ops[e]:
                if o.inc and not o.is_dma:
                    c += 1
                inc_count[id(o)] = c
        plans = {e: [] for e in ENGINES}
        for e in ENGINES:
            last_wait = defaultdict(int)
            pend_w = set()
            pend_r = set()
            for o in self.ops[e]:
                waits = []
                for d in o.deps:
                    if d.is_dma:
                        key = ("d", d.eng, d.dma_idx % 8)
                        val = 16 * (d.dma_idx // 8 + 1)
                    else:
                        key = ("c", d.eng)
                        val = inc_count[id(d)]
                        assert val > 0
                    if val > last_wait[key]:
                        last_wait[key] = val
                        waits.append((key, val))
                rd, wr = set(o.reads), set(o.writes)
                need_gap = bool((rd & pend_w) or (wr & pend_w) or (wr & pend_r))
                if need_gap:
                    pend_w.clear()
                    pend_r.clear()
                pend_w.update(wr)
                pend_r.update(rd)
                plans[e].append((o, waits, need_gap))
        self._plans = plans
        self._inc_count = inc_count
        return plans

    def emit(self, e, eng_obj, sems, dma_sems):
        for o, waits, need_gap in self._plans[e]:
            for key, val in waits:
                if key[0] == "d":
                    eng_obj.wait_ge(dma_sems[key[1]][key[2]], val)
                else:
                    eng_obj.wait_ge(sems[key[1]], val)
            if need_gap:
                eng_obj.drain()
            if o.is_dma:
                slot = o.dma_idx % 8
                rnd = o.dma_idx // 8
                if rnd > 0:
                    eng_obj.wait_ge(dma_sems[e][slot], 16 * rnd)
                inst = o.fn(eng_obj)
                inst.then_inc(dma_sems[e][slot], 16)
            else:
                inst = o.fn(eng_obj)
                if o.inc:
                    if inst is None:
                        eng_obj.nop().then_inc(sems[e], 1)
                    else:
                        inst.then_inc(sems[e], 1)

    def stats(self):
        return {e: (len(self.ops[e]), sum(1 for o in self.ops[e] if o.inc),
                    self.ndma[e]) for e in ENGINES}


F32 = mybir.dt.float32
BF16 = mybir.dt.bfloat16
I16 = mybir.dt.int16
ALU = mybir.AluOpType
ACT = mybir.ActivationFunctionType
AX = mybir.AxisListType

D = 1024
NH = 16
HD = 64
C = 64
S = 2048
NSTEP = S // C
NSEQ = 2
IFF = 3072
NTOK = NSEQ * S
MAGIC = float(1.5 * 2 ** 23)
BIGNEG = -1.0e30
QMAX = 32767.0
EPS = 1e-5

def host_constants():
    cs = {}
    cs["c_id128f"] = np.eye(128, dtype=np.float32)
    cs["c_id128b"] = np.eye(128, dtype=np.float32)
    triu = np.triu(np.ones((C, C), np.float32))
    z = np.zeros((C, C), np.float32)
    cs["c_tril2"] = np.block([[triu, z], [z, triu]])
    sel = np.zeros((128, 128), np.float32)
    sel[C - 1, 0:C] = 1.0
    sel[127, C:128] = 1.0
    cs["c_selC"] = sel
    q = np.arange(128)
    s_in = q % C
    par_row = (q // C)
    t_idx = np.arange(C)
    dind = (s_in[:, None] == t_idx[None, :]).astype(np.float32)
    blkN = np.zeros((128, 2 * C), np.float32)
    blkN[:C, :C] = dind[:C]
    blkN[C:, C:] = dind[C:]
    cs["c_dindN"] = np.tile(blkN, (1, NH))
    cs["c_dindW"] = np.tile(dind, (1, NH))
    okN = np.zeros((128, 2 * C), np.float32) + BIGNEG
    okN[:C, :C] = np.where(s_in[:C, None] < t_idx[None, :], 0.0, BIGNEG)
    okN[C:, C:] = np.where(s_in[C:, None] < t_idx[None, :], 0.0, BIGNEG)
    cs["c_negN"] = np.tile(okN, (1, NH))
    okW = np.where(s_in[:, None] <= t_idx[None, :], 0.0, BIGNEG)
    cs["c_negW"] = np.tile(okW, (1, NH))
    ind = np.zeros((16, 8 * 128), np.float32)
    for ci in range(8):
        ind[2 * ci, ci * 128: ci * 128 + 64] = 1.0
        ind[2 * ci + 1, ci * 128 + 64: ci * 128 + 128] = 1.0
    cs["c_indh"] = ind
    cs["c_ones2"] = np.block([[np.ones((C, C), np.float32), z],
                              [z, np.ones((C, C), np.float32)]])
    okT = np.zeros((128, 2 * C), np.float32) + BIGNEG
    okT[:C, :C] = np.where(t_idx[None, :] < s_in[:C, None], 0.0, BIGNEG)
    okT[C:, C:] = np.where(t_idx[None, :] < s_in[C:, None], 0.0, BIGNEG)
    cs["c_negNT"] = np.tile(okT, (1, NH))
    i1 = np.zeros((16, 1024), np.float32)
    for ci in range(8):
        for pp in range(128):
            i1[2 * ci + pp // 64, ci * 128 + pp] = 1.0
    cs["c_indh1"] = i1
    s4 = np.zeros((128, 512), np.float32)
    for e in range(2):
        for sq in range(2):
            for pp in range(128):
                if pp // 64 == e:
                    s4[63 + 64 * sq, (e * 2 + sq) * 128 + pp] = 1.0
    cs["c_sel4"] = s4
    return cs


class KB:

    def __init__(self, nc, stage="full", dbg=()):
        self.nc = nc
        self.p = Prog()
        self.stage = stage
        self.dbg = set(dbg)
        self.es = ExitStack()
        self.scope = None
        self.T = {}
        self.dram = {}
        self.dbg_outs = {}

    def push_scope(self):
        self.scope = ExitStack()

    def pop_scope(self):
        self.scope.close()
        self.scope = None

    def barrier(self):
        p = self.p
        lasts = {}
        for e in ("t", "v", "s", "g", "y"):
            real = [op for op in p.ops[e] if not op.is_dma]
            lasts[e] = real[-1] if real else None
        dlasts = {}
        for e in ("t", "v", "s", "g", "y"):
            per_slot = {}
            for op in p.ops[e]:
                if op.is_dma:
                    per_slot[op.dma_idx % 8] = op
            dlasts[e] = list(per_slot.values())
        for e in ("t", "v", "s", "g", "y"):
            o = p.op(e, lambda eng: eng.nop(), tag="barr")
            for x in ("t", "v", "s", "g", "y"):
                if x != e and lasts[x] is not None:
                    o.deps.append(lasts[x])
                o.deps.extend(dlasts[x])

    def sb(self, name, shape, dt):
        es = self.scope if self.scope is not None else self.es
        self.T[name] = es.enter_context(self.nc.sbuf_tensor(name, shape, dt))
        import numpy as _np
        if int(_np.prod(shape[1:])) <= 64:
            self.p.small_res.add(name)
        return self.T[name]

    def ps(self, name, shape):
        self.T[name] = self.es.enter_context(self.nc.psum_tensor(name, shape, F32))
        return self.T[name]

    def din(self, name, shape, dt=F32):
        self.dram[name] = self.nc.dram_tensor(name, shape, dt, kind="ExternalInput")
        return self.dram[name]

    def dout(self, name, shape, dt=F32):
        self.dram[name] = self.nc.dram_tensor(name, shape, dt, kind="ExternalOutput")
        return self.dram[name]

    def dtmp(self, name, shape, dt=F32):
        self.dram[name] = self.nc.dram_tensor(name, shape, dt)
        return self.dram[name]

    def dbg_out(self, key, shape, dt=F32):
        if key in self.dbg:
            self.dbg_outs[key] = self.dout("dbg_" + key, shape, dt)
            return self.dram["dbg_" + key].ap()
        return None

    def dump(self, key, tile_ap, reads):
        if key in self.dbg_outs:
            d = self.dram["dbg_" + key].ap()
            self.p.dma("g", lambda g, d=d, t=tile_ap: g.dma_start(out=d, in_=t),
                       reads=reads, tag="dbg_" + key)

    def op(self, *a, **k):
        return self.p.op(*a, **k)

    def dma(self, *a, **k):
        return self.p.dma(*a, **k)


def build_attn(kb, ncores_note=""):
    nc, p, T = kb.nc, kb.p, kb.T
    x_d = kb.din("x", [NSEQ, S, D]).ap()
    Wnames = ["Wq", "Wk", "Wv", "Wg", "Wo"]
    for w in Wnames:
        kb.din(w, [D, D])
    kb.din("Wa", [D, NH]); kb.din("Wb", [D, NH])
    kb.din("i_ba_bc", [128, NH]); kb.din("i_bb_bc", [128, NH])
    kb.din("i_gnorm_bc", [128, D])
    for cname, cshape, cdt in [
        ("c_id128f", [128, 128], F32), ("c_id128b", [128, 128], F32),
        ("c_tril2", [128, 128], F32), ("c_selC", [128, 128], F32),
        ("c_dindN", [128, 2048], F32), ("c_dindW", [128, 1024], F32),
        ("c_negN", [128, 2048], F32), ("c_negW", [128, 1024], F32),
        ("c_indh1", [16, 1024], F32), ("c_ones2", [128, 128], F32),
        ("c_negNT", [128, 2048], F32), ("c_sel4", [128, 512], F32),
    ]:
        kb.din("i_" + cname, cshape, cdt)
    x2_d = kb.dtmp("x2_d", [NSEQ, S, D]).ap()
    x2ln_d = kb.dtmp("x2ln_d", [NSEQ, S, D]).ap()

    for cn in ["id128f"]:
        pass
    kb.sb("c_id128f", [128, 128], F32)
    kb.sb("c_id128b", [128, 128], BF16)
    for nm in ["sum1", "mn", "ssq", "sd", "rstd", "bias2", "m2", "r2", "b2",
               "m3", "r3", "b3", "mx1", "eps1", "magicC"]:
        kb.sb(nm, [128, 1], F32)
    kb.sb("mxacc", [128, 1], F32)
    kb.push_scope()
    for w in Wnames:
        kb.sb("w_" + w, [128, 8 * 1024], BF16)
    kb.sb("w_ab", [128, 8 * 32], BF16)
    kb.sb("ba_bc", [128, NH], F32); kb.sb("bb_bc", [128, NH], F32)
    kb.sb("gnorm_bc", [128, D], F32)
    consts_b = [("dindN", BF16), ("dindW", BF16),
                ("negN", BF16), ("negW", BF16), ("ones2", BF16), ("negNT", BF16)]
    for cn, cdt in consts_b:
        kb.sb("c_" + cn, [128, {"dindN": 2048, "negN": 2048, "negNT": 2048,
                                "dindW": 1024, "negW": 1024}.get(cn, 128)], cdt)
    for cn in ["tril2", "selC"]:
        kb.sb("c_" + cn, [128, 128], F32)
    kb.sb("c_indh1", [16, 1024], F32)
    kb.sb("c_sel4", [128, 512], F32)
    kb.sb("scT3", [16, 128], F32)
    kb.sb("dN2", [128, 2048], BF16)
    kb.sb("x2ln_c", [128, D], F32)
    kb.sb("e_lC_st", [128, 16], F32)

    kb.sb("xc", [128, D], F32)
    kb.sb("xln_tm", [128, D], BF16)
    kb.sb("xln_fm", [128, D], BF16)
    kb.sb("q_fm", [128, D], BF16); kb.sb("k_fm", [128, D], BF16)
    kb.sb("qh_fm", [128, D], BF16)
    kb.sb("q_tm", [128, D], BF16); kb.sb("k_tm", [128, D], BF16)
    kb.sb("v_tm", [128, D], BF16)
    kb.sb("g_tm", [128, D], F32)
    kb.sb("R", [128, 2048], BF16)
    kb.sb("kt_tm", [128, D], BF16)
    kb.sb("scrD", [128, D], F32)
    kb.sb("scrD2", [128, D], F32)
    kb.sb("dN", [128, 2048], BF16)
    kb.sb("nbar", [128, 2048], BF16)
    kb.sb("pbar", [128, 2048], BF16)
    kb.sb("dW", [128, 1024], BF16)
    kb.sb("wbar", [128, 1024], BF16)
    kb.sb("Xc", [128, 2048], BF16)
    kb.sb("xkT", [128, 1024], BF16)
    kb.sb("gt", [128, 1024], BF16)
    kb.sb("y_tm", [128, D], F32)
    kb.sb("yf_fm", [128, D], BF16)
    kb.sb("negS", [128, 1024], F32)
    kb.sb("negS_b", [128, 1024], BF16)
    for nm in ["zat", "zbt", "alpha", "beta", "loga", "logb", "ell", "ellprev",
               "glt", "ellC", "e_l", "e_lp", "e_ct", "e_lC", "d1",
               "lhi", "llo", "nlhi", "nllo", "ghi", "glo", "negell",
               "ssq_q", "ssq_k", "nq", "nk", "rn_q", "rn_k",
               "skh", "skh2", "sct", "sq2", "sq2n", "lnrnq", "lnrnk",
               "biasWN", "nlt", "wlt", "whi", "wlo", "bhi", "blo"]:
        kb.sb(nm, [128, NH], F32)
    kb.sb("hilo16", [128, NH], BF16)
    for nm in ["scTq", "scTqh", "scTk"]:
        kb.sb(nm, [16, 128], F32)
    kb.sb("sc3", [128, 64], F32)
    kb.ps("psA", [128, 1024])
    kb.ps("psB", [128, 1024])
    kb.ps("psC", [128, 2048])
    psum_init(kb)

    for w in Wnames:
        wd = kb.dram[w].ap().rearrange("(i p) o -> p i o", p=128)
        p.dma("g", lambda g, o=T["w_" + w], i=wd: g.dma_start(out=o[:].rearrange(
            "p (i o) -> p i o", i=8), in_=i), writes=["w_" + w], tag="ldw")
    wa = kb.dram["Wa"].ap().rearrange("(i p) o -> p i o", p=128)
    wb = kb.dram["Wb"].ap().rearrange("(i p) o -> p i o", p=128)
    p.dma("g", lambda g, i=wa: g.dma_start(
        out=T["w_ab"][:].rearrange("p (i o) -> p i o", i=8)[:, :, 0:16], in_=i),
        writes=["w_ab"], tag="ldab")
    p.dma("g", lambda g, i=wb: g.dma_start(
        out=T["w_ab"][:].rearrange("p (i o) -> p i o", i=8)[:, :, 16:32], in_=i),
        writes=["w_ab"], tag="ldab")
    for nm in ["ba_bc", "bb_bc", "gnorm_bc"]:
        p.dma("y", lambda y, o=T[nm], i=kb.dram["i_" + nm].ap(): y.dma_start(out=o[:], in_=i),
              writes=[nm], tag="ldc")
    for cn, cdt in consts_b + [("id128b", BF16)]:
        p.dma("g", lambda g, o=T["c_" + cn], i=kb.dram["i_c_" + cn].ap():
              g.dma_start(out=o[:], in_=i), writes=["c_" + cn], tag="ldc")
    for cn in ["id128f", "tril2", "selC", "indh1", "sel4"]:
        p.dma("y", lambda y, o=T["c_" + cn], i=kb.dram["i_c_" + cn].ap():
              y.dma_start(out=o[:], in_=i), writes=["c_" + cn], tag="ldc")
    p.op("v", lambda v: v.memset(T["eps1"][:], EPS), writes=["eps1"])
    p.op("v", lambda v: v.memset(T["magicC"][:], MAGIC), writes=["magicC"])
    p.op("v", lambda v: v.memset(T["negS"][:], 0.0), writes=["negS"])
    p.op("v", lambda v: v.memset(T["negS_b"][:], 0.0), writes=["negS_b"])
    p.op("v", lambda v: v.memset(T["mxacc"][:], 0.0), writes=["mxacc"])

    for key, shp, dt in [("qh_fm", [128, D], F32), ("zat", [128, NH], F32),
                         ("scT3", [16, 128], F32), ("sq2n", [128, NH], F32),
                         ("qfm", [128, D], F32),
                         ("alpha", [128, NH], F32), ("loga", [128, NH], F32),
                         ("xlntm", [128, D], F32), ("xlnfm", [128, D], F32),
                         ("mn", [128, 1], F32), ("sum1", [128, 1], F32),
                         ("rstd", [128, 1], F32),
                         ("ssqd", [128, 1], F32), ("xcc", [128, D], F32),
                         ("R", [128, 2048], F32), ("ell", [128, NH], F32),
                         ("glt", [128, NH], F32), ("kt", [128, D], F32),
                         ("nbar", [128, 2048], F32), ("wbar", [128, 1024], F32),
                         ("Xs", [128, 2048], F32), ("gt", [128, 1024], F32),
                         ("y0", [128, D], F32), ("y1", [128, D], F32),
                         ("x2_0", [128, D], F32), ("x2ln_0", [128, D], F32)]:
        kb.dbg_out(key, shp, dt)

    nsteps = 1 if kb.stage in ("A1", "B1", "M1", "S1", "S0", "Sg", "SZ", "SA", "SB") else NSTEP
    for step in range(nsteps):
        emit_attn_step(kb, step, x_d, x2_d, x2ln_d)
        emit_attn_step2(kb, step)
        if kb.stage in ("A",):
            continue
        emit_attn_step3(kb, step)
        if kb.stage in ("M1", "M"):
            continue
        emit_attn_step4(kb, step, x2_d, x2ln_d)
    kb.pop_scope()
    return x2_d, x2ln_d


def _ln_ops(kb, src_res, src_ap_fn, out_res, out_ap, mn_t, r_t, b_t,
            out_dt_note="", src_is_psum=False, tag="ln", scratch="scrD2",
            in_scale=None, in_scale_sq=None):
    p, T = kb.p, kb.T
    sumr = ["sum1"]
    p.op("v", lambda v: v.tensor_reduce(out=T["sum1"][:], in_=src_ap_fn(T),
         axis=AX.X, op=ALU.add), reads=[src_res], writes=["sum1"],
         tag=tag + "_sum", deferred=True)
    p.op("v", lambda v: v.tensor_scalar(out=T[mn_t][:], in0=T["sum1"][:],
         scalar1=-1.0 / D, scalar2=None, op0=ALU.mult),
         reads=["sum1"], writes=[mn_t], tag=tag + "_mn")
    p.op("s", lambda s: s.activation(out=T[scratch][:], in_=src_ap_fn(T),
         func=ACT.Square, bias=T[mn_t][:, 0:1], scale=1.0, accum_out=T["ssq"][:]),
         reads=[src_res], writes=[scratch, "ssq"], scalar_reads=[mn_t],
         tag=tag + "_sq", deferred=True)
    if in_scale is None:
        p.op("s", lambda s: s.activation(out=T["sd"][:], in_=T["ssq"][:],
             func=ACT.Sqrt, bias=T["eps1"][:, 0:1], scale=1.0 / D),
             reads=["ssq", "eps1"], writes=["sd"], tag=tag + "_sd")
    else:
        p.op("s", lambda s: s.activation(out=T["sd"][:], in_=T["ssq"][:],
             func=ACT.Sqrt, bias=T["eps1"][:, 0:1],
             scale=T[in_scale_sq][:, 0:1]),
             reads=["ssq", "eps1"], writes=["sd"],
             scalar_reads=[in_scale_sq], tag=tag + "_sd")
    p.op("v", lambda v: v.reciprocal(out=T[r_t][:], in_=T["sd"][:]),
         reads=["sd"], writes=[r_t], tag=tag + "_r", deferred=True)
    if in_scale is not None:
        p.op("v", lambda v: v.tensor_tensor(out=T[r_t][:], in0=T[r_t][:],
             in1=T[in_scale][:], op=ALU.mult), reads=[r_t, in_scale],
             writes=[r_t], tag=tag + "_rs", deferred=True)
    p.op("v", lambda v: v.tensor_tensor(out=T[b_t][:], in0=T[mn_t][:],
         in1=T[r_t][:], op=ALU.mult), reads=[mn_t, r_t], writes=[b_t],
         tag=tag + "_b", deferred=True)
    p.op("s", lambda s: s.activation(out=out_ap(T), in_=src_ap_fn(T),
         func=ACT.Identity, scale=T[r_t][:, 0:1], bias=T[b_t][:, 0:1]),
         reads=[src_res], writes=[out_res], scalar_reads=[r_t, b_t],
         tag=tag + "_app")


def emit_attn_step(kb, step, x_d, x2_d, x2ln_d):
    p, T = kb.p, kb.T
    r0, r1 = step * C, (step + 1) * C
    dbg = (step == 0)

    for sq in range(NSEQ):
        p.dma("y", lambda y, sq=sq: y.dma_start(
            out=T["xc"][sq * C:(sq + 1) * C, :], in_=x_d[sq, r0:r1, :]),
            writes=["xc"], tag=f"ldx{step}")

    _ln_ops(kb, "xc", lambda T: T["xc"][:], "xln_tm",
            lambda T: T["xln_tm"][:], "mn", "rstd", "bias2", tag=f"ln1_{step}")

    def tr8(t, src="xln_tm"):
        pb = T["psA"][:].bitcast(BF16)
        for ci in range(8):
            inst = t.transpose(pb[:, ci * 128:(ci + 1) * 128],
                               T[src][:, ci * 128:(ci + 1) * 128], T["c_id128b"][:])
        return inst
    p.op("t", lambda t: tr8(t), reads=["xln_tm", "c_id128b"],
         writes=["psA"], tag=f"trx{step}")
    p.op("v", lambda v: v.tensor_copy(out=T["xln_fm"][:],
         in_=T["psA"][:].bitcast(BF16)[:, 0:1024]),
         reads=["psA"], writes=["xln_fm"], tag=f"trxe{step}")

    def proj_fm(t, wname):
        for j in range(8):
            for i in range(8):
                inst = t.matmul(T["psB"][:, j * 128:(j + 1) * 128],
                                T["w_" + wname][:, i * 1024 + j * 128:i * 1024 + (j + 1) * 128],
                                T["xln_fm"][:, i * 128:(i + 1) * 128],
                                start=(i == 0), stop=(i == 7))
        return inst
    for wname, dst in [("Wq", "q_fm"), ("Wk", "k_fm")]:
        p.op("t", lambda t, w=wname: proj_fm(t, w), reads=["w_" + wname, "xln_fm"],
             writes=["psB"], tag=f"pfm{wname}{step}")
        p.op("s", lambda s, d=dst: s.copy(out=T[d][:], in_=T["psB"][:]),
             reads=["psB"], writes=[dst], tag=f"pfme{wname}{step}")

    def proj_tm(t, wname):
        for half in range(2):
            for i in range(8):
                inst = t.matmul(T["psB"][:, half * 512:(half + 1) * 512],
                                T["xln_fm"][:, i * 128:(i + 1) * 128],
                                T["w_" + wname][:, i * 1024 + half * 512:
                                                i * 1024 + (half + 1) * 512],
                                start=(i == 0), stop=(i == 7))
        return inst
    for wname, dst in [("Wq", "q_tm"), ("Wk", "k_tm"), ("Wv", "v_tm")]:
        p.op("t", lambda t, w=wname: proj_tm(t, w), reads=["w_" + wname, "xln_fm"],
             writes=["psB"], tag=f"ptm{wname}{step}")
        p.op("v", lambda v, d=dst: v.tensor_copy(out=T[d][:], in_=T["psB"][:]),
             reads=["psB"], writes=[dst], tag=f"ptme{wname}{step}")
    p.op("t", lambda t: proj_tm(t, "Wg"), reads=["w_Wg", "xln_fm"],
         writes=["psB"], tag=f"ptmWg{step}")
    p.op("s", lambda s: s.activation(out=T["g_tm"][:], in_=T["psB"][:],
         func=ACT.Silu), reads=["psB"], writes=["g_tm"], tag=f"silu{step}")

    def logits(t):
        for i in range(8):
            inst = t.matmul(T["psA"][:, 896:928], T["xln_fm"][:, i * 128:(i + 1) * 128],
                            T["w_ab"][:, i * 32:(i + 1) * 32],
                            start=(i == 0), stop=(i == 7))
        return inst
    p.op("t", logits, reads=["w_ab", "xln_fm"], writes=["psA"], tag=f"lg{step}")
    p.op("v", lambda v: v.tensor_tensor(out=T["zat"][:], in0=T["psA"][:, 896:912],
         in1=T["ba_bc"][:], op=ALU.add), reads=["psA", "ba_bc"], writes=["zat"])
    p.op("v", lambda v: v.tensor_tensor(out=T["zbt"][:], in0=T["psA"][:, 912:928],
         in1=T["bb_bc"][:], op=ALU.add), reads=["psA", "bb_bc"], writes=["zbt"])
    p.op("s", lambda s: s.activation(out=T["alpha"][:], in_=T["zat"][:],
         func=ACT.Sigmoid), reads=["zat"], writes=["alpha"])
    p.op("s", lambda s: s.activation(out=T["beta"][:], in_=T["zbt"][:],
         func=ACT.Sigmoid), reads=["zbt"], writes=["beta"])
    p.op("s", lambda s: s.activation(out=T["loga"][:], in_=T["alpha"][:],
         func=ACT.Ln), reads=["alpha"], writes=["loga"])
    p.op("s", lambda s: s.activation(out=T["logb"][:], in_=T["beta"][:],
         func=ACT.Ln), reads=["beta"], writes=["logb"])

    p.op("t", lambda t: t.matmul(T["psA"][:, 928:944], T["c_tril2"][:],
         T["loga"][:], start=True, stop=True), reads=["c_tril2", "loga"],
         writes=["psA"], tag=f"cum{step}")
    p.op("v", lambda v: v.tensor_copy(out=T["ell"][:], in_=T["psA"][:, 928:944]),
         reads=["psA"], writes=["ell"])
    p.op("t", lambda t: t.matmul(T["psA"][:, 928:944], T["c_selC"][:],
         T["ell"][:], start=True, stop=True), reads=["c_selC", "ell"],
         writes=["psA"], tag=f"selC{step}")
    p.op("v", lambda v: v.tensor_copy(out=T["ellC"][:], in_=T["psA"][:, 928:944]),
         reads=["psA"], writes=["ellC"])
    p.op("v", lambda v: v.tensor_tensor(out=T["ellprev"][:], in0=T["ell"][:],
         in1=T["loga"][:], op=ALU.subtract), reads=["ell", "loga"], writes=["ellprev"])
    p.op("v", lambda v: v.tensor_tensor(out=T["glt"][:], in0=T["ellprev"][:],
         in1=T["logb"][:], op=ALU.add), reads=["ellprev", "logb"], writes=["glt"])
    p.op("v", lambda v: v.tensor_tensor(out=T["d1"][:], in0=T["ellC"][:],
         in1=T["ell"][:], op=ALU.subtract), reads=["ellC", "ell"], writes=["d1"])
    for src, dst in [("ell", "e_l"), ("ellprev", "e_lp"), ("d1", "e_ct"),
                     ("ellC", "e_lC")]:
        p.op("s", lambda s, a=src, b=dst: s.activation(out=T[b][:], in_=T[a][:],
             func=ACT.Exp), reads=[src], writes=[dst])
    def selmm(t):
        for e in range(2):
            for sq in range(2):
                j = e * 2 + sq
                inst = t.matmul(T["psA"][:, 944 + sq * 8:944 + sq * 8 + 8],
                                T["c_sel4"][:, j * 128:(j + 1) * 128],
                                T["ell"][:, e:16:2],
                                start=(e == 0 and sq == 0), stop=(e == 1 and sq == 1))
        return inst
    p.op("t", selmm, reads=["c_sel4", "ell"], writes=["psA"], tag=f"selmm{step}")
    p.op("s", lambda s: s.activation(out=T["e_lC_st"][:], in_=T["psA"][:, 944:960],
         func=ACT.Exp), reads=["psA"], writes=["e_lC_st"], tag=f"elcst{step}")
    if dbg:
        kb.dump("ell", T["ell"][:], ["ell"])
        kb.dump("glt", T["glt"][:], ["glt"])
        kb.dump("zat", T["zat"][:], ["zat"])
        kb.dump("alpha", T["alpha"][:], ["alpha"])
        kb.dump("loga", T["loga"][:], ["loga"])
        kb.dump("xlntm", T["xln_tm"][:], ["xln_tm"])
        kb.dump("mn", T["mn"][:], ["mn"])
        kb.dump("sum1", T["sum1"][:], ["sum1"])
        kb.dump("rstd", T["rstd"][:], ["rstd"])
        kb.dump("ssqd", T["ssq"][:], ["ssq"])
        kb.dump("xcc", T["xc"][:], ["xc"])
        kb.dump("xlnfm", T["xln_fm"][:], ["xln_fm"])


def emit_attn_step2(kb, step):
    p, T = kb.p, kb.T
    dbg = (step == 0)
    for src, ssq in [("q_tm", "ssq_q"), ("k_tm", "ssq_k")]:
        p.op("s", lambda s, a=src: s.activation(out=T["scrD"][:], in_=T[a][:],
             func=ACT.Square), reads=[src], writes=["scrD"], tag=f"l2sq{step}")
        p.op("v", lambda v, o=ssq: v.tensor_reduce(
             out=T[o][:], in_=T["scrD"][:].rearrange("p (h e) -> p h e", h=NH),
             axis=AX.X, op=ALU.add), reads=["scrD"], writes=[ssq],
             tag=f"l2r{step}", deferred=True)
    for ssq, n, rn in [("ssq_q", "nq", "rn_q"), ("ssq_k", "nk", "rn_k")]:
        p.op("s", lambda s, a=ssq, b=n: s.activation(out=T[b][:], in_=T[a][:],
             func=ACT.Sqrt), reads=[ssq], writes=[n])
        p.op("v", lambda v, b=n: v.tensor_scalar_max(out=T[b][:], in0=T[b][:],
             scalar1=1e-12), reads=[n], writes=[n], deferred=True)
        p.op("v", lambda v, b=n, c=rn: v.reciprocal(out=T[c][:], in_=T[b][:]),
             reads=[n], writes=[rn], deferred=True)
    p.op("s", lambda s: s.activation(out=T["lnrnq"][:], in_=T["nq"][:],
         func=ACT.Ln), reads=["nq"], writes=["lnrnq"])
    p.op("s", lambda s: s.activation(out=T["lnrnk"][:], in_=T["nk"][:],
         func=ACT.Ln), reads=["nk"], writes=["lnrnk"])
    p.op("v", lambda v: v.tensor_scalar(out=T["lnrnq"][:], in0=T["lnrnq"][:],
         scalar1=-1.0, scalar2=None, op0=ALU.mult), reads=["lnrnq"], writes=["lnrnq"])
    p.op("v", lambda v: v.tensor_scalar(out=T["lnrnk"][:], in0=T["lnrnk"][:],
         scalar1=-1.0, scalar2=None, op0=ALU.mult), reads=["lnrnk"], writes=["lnrnk"])
    p.op("v", lambda v: v.tensor_tensor(out=T["biasWN"][:], in0=T["lnrnk"][:],
         in1=T["ell"][:], op=ALU.subtract), reads=["lnrnk", "ell"], writes=["biasWN"])
    p.op("v", lambda v: v.tensor_tensor(out=T["nlt"][:], in0=T["glt"][:],
         in1=T["lnrnk"][:], op=ALU.add), reads=["glt", "lnrnk"], writes=["nlt"])
    p.op("v", lambda v: v.tensor_tensor(out=T["wlt"][:], in0=T["ell"][:],
         in1=T["lnrnq"][:], op=ALU.add), reads=["ell", "lnrnq"], writes=["wlt"])
    for srcn, hi, lo in [("wlt", "whi", "wlo"), ("nlt", "ghi", "glo"),
                         ("biasWN", "bhi", "blo")]:
        p.op("v", lambda v, a=srcn: v.tensor_copy(out=T["hilo16"][:], in_=T[a][:]),
             reads=[srcn], writes=["hilo16"], tag="hs")
        p.op("v", lambda v, h=hi: v.tensor_copy(out=T[h][:], in_=T["hilo16"][:]),
             reads=["hilo16"], writes=[hi], tag="hs2")
        p.op("v", lambda v, a=srcn, h=hi, l=lo: v.tensor_tensor(out=T[l][:],
             in0=T[a][:], in1=T[h][:], op=ALU.subtract), reads=[srcn, hi],
             writes=[lo], tag="hs3")
    p.op("v", lambda v: v.tensor_tensor(out=T["sq2"][:], in0=T["rn_q"][:],
         in1=T["e_l"][:], op=ALU.mult), reads=["rn_q", "e_l"], writes=["sq2"])
    p.op("v", lambda v: v.tensor_scalar(out=T["sq2n"][:], in0=T["sq2"][:],
         scalar1=-1.0, scalar2=None, op0=ALU.mult), reads=["sq2"], writes=["sq2n"])
    p.op("v", lambda v: v.memset(T["sc3"][:, 16:64], 0.0), writes=["sc3"])
    p.op("v", lambda v: v.tensor_copy(out=T["sc3"][:, 0:16], in_=T["sq2n"][:]),
         reads=["sq2n"], writes=["sc3"])
    p.op("t", lambda t: t.transpose(T["psA"][0:64, 896:1024], T["sc3"][:],
         T["c_id128f"][:]), reads=["sc3", "c_id128f"], writes=["psA"],
         tag=f"trsc{step}")
    p.op("v", lambda v: v.tensor_copy(out=T["scT3"][:], in_=T["psA"][0:16, 896:1024]),
         reads=["psA"], writes=["scT3"], tag=f"trsce{step}")
    def indmm(t):
        for ci in range(8):
            inst = t.matmul(T["psA"][:, ci * 128:(ci + 1) * 128],
                            T["c_indh1"][:, ci * 128:(ci + 1) * 128],
                            T["scT3"][:], start=True, stop=True)
        return inst
    p.op("t", indmm, reads=["c_indh1", "scT3"], writes=["psA"], tag=f"indmm{step}")
    p.op("v", lambda v: v.tensor_tensor(out=T["qh_fm"][:], in0=T["q_fm"][:],
         in1=T["psA"][:], op=ALU.mult), reads=["q_fm", "psA"], writes=["qh_fm"],
         tag=f"scap{step}")
    p.op("v", lambda v: v.tensor_tensor(out=T["skh"][:], in0=T["beta"][:],
         in1=T["e_lp"][:], op=ALU.mult), reads=["beta", "e_lp"], writes=["skh"])
    p.op("v", lambda v: v.tensor_tensor(out=T["skh2"][:], in0=T["skh"][:],
         in1=T["rn_k"][:], op=ALU.mult), reads=["skh", "rn_k"], writes=["skh2"])
    p.op("v", lambda v: v.tensor_tensor(out=T["sct"][:], in0=T["e_ct"][:],
         in1=T["rn_k"][:], op=ALU.mult), reads=["e_ct", "rn_k"], writes=["sct"])
    R3 = lambda: T["R"][:].rearrange("p (h c) -> p h c", h=NH)
    v3 = lambda nm: T[nm][:].rearrange("p (h e) -> p h e", h=NH)
    bc = lambda nm: T[nm][:].unsqueeze(2).broadcast_to([128, NH, HD])
    p.op("v", lambda v: v.tensor_tensor(out=R3()[:, :, 0:64], in0=v3("v_tm"),
         in1=bc("beta"), op=ALU.mult), reads=["v_tm", "beta"], writes=["R"],
         tag=f"Rv{step}")
    p.op("v", lambda v: v.tensor_tensor(out=R3()[:, :, 64:128], in0=v3("k_tm"),
         in1=bc("skh2"), op=ALU.mult), reads=["k_tm", "skh2"], writes=["R"],
         tag=f"Rk{step}")
    p.op("v", lambda v: v.tensor_tensor(out=v3("kt_tm"), in0=v3("k_tm"),
         in1=bc("sct"), op=ALU.mult), reads=["k_tm", "sct"], writes=["kt_tm"],
         tag=f"kt{step}")
    if dbg:
        kb.dump("qh_fm", T["qh_fm"][:], ["qh_fm"])
        kb.dump("scT3", T["scT3"][:], ["scT3"])
        kb.dump("sq2n", T["sq2n"][:], ["sq2n"])
        kb.dump("qfm", T["q_fm"][:], ["q_fm"])
        kb.dump("R", T["R"][:], ["R"])
        kb.dump("kt", T["kt_tm"][:], ["kt_tm"])


def _halves(fn):
    return [(0, 1024), (1024, 2048)]


def emit_attn_step3(kb, step):
    p, T = kb.p, kb.T
    dbg = (step == 0)
    dv3 = lambda nm, nh: T[nm][:].rearrange("p (h c) -> p h c", h=nh)
    bc16 = lambda nm, w: T[nm][:].unsqueeze(2).broadcast_to([128, NH, w])

    if kb.stage == "SZ":
        return
    for hl in ["whi", "wlo"]:
        p.op("v", lambda v, a=hl: v.tensor_tensor(out=dv3("dW", NH),
             in0=dv3("c_dindW", NH), in1=bc16(a, 64), op=ALU.mult),
             reads=["c_dindW", hl], writes=["dW"], tag=f"dgW{hl}{step}")
        def mmW(t, first=(hl == "whi")):
            for q in range(2):
                inst = t.matmul(T["psB"][:, q * 512:(q + 1) * 512],
                                T["c_ones2"][:], T["dW"][:, q * 512:(q + 1) * 512],
                                start=first, stop=False)
            return inst
        p.op("t", mmW, reads=["c_ones2", "dW"], writes=["psB"], tag=f"mmW{hl}{step}")
        if kb.stage == "SA" and hl == "whi":
            return
    def mmWneg(t):
        for q in range(2):
            inst = t.matmul(T["psB"][:, q * 512:(q + 1) * 512], T["c_id128b"][:],
                            T["c_negW"][:, q * 512:(q + 1) * 512],
                            start=False, stop=True)
        return inst
    p.op("t", mmWneg, reads=["c_id128b", "c_negW"], writes=["psB"],
         tag=f"mmWneg{step}")
    if kb.stage == "SB":
        return
    def expW(s):
        for h in range(NH):
            inst = s.activation(out=T["dW"][:, h * 64:(h + 1) * 64],
                                in_=T["psB"][:, h * 64:(h + 1) * 64], func=ACT.Exp,
                                scale=1.0, bias=T["biasWN"][:, h:h + 1])
        return inst
    p.op("s", expW, reads=["psB"], writes=["dW"], scalar_reads=["biasWN"],
         tag=f"expW{step}")
    if kb.stage == "S0":
        return
    def gramKQ(t):
        inst = None
        for parity in range(2):
            if parity == 1:
                t.drain()
            rlo = 64 * parity
            for h in range(parity, NH, 2):
                for par in range(2):
                    clo = (h // 2) * 128 + 64 * par
                    inst = t.matmul(
                        T["psB"][64 * par:64 * par + 64, h * 64:(h + 1) * 64],
                        T["k_fm"][rlo:rlo + 64, clo:clo + 64],
                        T["q_fm"][rlo:rlo + 64, clo:clo + 64],
                        start=True, stop=True, skip_group_check=True)
        return inst
    p.op("t", gramKQ, reads=["k_fm", "q_fm"], writes=["psB"], tag=f"gKQ{step}")
    p.op("v", lambda v: v.tensor_tensor(out=T["wbar"][:], in0=T["dW"][:],
         in1=T["psB"][:], op=ALU.mult), reads=["dW", "psB"], writes=["wbar"],
         tag=f"wbar{step}")
    if kb.stage == "Sg":
        return

    for hl in ["ghi", "glo"]:
        p.op("v", lambda v, a=hl: v.tensor_tensor(out=dv3("dN2", NH),
             in0=dv3("c_dindN", NH), in1=bc16(a, 128), op=ALU.mult),
             reads=["c_dindN", hl], writes=["dN2"], tag=f"dgN{hl}{step}")
        def mmN(t, first=(hl == "ghi")):
            for q in range(4):
                inst = t.matmul(T["psC"][:, q * 512:(q + 1) * 512], T["c_ones2"][:],
                                T["dN2"][:, q * 512:(q + 1) * 512],
                                start=first, stop=False)
            return inst
        p.op("t", mmN, reads=["c_ones2", "dN2"], writes=["psC"],
             tag=f"mmN{hl}{step}")
    def mmNneg(t):
        for q in range(4):
            inst = t.matmul(T["psC"][:, q * 512:(q + 1) * 512], T["c_id128b"][:],
                            T["c_negN"][:, q * 512:(q + 1) * 512],
                            start=False, stop=True)
        return inst
    p.op("t", mmNneg, reads=["c_id128b", "c_negN"], writes=["psC"],
         tag=f"mmNneg{step}")
    def expN(s):
        for h in range(NH):
            inst = s.activation(out=T["dN"][:, h * 128:(h + 1) * 128],
                                in_=T["psC"][:, h * 128:(h + 1) * 128],
                                func=ACT.Exp, scale=1.0,
                                bias=T["biasWN"][:, h:h + 1])
        return inst
    p.op("s", expN, reads=["psC"], writes=["dN"], scalar_reads=["biasWN"],
         tag=f"expN{step}")

    for hl in ["bhi", "blo"]:
        p.op("v", lambda v, a=hl: v.tensor_tensor(out=dv3("dN2", NH),
             in0=dv3("c_dindN", NH), in1=bc16(a, 128), op=ALU.mult),
             reads=["c_dindN", hl], writes=["dN2"], tag=f"dgT{hl}{step}")
        def mmT(t, first=(hl == "bhi")):
            for q in range(4):
                inst = t.matmul(T["psC"][:, q * 512:(q + 1) * 512], T["c_ones2"][:],
                                T["dN2"][:, q * 512:(q + 1) * 512],
                                start=first, stop=False)
            return inst
        p.op("t", mmT, reads=["c_ones2", "dN2"], writes=["psC"],
             tag=f"mmT{hl}{step}")
    def mmTneg(t):
        for q in range(4):
            inst = t.matmul(T["psC"][:, q * 512:(q + 1) * 512], T["c_id128b"][:],
                            T["c_negNT"][:, q * 512:(q + 1) * 512],
                            start=False, stop=True)
        return inst
    p.op("t", mmTneg, reads=["c_id128b", "c_negNT"], writes=["psC"],
         tag=f"mmTneg{step}")
    def expT(s):
        for h in range(NH):
            inst = s.activation(out=T["dN2"][:, h * 128:(h + 1) * 128],
                                in_=T["psC"][:, h * 128:(h + 1) * 128],
                                func=ACT.Exp, scale=1.0, bias=T["nlt"][:, h:h + 1])
        return inst
    p.op("s", expT, reads=["psC"], writes=["dN2"], scalar_reads=["nlt"],
         tag=f"expT{step}")

    def gramKK(t):
        inst = None
        for parity in range(2):
            if parity == 1:
                t.drain()
            rlo = 64 * parity
            for h in range(parity, NH, 2):
                for par in range(2):
                    clo = (h // 2) * 128 + 64 * par
                    inst = t.matmul(
                        T["psC"][64 * par:64 * par + 64,
                                 h * 128 + 64 * par:h * 128 + 64 * par + 64],
                        T["k_fm"][rlo:rlo + 64, clo:clo + 64],
                        T["k_fm"][rlo:rlo + 64, clo:clo + 64],
                        start=True, stop=True, skip_group_check=True)
        return inst
    p.op("t", gramKK, reads=["k_fm"], writes=["psC"], tag=f"gKK{step}")
    p.op("v", lambda v: v.scalar_tensor_tensor(out=T["nbar"][:], in0=T["dN"][:],
         scalar=-1.0, in1=T["psC"][:], op0=ALU.mult, op1=ALU.mult),
         reads=["dN", "psC"], writes=["nbar"], tag=f"nbar{step}")
    p.op("v", lambda v: v.scalar_tensor_tensor(out=T["pbar"][:], in0=T["dN2"][:],
         scalar=-1.0, in1=T["psC"][:], op0=ALU.mult, op1=ALU.mult),
         reads=["dN2", "psC"], writes=["pbar"], tag=f"pbar{step}")
    if dbg:
        kb.dump("nbar", T["nbar"][:], ["nbar"])
        kb.dump("wbar", T["wbar"][:], ["wbar"])
    if kb.stage in ("S1",):
        return

    Bname, Pname = "nbar", "pbar"
    Balt, Palt = "dN", "dN2"
    Xsrc = "R"
    for lvl in range(6):
        def apply(t, B=Bname, X=Xsrc):
            for h in range(NH):
                sl = slice(h * 128, (h + 1) * 128)
                t.matmul(T["psC"][:, sl], T[B][:, sl], T[X][:, sl],
                         start=(h % 4 == 0), stop=False)
                inst = t.matmul(T["psC"][:, sl], T["c_id128b"][:], T[X][:, sl],
                                start=False, stop=(h % 4 == 3))
            return inst
        p.op("t", lambda t, f=apply: f(t), reads=[Bname, Xsrc, "c_id128b"],
             writes=["psC"], tag=f"ap{lvl}_{step}")
        p.op("v", lambda v: v.tensor_copy(out=T["Xc"][:, 0:1024],
             in_=T["psC"][:, 0:1024]), reads=["psC"], writes=["Xc"],
             tag=f"apev{lvl}_{step}")
        p.op("s", lambda s: s.copy(out=T["Xc"][:, 1024:2048],
             in_=T["psC"][:, 1024:2048]), reads=["psC"], writes=["Xc"],
             tag=f"apes{lvl}_{step}")
        Xsrc = "Xc"
        if lvl < 5:
            def sqB(t, B=Bname, P=Pname):
                for h in range(NH):
                    sl = slice(h * 128, (h + 1) * 128)
                    inst = t.matmul(T["psC"][:, sl], T[P][:, sl], T[B][:, sl],
                                    start=(h % 4 == 0), stop=(h % 4 == 3))
                return inst
            p.op("t", lambda t, f=sqB: f(t), reads=[Bname, Pname],
                 writes=["psC"], tag=f"sqB{lvl}_{step}")
            p.op("v", lambda v, d=Balt: v.tensor_copy(out=T[d][:], in_=T["psC"][:]),
                 reads=["psC"], writes=[Balt], tag=f"sqBe{lvl}_{step}")
            if lvl < 4:
                def sqP(t, B=Bname, P=Pname):
                    for h in range(NH):
                        sl = slice(h * 128, (h + 1) * 128)
                        inst = t.matmul(T["psC"][:, sl], T[B][:, sl], T[P][:, sl],
                                        start=(h % 4 == 0), stop=(h % 4 == 3))
                    return inst
                p.op("t", lambda t, f=sqP: f(t), reads=[Bname, Pname],
                     writes=["psC"], tag=f"sqP{lvl}_{step}")
                p.op("s", lambda s, d=Palt: s.copy(out=T[d][:], in_=T["psC"][:]),
                     reads=["psC"], writes=[Palt], tag=f"sqPe{lvl}_{step}")
                Pname, Palt = Palt, Pname
            Bname, Balt = Balt, Bname
    if dbg:
        kb.dump("Xs", T["Xc"][:], ["Xc"])


def emit_attn_step4(kb, step, x2_d, x2ln_d):
    p, T = kb.p, kb.T
    dbg = (step == 0)
    r0, r1 = step * C, (step + 1) * C

    def xkt(t):
        pb = T["psA"][:].bitcast(BF16)
        for h in range(NH):
            rlo = 64 * (h % 2)
            inst = t.transpose(pb[rlo:rlo + 64, (h // 2) * 128:(h // 2) * 128 + 128],
                               T["Xc"][:, h * 128 + 64:(h + 1) * 128],
                               T["c_id128b"][:], tile_position=(0, rlo))
        return inst
    p.op("t", xkt, reads=["Xc", "c_id128b"], writes=["psA"], tag=f"xkt{step}")
    p.op("v", lambda v: v.tensor_copy(out=T["xkT"][:],
         in_=T["psA"][:].bitcast(BF16)[:, 0:1024]),
         reads=["psA"], writes=["xkT"], tag=f"xkte{step}")

    def gmm(t):
        inst = None
        for h in range(NH):
            cl = slice(h * 64, (h + 1) * 64)
            for par in range(2):
                rlo = 64 * (h % 2)
                t.matmul(
                    T["psB"][64 * par:64 * par + 64, cl],
                    T["xkT"][rlo:rlo + 64, (h // 2) * 128 + 64 * par:
                             (h // 2) * 128 + 64 * par + 64],
                    T["negS_b"][rlo:rlo + 64,
                                (par * 8 + h // 2) * 64:(par * 8 + h // 2) * 64 + 64],
                    start=True, stop=False, skip_group_check=True)
            inst = t.matmul(T["psB"][:, cl], T["c_id128b"][:],
                            T["Xc"][:, h * 128:h * 128 + 64],
                            start=False, stop=True, skip_group_check=True)
        return inst
    p.op("t", gmm, reads=["Xc", "xkT", "negS_b", "c_id128b"], writes=["psB"],
         tag=f"gmm{step}")
    p.op("v", lambda v: v.tensor_copy(out=T["gt"][:], in_=T["psB"][:]),
         reads=["psB"], writes=["gt"], tag=f"gte{step}")
    if dbg:
        kb.dump("gt", T["gt"][:], ["gt"])

    def omm1(t):
        inst = None
        for parity in range(2):
            if parity == 1:
                t.drain()
            rlo = 64 * parity
            for h in range(parity, NH, 2):
                cl = slice(h * 64, (h + 1) * 64)
                for par in range(2):
                    inst = t.matmul(
                        T["psC"][64 * par:64 * par + 64, cl],
                        T["qh_fm"][rlo:rlo + 64, (h // 2) * 128 + 64 * par:
                                   (h // 2) * 128 + 64 * par + 64],
                        T["negS_b"][rlo:rlo + 64, (par * 8 + h // 2) * 64:
                                    (par * 8 + h // 2) * 64 + 64],
                        start=True, stop=True, skip_group_check=True)
        return inst

    def omm2(t):
        inst = None
        for par in (1, 0):
            if par == 0:
                t.drain()
            for h in range(NH):
                cl = slice(h * 64, (h + 1) * 64)
                inst = t.matmul(
                    T["psB"][64 * par:64 * par + 64, cl],
                    T["wbar"][64 * par:64 * par + 64, cl],
                    T["gt"][64 * par:64 * par + 64, cl],
                    start=True, stop=True, skip_group_check=True)
        return inst
    p.op("t", omm1, reads=["qh_fm", "negS_b"], writes=["psC"],
         tag=f"omm1{step}")
    p.op("t", omm2, reads=["wbar", "gt"], writes=["psB"],
         tag=f"omm2{step}")
    p.op("s", lambda s: s.copy(out=T["y_tm"][:], in_=T["psC"][:, 0:1024]),
         reads=["psC"], writes=["y_tm"], tag=f"ye1{step}")
    p.op("v", lambda v: v.tensor_tensor(out=T["y_tm"][:], in0=T["y_tm"][:],
         in1=T["psB"][:, 0:1024], op=ALU.add), reads=["y_tm", "psB"],
         writes=["y_tm"], tag=f"ye2{step}")
    if dbg:
        kb.dump("y0", T["y_tm"][:], ["y_tm"])
    if step == 1 and "y1" in kb.dbg_outs:
        kb.dump("y1", T["y_tm"][:], ["y_tm"])

    def smm(t):
        inst = None
        for par in range(2):
            if par == 1:
                t.drain()
            for h in range(NH):
                rlo = 64 * (h % 2)
                inst = t.matmul(
                    T["psA"][rlo:rlo + 64,
                             (par * 8 + h // 2) * 64:(par * 8 + h // 2) * 64 + 64],
                    T["kt_tm"][64 * par:64 * par + 64, h * 64:(h + 1) * 64],
                    T["gt"][64 * par:64 * par + 64, h * 64:(h + 1) * 64],
                    start=True, stop=True, skip_group_check=True)
        return inst
    p.op("t", smm, reads=["kt_tm", "gt"], writes=["psA"], tag=f"smm{step}")
    st3 = lambda: T["negS"][:].rearrange("p (c e) -> p c e", c=16)
    ebc = lambda: T["e_lC_st"][:].unsqueeze(2).broadcast_to([128, 16, HD])
    p.op("v", lambda v: v.tensor_tensor(out=T["scrD"][:].rearrange(
         "p (c e) -> p c e", c=16), in0=st3(), in1=ebc(), op=ALU.mult),
         reads=["negS", "e_lC_st"], writes=["scrD"], tag=f"sdec{step}")
    p.op("v", lambda v: v.tensor_tensor(out=T["negS"][:], in0=T["scrD"][:],
         in1=T["psA"][:], op=ALU.subtract), reads=["scrD", "psA"],
         writes=["negS"], tag=f"supd{step}")
    p.op("s", lambda s: s.copy(out=T["negS_b"][:], in_=T["negS"][:]),
         reads=["negS"], writes=["negS_b"], tag=f"supb{step}")

    p.op("s", lambda s: s.activation(out=T["scrD2"][:], in_=T["y_tm"][:],
         func=ACT.Square), reads=["y_tm"], writes=["scrD2"], tag=f"ysq{step}")
    p.op("v", lambda v: v.tensor_reduce(out=T["ssq"][:], in_=T["scrD2"][:],
         axis=AX.X, op=ALU.add), reads=["scrD2"], writes=["ssq"],
         tag=f"yssq{step}", deferred=True)
    p.op("s", lambda s: s.activation(out=T["sd"][:], in_=T["ssq"][:],
         func=ACT.Sqrt, bias=T["eps1"][:, 0:1], scale=1.0 / D),
         reads=["ssq", "eps1"], writes=["sd"], tag=f"ysd{step}")
    p.op("v", lambda v: v.reciprocal(out=T["rstd"][:], in_=T["sd"][:]),
         reads=["sd"], writes=["rstd"], tag=f"yr{step}", deferred=True)
    p.op("s", lambda s: s.activation(out=T["scrD2"][:], in_=T["y_tm"][:],
         func=ACT.Identity, scale=T["rstd"][:, 0:1]), reads=["y_tm"],
         writes=["scrD2"], scalar_reads=["rstd"], tag=f"yn{step}")
    p.op("v", lambda v: v.tensor_tensor(out=T["scrD2"][:], in0=T["scrD2"][:],
         in1=T["gnorm_bc"][:], op=ALU.mult), reads=["scrD2", "gnorm_bc"],
         writes=["scrD2"], tag=f"yg{step}")
    p.op("v", lambda v: v.tensor_tensor(out=T["xln_tm"][:], in0=T["scrD2"][:],
         in1=T["g_tm"][:], op=ALU.mult), reads=["scrD2", "g_tm"],
         writes=["xln_tm"], tag=f"ygg{step}")
    def try8(t):
        pb = T["psA"][:].bitcast(BF16)
        for ci in range(8):
            inst = t.transpose(pb[:, ci * 128:(ci + 1) * 128],
                               T["xln_tm"][:, ci * 128:(ci + 1) * 128],
                               T["c_id128b"][:])
        return inst
    p.op("t", try8, reads=["xln_tm", "c_id128b"], writes=["psA"], tag=f"try{step}")
    p.op("v", lambda v: v.tensor_copy(out=T["yf_fm"][:],
         in_=T["psA"][:].bitcast(BF16)[:, 0:1024]),
         reads=["psA"], writes=["yf_fm"], tag=f"trye{step}")
    def womm(t):
        for half in range(2):
            for i in range(8):
                inst = t.matmul(T["psB"][:, half * 512:(half + 1) * 512],
                                T["yf_fm"][:, i * 128:(i + 1) * 128],
                                T["w_Wo"][:, i * 1024 + half * 512:
                                          i * 1024 + (half + 1) * 512],
                                start=(i == 0), stop=(i == 7))
        return inst
    p.op("t", womm, reads=["yf_fm", "w_Wo"], writes=["psB"], tag=f"womm{step}")

    _ln_ops(kb, "psB", lambda T: T["psB"][:], "scrD",
            lambda T: T["scrD"][:], "m2", "r2", "b2", tag=f"ln2_{step}")
    p.op("v", lambda v: v.tensor_tensor(out=T["xc"][:], in0=T["xc"][:],
         in1=T["scrD"][:], op=ALU.add), reads=["xc", "scrD"], writes=["xc"],
         tag=f"res1{step}")
    for sq in range(NSEQ):
        p.dma("g", lambda g, sq=sq: g.dma_start(
            out=x2_d[sq, r0:r1, :], in_=T["xc"][sq * C:(sq + 1) * C, :]),
            reads=["xc"], tag=f"stx2_{step}")
    if dbg:
        kb.dump("x2_0", T["xc"][:], ["xc"])

    _ln_ops(kb, "xc", lambda T: T["xc"][:], "x2ln_c",
            lambda T: T["x2ln_c"][:], "m3", "r3", "b3", tag=f"ln3_{step}")
    for sq in range(NSEQ):
        p.dma("g", lambda g, sq=sq: g.dma_start(
            out=x2ln_d[sq, r0:r1, :], in_=T["x2ln_c"][sq * C:(sq + 1) * C, :]),
            reads=["x2ln_c"], tag=f"stxln_{step}")
    p.op("v", lambda v: v.tensor_reduce(out=T["mx1"][:], in_=T["x2ln_c"][:],
         axis=AX.X, op=ALU.max, apply_absolute_value=True),
         reads=["x2ln_c"], writes=["mx1"], tag=f"mx{step}", deferred=True)
    p.op("v", lambda v: v.tensor_tensor(out=T["mxacc"][:], in0=T["mxacc"][:],
         in1=T["mx1"][:], op=ALU.max), reads=["mxacc", "mx1"], writes=["mxacc"],
         tag=f"mxa{step}")
    if dbg:
        kb.dump("x2ln_0", T["x2ln_c"][:], ["x2ln_c"])


def lower(kb):
    nc, p = kb.nc, kb.p
    T = kb.T
    es = kb.es
    sems = {}
    dsems = {}
    for e in ("t", "v", "s", "g", "y"):
        sems[e] = es.enter_context(nc.semaphore("sem_" + e))
    for e in ("g", "y", "s", "v"):
        dsems[e] = [es.enter_context(nc.semaphore(f"dsem_{e}{i}")) for i in range(8)]
    block = es.enter_context(nc.Block())
    p.finalize(nc, None, sems, dsems)

    @block.tensor
    def _(t):
        p.emit("t", t, sems, dsems)

    @block.vector
    def _(v):
        p.emit("v", v, sems, dsems)

    @block.scalar
    def _(s):
        p.emit("s", s, sems, dsems)

    @block.gpsimd
    def _(g):
        p.emit("g", g, sems, dsems)

    @block.sync
    def _(y):
        p.emit("y", y, sems, dsems)

    kb.es.close()
    return nc


def psum_init(kb):
    p, T = kb.p, kb.T
    for ps, res in [("psA", "psA"), ("psB", "psB"), ("psC", "psC")]:
        p.op("v", lambda v, a=ps: v.memset(T[a][:], 0.0), writes=[res],
             tag="psz")


def final_barrier(kb):
    p = kb.p
    for e in ("g", "y", "s", "v"):
        n = p.ndma[e]
        if n:
            def mk(eng_obj, e=e, n=n):
                return None
            o = p.op(e, lambda eng: eng.nop(), tag="final_wait")
            last = [op for op in p.ops[e] if op.is_dma]
            if last:
                o.deps.append(last[-1])


def _gmax_collective(kb, srcs, dsts, tagn):
    p, T, nc = kb.p, kb.T, kb.nc
    n = len(srcs)
    cin = kb.dtmp(f"cc_in_{tagn}", [128, n])
    cout = kb.dtmp(f"cc_out_{tagn}", [128, n])
    sd = kb.dtmp(f"cc_s_{tagn}", [n])
    for i, s in enumerate(srcs):
        p.dma("g", lambda g, i=i, s=s: g.dma_start(
            out=cin.ap()[:, i:i + 1], in_=T[s][:],
            allow_slow_non_contiguous=True), reads=[s],
            writes=[f"ccin_{tagn}"], tag=f"cc{tagn}a")
    p.op("g", lambda g: g.collective_compute(
        "AllReduce", ALU.max, replica_groups=[list(range(8))],
        ins=[cin.ap()], outs=[cout.ap()]),
        reads=[f"ccin_{tagn}"], writes=[f"ccout_{tagn}"], tag=f"cc{tagn}b",
        is_cc=True)
    p.dma("g", lambda g: g.dma_start(out=T["ccrow"][0:128, 0:n], in_=cout.ap()),
          reads=[f"ccout_{tagn}"], writes=["ccrow"], tag=f"cc{tagn}c")
    p.op("t", lambda t: t.transpose(T["psA"][0:n, 896:1024],
         T["ccrow"][:, 0:n], T["c_id128f"][:]), reads=["ccrow", "c_id128f"],
         writes=["psA"], tag=f"cc{tagn}t")
    p.op("v", lambda v: v.tensor_reduce(out=T["ccred"][0:n, 0:1],
         in_=T["psA"][0:n, 896:1024], axis=AX.X, op=ALU.max),
         reads=["psA"], writes=["ccred"], tag=f"cc{tagn}r")
    p.dma("g", lambda g: g.dma_start(out=sd.ap(), in_=T["ccred"][0:n, 0:1]),
          reads=["ccred"], writes=[f"ccs_{tagn}"], tag=f"cc{tagn}d")
    for i, d in enumerate(dsts):
        p.dma("y", lambda y, i=i, d=d: y.dma_start(
            out=T[d][:], in_=sd.ap()[i:i + 1].rearrange("(a b) -> a b", a=1)
            .broadcast_to([128, 1])), reads=[f"ccs_{tagn}"], writes=[d],
            tag=f"cc{tagn}e")


def _fix_cc_deps(kb, tagn):
    pass


def emit_ffn_head(kb, x2_d, x2ln_d):
    p, T, nc = kb.p, kb.T, kb.nc
    NB = IFF // 32
    kb.din("w_gate_d", [IFF, D]); kb.din("w_up_d", [IFF, D])
    kb.din("w_down_d", [D, IFF])
    uraw_d = kb.dtmp("uraw_d", [IFF, NTOK]).ap()
    graw_d = kb.dtmp("graw_d", [IFF, NTOK]).ap()
    kb.sb("ccrow", [128, 8], F32)
    kb.sb("ccred", [8, 1], F32)
    kb.sb("sgx", [128, 1], F32)
    kb.sb("rsgx", [128, 1], F32)
    kb.sb("Mxg", [128, 1], F32)
    kb.sb("xt1", [128, D], F32)
    kb.sb("xq_tm", [128, D], BF16)
    kb.sb("mxu", [128, 1], F32); kb.sb("mxg", [128, 1], F32)
    kb.sb("mxp", [128, 1], F32); kb.sb("mxo", [128, 1], F32)
    kb.sb("Mug", [128, 1], F32); kb.sb("Mgg", [128, 1], F32)
    kb.sb("Mpg", [128, 1], F32); kb.sb("Mog", [128, 1], F32)
    kb.sb("cu", [128, 1], F32); kb.sb("cg", [128, 1], F32)
    kb.sb("sgu", [128, 1], F32); kb.sb("sgg", [128, 1], F32)
    kb.sb("c2", [128, 1], F32); kb.sb("co", [128, 1], F32)
    kb.sb("sgo", [128, 1], F32)
    kb.push_scope()
    kb.sb("wu_l", [128, 8 * IFF], BF16)
    kb.sb("wg_l", [128, 8 * IFF], BF16)
    kb.sb("xq_fm", [128, 8 * NTOK], BF16)
    kb.sb("wst", [128, IFF], F32)
    kb.sb("bmax", [128, 96], F32)
    kb.sb("bsc", [128, 96], F32)
    kb.sb("brs", [128, 96], F32)
    kb.sb("dq1", [128, IFF], F32)
    kb.sb("dqo", [128, IFF], BF16)
    kb.sb("u_st", [128, 1024], F32)
    kb.sb("g_st", [128, 1024], F32)

    _gmax_collective(kb, ["mxacc"], ["Mxg"], "x")
    p.op("v", lambda v: v.tensor_scalar(out=T["sgx"][:], in0=T["Mxg"][:],
         scalar1=1.0 / QMAX, scalar2=1e-12, op0=ALU.mult, op1=ALU.max),
         reads=["Mxg"], writes=["sgx"], tag="sgx")
    p.op("v", lambda v: v.reciprocal(out=T["rsgx"][:], in_=T["sgx"][:]),
         reads=["sgx"], writes=["rsgx"], tag="rsgx")

    for wsrc, wdst in [("w_up_d", "wu_l"), ("w_gate_d", "wg_l")]:
        for ot in range(24):
            wd = kb.dram[wsrc].ap()
            p.dma("y", lambda y, ot=ot, wd=wd: y.dma_start(
                out=T["wst"][:, 0:1024], in_=wd[ot * 128:(ot + 1) * 128, :]),
                writes=["wst"], tag="wld")
            emit_dequant_tile(kb, 1024, wsrc + str(ot))
            def wtr(t, ot=ot):
                pb = T["psA"][:].bitcast(BF16)
                for ci in range(8):
                    inst = t.transpose(pb[:, ci * 128:(ci + 1) * 128],
                                       T["dqo"][:, ci * 128:(ci + 1) * 128],
                                       T["c_id128b"][:])
                return inst
            p.op("t", wtr, reads=["dqo", "c_id128b"], writes=["psA"], tag="wtr")
            p.op("v", lambda v, wdst=wdst, ot=ot: v.tensor_copy(
                out=T[wdst][:].rearrange("p (c o) -> p c o", c=8)[:, :, ot * 128:(ot + 1) * 128],
                in_=T["psA"][:].bitcast(BF16)[:, 0:1024].rearrange(
                    "p (c o) -> p c o", c=8)),
                reads=["psA"], writes=[wdst], tag="wtre")

    for ti in range(NTOK // 128):
        sq, r0 = divmod(ti * 128, S)
        p.dma("y", lambda y, sq=sq, r0=r0: y.dma_start(
            out=T["xt1"][:], in_=x2ln_d[sq, r0:r0 + 128, :]), writes=["xt1"],
            tag=f"xqld{ti}")
        p.op("s", lambda s: s.activation(out=T["xt1"][:], in_=T["xt1"][:],
             func=ACT.Identity, scale=T["rsgx"][:, 0:1], bias=T["magicC"][:, 0:1]),
             reads=["xt1"], writes=["xt1"], scalar_reads=["rsgx", "magicC"],
             tag=f"xqr{ti}")
        p.op("v", lambda v: v.tensor_scalar(out=T["xq_tm"][:], in0=T["xt1"][:],
             scalar1=MAGIC, scalar2=None, op0=ALU.subtract),
             reads=["xt1"], writes=["xq_tm"], tag=f"xqs{ti}")
        def xtr(t):
            pb = T["psA"][:].bitcast(BF16)
            for ci in range(8):
                inst = t.transpose(pb[:, ci * 128:(ci + 1) * 128],
                                   T["xq_tm"][:, ci * 128:(ci + 1) * 128],
                                   T["c_id128b"][:])
            return inst
        p.op("t", xtr, reads=["xq_tm", "c_id128b"], writes=["psA"], tag=f"xqt{ti}")
        p.op("v", lambda v, ti=ti: v.tensor_copy(
             out=T["xq_fm"][:].rearrange("p (c t) -> p c t", c=8)[:, :, ti * 128:(ti + 1) * 128],
             in_=T["psA"][:].bitcast(BF16)[:, 0:1024].rearrange("p (c t) -> p c t", c=8)),
             reads=["psA"], writes=["xq_fm"], tag=f"xqe{ti}")
    return uraw_d, graw_d


def emit_dequant_tile(kb, width, tagn, sfx=""):
    p, T = kb.p, kb.T
    nb = width // 32
    W, BM, BS, BR, D1, DQ = ("wst" + sfx, "bmax" + sfx, "bsc" + sfx,
                             "brs" + sfx, "dq1" + sfx, "dqo" + sfx)
    v3 = lambda nm, w=width: T[nm][:, 0:w].rearrange("p (b k) -> p b k", k=32)
    bcb = lambda nm, c=nb: T[nm][:, 0:c].unsqueeze(2).broadcast_to([128, c, 32])
    p.op("v", lambda v: v.tensor_reduce(out=T[BM][:, 0:nb], in_=v3(W),
         axis=AX.X, op=ALU.max, apply_absolute_value=True),
         reads=[W], writes=[BM], tag="dq_bm", deferred=True)
    p.op("v", lambda v: v.tensor_scalar(out=T[BS][:, 0:nb], in0=T[BM][:, 0:nb],
         scalar1=1.0 / 7.0, scalar2=1e-12, op0=ALU.mult, op1=ALU.max),
         reads=[BM], writes=[BS], tag="dq_bs", deferred=True)
    p.op("v", lambda v: v.reciprocal(out=T[BR][:, 0:nb], in_=T[BS][:, 0:nb]),
         reads=[BS], writes=[BR], tag="dq_rs", deferred=True)
    p.op("v", lambda v: v.tensor_tensor(out=v3(D1), in0=v3(W),
         in1=bcb(BR), op=ALU.mult), reads=[W, BR], writes=[D1],
         tag="dq_m1")
    p.op("v", lambda v: v.tensor_scalar(out=T[D1][:, 0:width], in0=T[D1][:, 0:width],
         scalar1=MAGIC, scalar2=MAGIC, op0=ALU.add, op1=ALU.subtract),
         reads=[D1], writes=[D1], tag="dq_rnd")
    p.op("v", lambda v: v.tensor_tensor(out=v3(DQ), in0=v3(D1),
         in1=bcb(BS), op=ALU.mult), reads=[D1, BS], writes=[DQ],
         tag="dq_m2")


def emit_ffn_f1(kb, uraw_d, graw_d):
    p, T = kb.p, kb.T
    NSLAB = NTOK // 512
    p.op("v", lambda v: v.memset(T["mxu"][:], 0.0), writes=["mxu"])
    p.op("v", lambda v: v.memset(T["mxg"][:], 0.0), writes=["mxg"])
    xq3 = lambda: T["xq_fm"][:].rearrange("p (c t) -> p c t", c=8)
    for sl in range(NSLAB):
        for wnm, stg, mx, raw, ps in [("wu_l", "u_st", "mxu", uraw_d, "psA"),
                                      ("wg_l", "g_st", "mxg", graw_d, "psB")]:
            for oc in range(24):
                half = oc % 2
                def mm(t, wnm=wnm, oc=oc, sl=sl, half=half, ps=ps):
                    for i in range(8):
                        inst = t.matmul(
                            T[ps][:, half * 512:(half + 1) * 512],
                            T[wnm][:].rearrange("p (c o) -> p c o", c=8)[:, i, oc * 128:(oc + 1) * 128],
                            xq3()[:, i, sl * 512:(sl + 1) * 512],
                            start=(i == 0), stop=(i == 7))
                    return inst
                p.op("t", mm, reads=[wnm, "xq_fm"], writes=[ps], tag=f"f1mm{sl}_{oc}")
                p.op("s", lambda s, stg=stg, half=half, ps=ps: s.copy(
                     out=T[stg][:, half * 512:(half + 1) * 512],
                     in_=T[ps][:, half * 512:(half + 1) * 512]),
                     reads=[ps], writes=[stg], tag=f"f1ev{sl}_{oc}")
                p.op("v", lambda v, stg=stg, half=half: v.tensor_reduce(
                     out=T["mx1"][:], in_=T[stg][:, half * 512:(half + 1) * 512],
                     axis=AX.X, op=ALU.max, apply_absolute_value=True),
                     reads=[stg], writes=["mx1"], tag=f"f1mx{sl}_{oc}", deferred=True)
                p.op("v", lambda v, mx=mx: v.tensor_tensor(out=T[mx][:],
                     in0=T[mx][:], in1=T["mx1"][:], op=ALU.max),
                     reads=[mx, "mx1"], writes=[mx], tag=f"f1mxa{sl}_{oc}")
                p.dma("g", lambda g, raw=raw, oc=oc, sl=sl, stg=stg, half=half:
                      g.dma_start(out=raw[oc * 128:(oc + 1) * 128,
                                          sl * 512:(sl + 1) * 512],
                                  in_=T[stg][:, half * 512:(half + 1) * 512]),
                      reads=[stg], tag=f"f1st{sl}_{oc}")


def _scal(kb, fn, reads, writes, tag):
    kb.p.op("v", fn, reads=reads, writes=writes, tag=tag, deferred=True)


def emit_ffn_rest(kb, x2_d, x2ln_d, out_d, uraw_d, graw_d):
    p, T, nc = kb.p, kb.T, kb.nc
    _gmax_collective(kb, ["mxu", "mxg"], ["Mug", "Mgg"], "ug")
    for Mg, sg, cc in [("Mug", "sgu", "cu"), ("Mgg", "sgg", "cg")]:
        _scal(kb, lambda v, Mg=Mg, sg=sg: v.tensor_tensor(out=T[sg][:],
              in0=T[Mg][:], in1=T["sgx"][:], op=ALU.mult), [Mg, "sgx"], [sg], "a2a")
        _scal(kb, lambda v, sg=sg: v.tensor_scalar(out=T[sg][:], in0=T[sg][:],
              scalar1=1.0 / QMAX, scalar2=1e-12, op0=ALU.mult, op1=ALU.max),
              [sg], [sg], "a2b")
        _scal(kb, lambda v, sg=sg, cc=cc: v.reciprocal(out=T[cc][:], in_=T[sg][:]),
              [sg], [cc], "a2c")
        _scal(kb, lambda v, cc=cc: v.tensor_tensor(out=T[cc][:], in0=T[cc][:],
              in1=T["sgx"][:], op=ALU.mult), [cc, "sgx"], [cc], "a2d")

    kb.pop_scope()
    kb.barrier()
    kb.push_scope()
    kb.sb("wst2", [128, IFF], F32)
    kb.sb("bmax2", [128, 96], F32); kb.sb("bsc2", [128, 96], F32)
    kb.sb("brs2", [128, 96], F32)
    kb.sb("dq12", [128, IFF], F32); kb.sb("dqo2", [128, IFF], BF16)
    kb.sb("swg", [128, 4096], F32)
    kb.sb("swu", [128, 4096], F32)
    kb.sb("sw1", [128, 4096], F32)
    kb.sb("sw2", [128, 4096], F32)
    kb.sb("oq_sl", [128, 24 * 512], BF16)
    kb.sb("wd_c0", [128, 1024], BF16); kb.sb("wd_c1", [128, 1024], BF16)
    kb.sb("xr", [128, D], F32)
    kb.sb("of", [128, D], F32)
    kb.sb("scrF", [128, D], F32)
    wdT_d = kb.dtmp("wdT_d", [IFF, 1024], BF16).ap()

    wd = kb.dram["w_down_d"].ap()
    for ot in range(8):
        p.dma("y", lambda y, ot=ot: y.dma_start(
            out=T["wst2"][:], in_=wd[ot * 128:(ot + 1) * 128, :]),
            writes=["wst2"], tag="wdld")
        emit_dequant_tile(kb, IFF, "wd" + str(ot), sfx="2")
        for grp in range(3):
            def wtr(t, grp=grp):
                pb = T["psA"][:].bitcast(BF16)
                for ci in range(8):
                    inst = t.transpose(pb[:, ci * 128:(ci + 1) * 128],
                                       T["dqo2"][:, (grp * 8 + ci) * 128:(grp * 8 + ci + 1) * 128],
                                       T["c_id128b"][:])
                return inst
            p.op("t", wtr, reads=["dqo2", "c_id128b"], writes=["psA"], tag="wdtr")
            p.op("v", lambda v: v.tensor_copy(
                 out=T["dq12"][:, 0:512].bitcast(BF16),
                 in_=T["psA"][:].bitcast(BF16)[:, 0:1024]), reads=["psA"],
                 writes=["dq12"], tag="wdtre")
            for ci in range(8):
                p.dma("g", lambda g, ot=ot, grp=grp, ci=ci: g.dma_start(
                    out=wdT_d[(grp * 8 + ci) * 128:(grp * 8 + ci + 1) * 128,
                              ot * 128:(ot + 1) * 128],
                    in_=T["dq12"][:, 0:512].bitcast(BF16)[:, ci * 128:(ci + 1) * 128]),
                    reads=["dq12"], tag="wdst")

    p_d = kb.dtmp("p_d", [IFF, NTOK]).ap()
    p.op("v", lambda v: v.memset(T["mxp"][:], 0.0), writes=["mxp"])
    for ot in range(24):
        p.dma("y", lambda y, ot=ot: y.dma_start(out=T["swg"][:],
              in_=graw_d[ot * 128:(ot + 1) * 128, :]), writes=["swg"], tag="mpld")
        emit_gq_chain(kb, "swg", "sw1", "sw2", p_out=True)
        p.dma("g", lambda g, ot=ot: g.dma_start(
            out=p_d[ot * 128:(ot + 1) * 128, :], in_=T["sw2"][:]),
            reads=["sw2"], writes=[f"p_d{ot}"], tag="pst")
        p.op("v", lambda v: v.tensor_reduce(out=T["mx1"][:], in_=T["sw2"][:],
             axis=AX.X, op=ALU.max, apply_absolute_value=True),
             reads=["sw2"], writes=["mx1"], tag="mpmx", deferred=True)
        p.op("v", lambda v: v.tensor_tensor(out=T["mxp"][:], in0=T["mxp"][:],
             in1=T["mx1"][:], op=ALU.max), reads=["mxp", "mx1"], writes=["mxp"],
             tag="mpmxa")
    _gmax_collective(kb, ["mxp"], ["Mpg"], "p")
    _scal(kb, lambda v: v.tensor_tensor(out=T["c2"][:], in0=T["Mpg"][:],
          in1=T["sgg"][:], op=ALU.mult), ["Mpg", "sgg"], ["c2"], "a3a")
    _scal(kb, lambda v: v.tensor_scalar(out=T["c2"][:], in0=T["c2"][:],
          scalar1=1.0 / (65536.0 * QMAX), scalar2=1e-12, op0=ALU.mult,
          op1=ALU.max), ["c2"], ["c2"], "a3b")
    _scal(kb, lambda v: v.reciprocal(out=T["co"][:], in_=T["c2"][:]),
          ["c2"], ["co"], "a3c")
    _scal(kb, lambda v: v.tensor_tensor(out=T["co"][:], in0=T["co"][:],
          in1=T["sgg"][:], op=ALU.mult), ["co", "sgg"], ["co"], "a3d")
    _scal(kb, lambda v: v.tensor_scalar(out=T["co"][:], in0=T["co"][:],
          scalar1=1.0 / 65536.0, scalar2=None, op0=ALU.mult), ["co"], ["co"],
          "a3e")
    o_d = kb.dtmp("o_d", [IFF, NTOK]).ap()
    p.op("v", lambda v: v.memset(T["mxo"][:], 0.0), writes=["mxo"])
    for ot in range(24):
        p.dma("y", lambda y, ot=ot: y.dma_start(out=T["sw2"][:],
              in_=p_d[ot * 128:(ot + 1) * 128, :]),
              reads=[f"p_d{ot}"], writes=["sw2"], tag="mold")
        p.dma("y", lambda y, ot=ot: y.dma_start(out=T["swu"][:],
              in_=uraw_d[ot * 128:(ot + 1) * 128, :]), writes=["swu"], tag="mold2")
        emit_o_chain(kb, 4096, from_p=True)
        p.dma("g", lambda g, ot=ot: g.dma_start(
            out=o_d[ot * 128:(ot + 1) * 128, :], in_=T["sw2"][:]),
            reads=["sw2"], writes=[f"o_d{ot}"], tag="ost")
        p.op("v", lambda v: v.tensor_reduce(out=T["mx1"][:], in_=T["sw2"][:],
             axis=AX.X, op=ALU.max, apply_absolute_value=True),
             reads=["sw2"], writes=["mx1"], tag="momx", deferred=True)
        p.op("v", lambda v: v.tensor_tensor(out=T["mxo"][:], in0=T["mxo"][:],
             in1=T["mx1"][:], op=ALU.max), reads=["mxo", "mx1"], writes=["mxo"],
             tag="momxa")
    _gmax_collective(kb, ["mxo"], ["Mog"], "o")
    _scal(kb, lambda v: v.tensor_tensor(out=T["sgo"][:], in0=T["c2"][:],
          in1=T["sgu"][:], op=ALU.mult), ["c2", "sgu"], ["sgo"], "a4a")
    _scal(kb, lambda v: v.tensor_tensor(out=T["Mog"][:], in0=T["Mog"][:],
          in1=T["sgo"][:], op=ALU.mult), ["Mog", "sgo"], ["Mog"], "a4b")
    _scal(kb, lambda v: v.tensor_scalar(out=T["Mog"][:], in0=T["Mog"][:],
          scalar1=1.0 / QMAX, scalar2=1e-12, op0=ALU.mult, op1=ALU.max),
          ["Mog"], ["Mog"], "a4c")
    _scal(kb, lambda v: v.reciprocal(out=T["mx1"][:], in_=T["Mog"][:]),
          ["Mog"], ["mx1"], "a4d")
    _scal(kb, lambda v: v.tensor_tensor(out=T["sgo"][:], in0=T["sgo"][:],
          in1=T["mx1"][:], op=ALU.mult), ["sgo", "mx1"], ["sgo"], "a4e")
    kb.sb("co2", [128, 1], F32)
    _scal(kb, lambda v: v.tensor_tensor(out=T["co2"][:], in0=T["Mog"][:],
          in1=T["Mog"][:], op=ALU.mult), ["Mog"], ["co2"], "a4f")
    _scal(kb, lambda v: v.tensor_scalar(out=T["co2"][:], in0=T["co2"][:],
          scalar1=1.0 / D, scalar2=None, op0=ALU.mult), ["co2"], ["co2"], "a4g")

    emit_ffn_final(kb, x2_d, out_d, uraw_d, graw_d, wdT_d, o_d)
    kb.pop_scope()


def emit_gq_chain(kb, src, t1, t2, p_out, width=4096):
    p, T = kb.p, kb.T
    w = width
    p.op("s", lambda s: s.activation(out=T[t1][:, 0:w], in_=T[src][:, 0:w],
         func=ACT.Identity, scale=T["cg"][:, 0:1], bias=T["magicC"][:, 0:1]),
         reads=[src], writes=[t1], scalar_reads=["cg", "magicC"], tag="gq1")
    p.op("v", lambda v: v.tensor_scalar(out=T[t1][:, 0:w], in0=T[t1][:, 0:w],
         scalar1=MAGIC, scalar2=None, op0=ALU.subtract), reads=[t1],
         writes=[t1], tag="gq2")
    p.op("s", lambda s: s.activation(out=T[t2][:, 0:w], in_=T[t1][:, 0:w],
         func=ACT.Sigmoid, scale=T["sgg"][:, 0:1]), reads=[t1], writes=[t2],
         scalar_reads=["sgg"], tag="gq3")
    p.op("s", lambda s: s.activation(out=T[t2][:, 0:w], in_=T[t2][:, 0:w],
         func=ACT.Identity, scale=65536.0, bias=T["magicC"][:, 0:1]),
         reads=[t2], writes=[t2], scalar_reads=["magicC"], tag="gq4")
    p.op("v", lambda v: v.tensor_scalar(out=T[t2][:, 0:w], in0=T[t2][:, 0:w],
         scalar1=MAGIC, scalar2=65535.0, op0=ALU.subtract, op1=ALU.min),
         reads=[t2], writes=[t2], tag="gq5")
    if p_out:
        p.op("v", lambda v: v.tensor_tensor(out=T[t2][:, 0:w], in0=T[t1][:, 0:w],
             in1=T[t2][:, 0:w], op=ALU.mult), reads=[t1, t2], writes=[t2],
             tag="gq6")


def emit_o_chain(kb, width, from_p=False):
    p, T = kb.p, kb.T
    w = width
    if not from_p:
        emit_gq_chain(kb, "swg", "sw1", "sw2", p_out=True, width=w)
    p.op("s", lambda s: s.activation(out=T["sw2"][:, 0:w], in_=T["sw2"][:, 0:w],
         func=ACT.Identity, scale=T["co"][:, 0:1], bias=T["magicC"][:, 0:1]),
         reads=["sw2"], writes=["sw2"], scalar_reads=["co", "magicC"], tag="oc1")
    p.op("v", lambda v: v.tensor_scalar(out=T["sw2"][:, 0:w], in0=T["sw2"][:, 0:w],
         scalar1=MAGIC, scalar2=None, op0=ALU.subtract), reads=["sw2"],
         writes=["sw2"], tag="oc2")
    p.op("s", lambda s: s.activation(out=T["sw1"][:, 0:w], in_=T["swu"][:, 0:w],
         func=ACT.Identity, scale=T["cu"][:, 0:1], bias=T["magicC"][:, 0:1]),
         reads=["swu"], writes=["sw1"], scalar_reads=["cu", "magicC"], tag="oc3")
    p.op("v", lambda v: v.tensor_scalar(out=T["sw1"][:, 0:w], in0=T["sw1"][:, 0:w],
         scalar1=MAGIC, scalar2=None, op0=ALU.subtract), reads=["sw1"],
         writes=["sw1"], tag="oc4")
    p.op("v", lambda v: v.tensor_tensor(out=T["sw2"][:, 0:w], in0=T["sw2"][:, 0:w],
         in1=T["sw1"][:, 0:w], op=ALU.mult), reads=["sw1", "sw2"], writes=["sw2"],
         tag="oc5")


def emit_ffn_final(kb, x2_d, out_d, uraw_d, graw_d, wdT_d, o_d):
    p, T = kb.p, kb.T
    NSLAB = NTOK // 512
    for sl in range(NSLAB):
        for ic in range(24):
            p.dma("y", lambda y, ic=ic, sl=sl: y.dma_start(
                out=T["sw2"][:, 0:512],
                in_=o_d[ic * 128:(ic + 1) * 128, sl * 512:(sl + 1) * 512]),
                reads=[f"o_d{ic}"], writes=["sw2"], tag=f"f4g{sl}_{ic}")
            p.op("s", lambda s: s.activation(out=T["sw2"][:, 0:512],
                 in_=T["sw2"][:, 0:512], func=ACT.Identity,
                 scale=T["sgo"][:, 0:1], bias=T["magicC"][:, 0:1]),
                 reads=["sw2"], writes=["sw2"], scalar_reads=["sgo", "magicC"],
                 tag=f"f4r{sl}_{ic}")
            p.op("v", lambda v, ic=ic: v.tensor_scalar(
                 out=T["oq_sl"][:, ic * 512:(ic + 1) * 512], in0=T["sw2"][:, 0:512],
                 scalar1=MAGIC, scalar2=None, op0=ALU.subtract),
                 reads=["sw2"], writes=["oq_sl"], tag=f"f4q{sl}_{ic}")
        pmap = [("psB", 0), ("psC", 0), ("psC", 1024), ("psA", 0)]
        for ic in range(24):
            buf = "wd_c0" if ic % 2 == 0 else "wd_c1"
            p.dma("g", lambda g, ic=ic, buf=buf: g.dma_start(
                out=T[buf][:], in_=wdT_d[ic * 128:(ic + 1) * 128, :]),
                writes=[buf], tag=f"f4w{sl}_{ic}")
            def dmm(t, ic=ic, buf=buf):
                for tt in range(4):
                    ps, off = pmap[tt]
                    for q in range(2):
                        inst = t.matmul(
                            T[ps][:, off + q * 512:off + (q + 1) * 512],
                            T["oq_sl"][:, ic * 512 + tt * 128:ic * 512 + (tt + 1) * 128],
                            T[buf][:, q * 512:(q + 1) * 512],
                            start=(ic == 0), stop=(ic == 23))
                return inst
            p.op("t", dmm, reads=["oq_sl", buf], writes=["psA", "psB", "psC"],
                 tag=f"f4mm{sl}_{ic}")
        for tt in range(4):
            ps, off = pmap[tt]
            ti = sl * 4 + tt
            sq, r0 = divmod(ti * 128, S)
            p.dma("y", lambda y, sq=sq, r0=r0: y.dma_start(
                out=T["xr"][:], in_=x2_d[sq, r0:r0 + 128, :]), writes=["xr"],
                tag=f"f4x{sl}_{tt}")
            _ln_ops(kb, ps, lambda T, ps=ps, off=off: T[ps][:, off:off + 1024],
                    "of", lambda T: T["of"][:], "m2", "r2", "b2",
                    tag=f"ln4_{sl}_{tt}", scratch="scrF",
                    in_scale="Mog", in_scale_sq="co2")
            p.op("v", lambda v: v.tensor_tensor(out=T["of"][:], in0=T["of"][:],
                 in1=T["xr"][:], op=ALU.add), reads=["of", "xr"], writes=["of"],
                 tag=f"f4res{sl}_{tt}")
            p.dma("g", lambda g, sq=sq, r0=r0: g.dma_start(
                out=out_d[sq, r0:r0 + 128, :], in_=T["of"][:]), reads=["of"],
                tag=f"f4o{sl}_{tt}")


def build_full(kb):
    x2_d, x2ln_d = build_attn(kb)
    out_d = kb.dout("out", [NSEQ, S, D]).ap()
    if kb.stage in ("A", "A1", "B", "B1", "M", "M1", "S0", "S1", "SZ", "SA",
                    "SB", "Sg"):
        final_barrier(kb)
        return
    kb.barrier()
    uraw_d, graw_d = emit_ffn_head(kb, x2_d, x2ln_d)
    emit_ffn_f1(kb, uraw_d, graw_d)
    emit_ffn_rest(kb, x2_d, x2ln_d, out_d, uraw_d, graw_d)
    final_barrier(kb)


_CACHE = {}


def _build():
    if "nc" not in _CACHE:
        nc = bass.Bass("TRN2", target_bir_lowering=False, debug=False)
        kb = KB(nc, stage="full", dbg=[])
        build_full(kb)
        lower(kb)
        _CACHE["nc"] = nc
    return _CACHE["nc"]


def kernel(L0_attn_x, L0_attn_Wq, L0_attn_Wk, L0_attn_Wv, L0_attn_Wa,
           L0_attn_ba, L0_attn_Wb, L0_attn_bb, L0_attn_Wg, L0_attn_Wo,
           L0_attn_g_norm, L0_ffn_w_gate, L0_ffn_w_up, L0_ffn_w_down):
    nc = _build()
    cs = host_constants()
    x = np.ascontiguousarray(np.asarray(L0_attn_x, dtype=np.float32))
    common = {
        "Wq": np.ascontiguousarray(L0_attn_Wq, dtype=np.float32),
        "Wk": np.ascontiguousarray(L0_attn_Wk, dtype=np.float32),
        "Wv": np.ascontiguousarray(L0_attn_Wv, dtype=np.float32),
        "Wg": np.ascontiguousarray(L0_attn_Wg, dtype=np.float32),
        "Wo": np.ascontiguousarray(L0_attn_Wo, dtype=np.float32),
        "Wa": np.ascontiguousarray(L0_attn_Wa, dtype=np.float32),
        "Wb": np.ascontiguousarray(L0_attn_Wb, dtype=np.float32),
        "i_ba_bc": np.ascontiguousarray(
            np.broadcast_to(np.asarray(L0_attn_ba, np.float32), (128, NH))),
        "i_bb_bc": np.ascontiguousarray(
            np.broadcast_to(np.asarray(L0_attn_bb, np.float32), (128, NH))),
        "i_gnorm_bc": np.ascontiguousarray(
            np.broadcast_to(np.asarray(L0_attn_g_norm, np.float32), (128, D))),
        "w_gate_d": np.ascontiguousarray(L0_ffn_w_gate, dtype=np.float32),
        "w_up_d": np.ascontiguousarray(L0_ffn_w_up, dtype=np.float32),
        "w_down_d": np.ascontiguousarray(L0_ffn_w_down, dtype=np.float32),
    }
    for k, v in cs.items():
        if k != "c_indh":
            common["i_" + k] = v
    in_maps = [dict(common, x=np.ascontiguousarray(x[2 * c:2 * c + 2]))
               for c in range(8)]
    res = run_bass_kernel_spmd(nc, in_maps, core_ids=list(range(8)))
    out = np.concatenate(
        [np.asarray(res.results[c]["out"], dtype=np.float32).reshape(NSEQ, S, D)
         for c in range(8)], axis=0)
    return out


# version 05 (final)
# speedup vs baseline: 1.0855x; 1.0307x over previous
import numpy as np
from collections import defaultdict
from contextlib import ExitStack

import concourse.bass as bass
import concourse.mybir as mybir
from concourse.bass_utils import run_bass_kernel_spmd

ENGINES = ("t", "v", "s", "g", "y")


class Op:
    __slots__ = ("eng", "fn", "reads", "writes", "scalar_reads", "is_dma",
                 "idx", "dma_idx", "deps", "inc", "tag", "deferred", "is_cc")

    def __init__(self, eng, fn, reads, writes, scalar_reads, is_dma, tag,
                 deferred=False, is_cc=False):
        self.eng = eng
        self.fn = fn
        self.reads = reads
        self.writes = writes
        self.scalar_reads = scalar_reads
        self.is_dma = is_dma
        self.idx = -1
        self.dma_idx = -1
        self.deps = []
        self.inc = False
        self.tag = tag
        self.deferred = deferred
        self.is_cc = is_cc


class Prog:
    def __init__(self):
        self.small_res = set()
        self.ops = {e: [] for e in ENGINES}
        self.all_ops = []
        self.res = defaultdict(lambda: [None, []])
        self.ndma = {e: 0 for e in ENGINES}

    def op(self, eng, fn, reads=(), writes=(), scalar_reads=(), tag="",
           deferred=False, is_cc=False):
        if not deferred and (set(writes) & self.small_res):
            deferred = True
        o = Op(eng, fn, tuple(reads) + tuple(scalar_reads), tuple(writes),
               tuple(scalar_reads), False, tag, deferred, is_cc)
        self._record(o)
        return o

    def dma(self, eng, fn, reads=(), writes=(), tag=""):
        assert eng in ("g", "y", "s", "v")
        o = Op(eng, fn, tuple(reads), tuple(writes), (), True, tag)
        o.dma_idx = self.ndma[eng]
        self.ndma[eng] += 1
        self._record(o)
        return o

    def _record(self, o):
        o.idx = len(self.ops[o.eng])
        self.ops[o.eng].append(o)
        self.all_ops.append(o)
        deps = []
        for r in o.reads:
            w, _readers = self.res[r]
            if w is not None:
                deps.append(w)
            self.res[r][1].append(o)
        for w_ in o.writes:
            lw, readers = self.res[w_]
            if lw is not None:
                deps.append(lw)
            deps.extend(readers)
            self.res[w_][0] = o
            self.res[w_][1] = []
        seen = set()
        for d in deps:
            if d is o:
                continue
            if id(d) in seen:
                continue
            seen.add(id(d))
            if d.eng == o.eng and not d.is_dma and not d.is_cc:
                continue
            o.deps.append(d)

    def finalize(self, nc, engines, sems, dma_sems, verbose=False):
        for o in self.all_ops:
            if o.is_cc:
                o.inc = True
            for d in o.deps:
                if not d.is_dma:
                    d.inc = True
        inc_count = {}
        for e in ENGINES:
            c = 0
            for o in self.ops[e]:
                if o.inc and not o.is_dma:
                    c += 1
                inc_count[id(o)] = c
        plans = {e: [] for e in ENGINES}
        for e in ENGINES:
            last_wait = defaultdict(int)
            pend_w = set()
            pend_r = set()
            for o in self.ops[e]:
                waits = []
                for d in o.deps:
                    if d.is_dma:
                        key = ("d", d.eng, d.dma_idx % 8)
                        val = 16 * (d.dma_idx // 8 + 1)
                    else:
                        key = ("c", d.eng)
                        val = inc_count[id(d)]
                        assert val > 0
                    if val > last_wait[key]:
                        last_wait[key] = val
                        waits.append((key, val))
                rd, wr = set(o.reads), set(o.writes)
                need_gap = bool((rd & pend_w) or (wr & pend_w) or (wr & pend_r))
                if need_gap:
                    pend_w.clear()
                    pend_r.clear()
                pend_w.update(wr)
                pend_r.update(rd)
                plans[e].append((o, waits, need_gap))
        self._plans = plans
        self._inc_count = inc_count
        return plans

    def emit(self, e, eng_obj, sems, dma_sems):
        for o, waits, need_gap in self._plans[e]:
            for key, val in waits:
                if key[0] == "d":
                    eng_obj.wait_ge(dma_sems[key[1]][key[2]], val)
                else:
                    eng_obj.wait_ge(sems[key[1]], val)
            if need_gap:
                eng_obj.drain()
            if o.is_dma:
                slot = o.dma_idx % 8
                rnd = o.dma_idx // 8
                if rnd > 0:
                    eng_obj.wait_ge(dma_sems[e][slot], 16 * rnd)
                inst = o.fn(eng_obj)
                inst.then_inc(dma_sems[e][slot], 16)
            else:
                inst = o.fn(eng_obj)
                if o.inc:
                    if inst is None:
                        eng_obj.nop().then_inc(sems[e], 1)
                    else:
                        inst.then_inc(sems[e], 1)

    def stats(self):
        return {e: (len(self.ops[e]), sum(1 for o in self.ops[e] if o.inc),
                    self.ndma[e]) for e in ENGINES}


F32 = mybir.dt.float32
BF16 = mybir.dt.bfloat16
I16 = mybir.dt.int16
ALU = mybir.AluOpType
ACT = mybir.ActivationFunctionType
AX = mybir.AxisListType

D = 1024
NH = 16
HD = 64
C = 64
S = 2048
NSTEP = S // C
NSEQ = 2
IFF = 3072
NTOK = NSEQ * S
MAGIC = float(1.5 * 2 ** 23)
BIGNEG = -1.0e30
QMAX = 32767.0
EPS = 1e-5

def host_constants():
    cs = {}
    cs["c_id128f"] = np.eye(128, dtype=np.float32)
    cs["c_id128b"] = np.eye(128, dtype=np.float32)
    triu = np.triu(np.ones((C, C), np.float32))
    z = np.zeros((C, C), np.float32)
    cs["c_tril2"] = np.block([[triu, z], [z, triu]])
    sel = np.zeros((128, 128), np.float32)
    sel[C - 1, 0:C] = 1.0
    sel[127, C:128] = 1.0
    cs["c_selC"] = sel
    q = np.arange(128)
    s_in = q % C
    par_row = (q // C)
    t_idx = np.arange(C)
    dind = (s_in[:, None] == t_idx[None, :]).astype(np.float32)
    blkN = np.zeros((128, 2 * C), np.float32)
    blkN[:C, :C] = dind[:C]
    blkN[C:, C:] = dind[C:]
    cs["c_dindN"] = np.tile(blkN, (1, NH))
    cs["c_dindW"] = np.tile(dind, (1, NH))
    okN = np.zeros((128, 2 * C), np.float32) + BIGNEG
    okN[:C, :C] = np.where(s_in[:C, None] < t_idx[None, :], 0.0, BIGNEG)
    okN[C:, C:] = np.where(s_in[C:, None] < t_idx[None, :], 0.0, BIGNEG)
    cs["c_negN"] = np.tile(okN, (1, NH))
    okW = np.where(s_in[:, None] <= t_idx[None, :], 0.0, BIGNEG)
    cs["c_negW"] = np.tile(okW, (1, NH))
    ind = np.zeros((16, 8 * 128), np.float32)
    for ci in range(8):
        ind[2 * ci, ci * 128: ci * 128 + 64] = 1.0
        ind[2 * ci + 1, ci * 128 + 64: ci * 128 + 128] = 1.0
    cs["c_indh"] = ind
    cs["c_ones2"] = np.block([[np.ones((C, C), np.float32), z],
                              [z, np.ones((C, C), np.float32)]])
    okT = np.zeros((128, 2 * C), np.float32) + BIGNEG
    okT[:C, :C] = np.where(t_idx[None, :] < s_in[:C, None], 0.0, BIGNEG)
    okT[C:, C:] = np.where(t_idx[None, :] < s_in[C:, None], 0.0, BIGNEG)
    cs["c_negNT"] = np.tile(okT, (1, NH))
    i1 = np.zeros((16, 1024), np.float32)
    for ci in range(8):
        for pp in range(128):
            i1[2 * ci + pp // 64, ci * 128 + pp] = 1.0
    cs["c_indh1"] = i1
    s4 = np.zeros((128, 512), np.float32)
    for e in range(2):
        for sq in range(2):
            for pp in range(128):
                if pp // 64 == e:
                    s4[63 + 64 * sq, (e * 2 + sq) * 128 + pp] = 1.0
    cs["c_sel4"] = s4
    return cs


class KB:

    def __init__(self, nc, stage="full", dbg=()):
        self.nc = nc
        self.p = Prog()
        self.stage = stage
        self.dbg = set(dbg)
        self.es = ExitStack()
        self.scope = None
        self.T = {}
        self.dram = {}
        self.dbg_outs = {}

    def push_scope(self):
        self.scope = ExitStack()

    def pop_scope(self):
        self.scope.close()
        self.scope = None

    def barrier(self):
        p = self.p
        lasts = {}
        for e in ("t", "v", "s", "g", "y"):
            real = [op for op in p.ops[e] if not op.is_dma]
            lasts[e] = real[-1] if real else None
        dlasts = {}
        for e in ("t", "v", "s", "g", "y"):
            per_slot = {}
            for op in p.ops[e]:
                if op.is_dma:
                    per_slot[op.dma_idx % 8] = op
            dlasts[e] = list(per_slot.values())
        for e in ("t", "v", "s", "g", "y"):
            o = p.op(e, lambda eng: eng.nop(), tag="barr")
            for x in ("t", "v", "s", "g", "y"):
                if x != e and lasts[x] is not None:
                    o.deps.append(lasts[x])
                o.deps.extend(dlasts[x])

    def sb(self, name, shape, dt):
        es = self.scope if self.scope is not None else self.es
        self.T[name] = es.enter_context(self.nc.sbuf_tensor(name, shape, dt))
        import numpy as _np
        if int(_np.prod(shape[1:])) <= 64:
            self.p.small_res.add(name)
        return self.T[name]

    def ps(self, name, shape):
        self.T[name] = self.es.enter_context(self.nc.psum_tensor(name, shape, F32))
        return self.T[name]

    def din(self, name, shape, dt=F32):
        self.dram[name] = self.nc.dram_tensor(name, shape, dt, kind="ExternalInput")
        return self.dram[name]

    def dout(self, name, shape, dt=F32):
        self.dram[name] = self.nc.dram_tensor(name, shape, dt, kind="ExternalOutput")
        return self.dram[name]

    def dtmp(self, name, shape, dt=F32):
        self.dram[name] = self.nc.dram_tensor(name, shape, dt)
        return self.dram[name]

    def dbg_out(self, key, shape, dt=F32):
        if key in self.dbg:
            self.dbg_outs[key] = self.dout("dbg_" + key, shape, dt)
            return self.dram["dbg_" + key].ap()
        return None

    def dump(self, key, tile_ap, reads):
        if key in self.dbg_outs:
            d = self.dram["dbg_" + key].ap()
            self.p.dma("g", lambda g, d=d, t=tile_ap: g.dma_start(out=d, in_=t),
                       reads=reads, tag="dbg_" + key)

    def op(self, *a, **k):
        return self.p.op(*a, **k)

    def dma(self, *a, **k):
        return self.p.dma(*a, **k)


def build_attn(kb, ncores_note=""):
    nc, p, T = kb.nc, kb.p, kb.T
    x_d = kb.din("x", [NSEQ, S, D]).ap()
    Wnames = ["Wq", "Wk", "Wv", "Wg", "Wo"]
    for w in Wnames:
        kb.din(w, [D, D])
    kb.din("Wa", [D, NH]); kb.din("Wb", [D, NH])
    kb.din("i_ba_bc", [128, NH]); kb.din("i_bb_bc", [128, NH])
    kb.din("i_gnorm_bc", [128, D])
    for cname, cshape, cdt in [
        ("c_id128f", [128, 128], F32), ("c_id128b", [128, 128], F32),
        ("c_tril2", [128, 128], F32), ("c_selC", [128, 128], F32),
        ("c_dindN", [128, 2048], F32), ("c_dindW", [128, 1024], F32),
        ("c_negN", [128, 2048], F32), ("c_negW", [128, 1024], F32),
        ("c_indh1", [16, 1024], F32), ("c_ones2", [128, 128], F32),
        ("c_negNT", [128, 2048], F32), ("c_sel4", [128, 512], F32),
    ]:
        kb.din("i_" + cname, cshape, cdt)
    x2_d = kb.dtmp("x2_d", [NSEQ, S, D]).ap()
    x2ln_d = kb.dtmp("x2ln_d", [NSEQ, S, D]).ap()

    for cn in ["id128f"]:
        pass
    kb.sb("c_id128f", [128, 128], F32)
    kb.sb("c_id128b", [128, 128], BF16)
    for nm in ["sum1", "mn", "ssq", "sd", "rstd", "bias2", "m2", "r2", "b2",
               "m3", "r3", "b3", "mx1", "eps1", "magicC"]:
        kb.sb(nm, [128, 1], F32)
    kb.sb("mxacc", [128, 1], F32)
    kb.push_scope()
    for w in Wnames:
        kb.sb("w_" + w, [128, 8 * 1024], BF16)
    kb.sb("w_ab", [128, 8 * 32], BF16)
    kb.sb("ba_bc", [128, NH], F32); kb.sb("bb_bc", [128, NH], F32)
    kb.sb("gnorm_bc", [128, D], F32)
    consts_b = [("dindN", BF16), ("dindW", BF16),
                ("negN", BF16), ("negW", BF16), ("ones2", BF16), ("negNT", BF16)]
    for cn, cdt in consts_b:
        kb.sb("c_" + cn, [128, {"dindN": 2048, "negN": 2048, "negNT": 2048,
                                "dindW": 1024, "negW": 1024}.get(cn, 128)], cdt)
    for cn in ["tril2", "selC"]:
        kb.sb("c_" + cn, [128, 128], F32)
    kb.sb("c_indh1", [16, 1024], F32)
    kb.sb("c_sel4", [128, 512], F32)
    kb.sb("scT3", [16, 128], F32)
    kb.sb("dN2", [128, 2048], BF16)
    kb.sb("x2ln_c", [128, D], F32)
    kb.sb("e_lC_st", [128, 16], F32)

    kb.sb("xc", [128, D], F32)
    kb.sb("xln_tm", [128, D], BF16)
    kb.sb("xln_fm", [128, D], BF16)
    kb.sb("q_fm", [128, D], BF16); kb.sb("k_fm", [128, D], BF16)
    kb.sb("qh_fm", [128, D], BF16)
    kb.sb("q_tm", [128, D], BF16); kb.sb("k_tm", [128, D], BF16)
    kb.sb("v_tm", [128, D], BF16)
    kb.sb("g_tm", [128, D], F32)
    kb.sb("R", [128, 2048], BF16)
    kb.sb("kt_tm", [128, D], BF16)
    kb.sb("scrD", [128, D], F32)
    kb.sb("scrD2", [128, D], F32)
    kb.sb("dN", [128, 2048], BF16)
    kb.sb("nbar", [128, 2048], BF16)
    kb.sb("pbar", [128, 2048], BF16)
    kb.sb("dW", [128, 1024], BF16)
    kb.sb("wbar", [128, 1024], BF16)
    kb.sb("Xc", [128, 2048], BF16)
    kb.sb("xkT", [128, 1024], BF16)
    kb.sb("gt", [128, 1024], BF16)
    kb.sb("y_tm", [128, D], F32)
    kb.sb("yf_fm", [128, D], BF16)
    kb.sb("negS", [128, 1024], F32)
    kb.sb("negS_b", [128, 1024], BF16)
    for nm in ["zat", "zbt", "alpha", "beta", "loga", "logb", "ell", "ellprev",
               "glt", "ellC", "e_l", "e_lp", "e_ct", "e_lC", "d1",
               "lhi", "llo", "nlhi", "nllo", "ghi", "glo", "negell",
               "ssq_q", "ssq_k", "nq", "nk", "rn_q", "rn_k",
               "skh", "skh2", "sct", "sq2", "sq2n", "lnrnq", "lnrnk",
               "biasWN", "nlt", "wlt", "whi", "wlo", "bhi", "blo"]:
        kb.sb(nm, [128, NH], F32)
    kb.sb("hilo16", [128, NH], BF16)
    for nm in ["scTq", "scTqh", "scTk"]:
        kb.sb(nm, [16, 128], F32)
    kb.sb("sc3", [128, 64], F32)
    kb.ps("psA", [128, 1024])
    kb.ps("psB", [128, 1024])
    kb.ps("psC", [128, 2048])
    psum_init(kb)

    for w in Wnames:
        wd = kb.dram[w].ap().rearrange("(i p) o -> p i o", p=128)
        p.dma("g", lambda g, o=T["w_" + w], i=wd: g.dma_start(out=o[:].rearrange(
            "p (i o) -> p i o", i=8), in_=i), writes=["w_" + w], tag="ldw")
    wa = kb.dram["Wa"].ap().rearrange("(i p) o -> p i o", p=128)
    wb = kb.dram["Wb"].ap().rearrange("(i p) o -> p i o", p=128)
    p.dma("g", lambda g, i=wa: g.dma_start(
        out=T["w_ab"][:].rearrange("p (i o) -> p i o", i=8)[:, :, 0:16], in_=i),
        writes=["w_ab"], tag="ldab")
    p.dma("g", lambda g, i=wb: g.dma_start(
        out=T["w_ab"][:].rearrange("p (i o) -> p i o", i=8)[:, :, 16:32], in_=i),
        writes=["w_ab"], tag="ldab")
    for nm in ["ba_bc", "bb_bc", "gnorm_bc"]:
        p.dma("y", lambda y, o=T[nm], i=kb.dram["i_" + nm].ap(): y.dma_start(out=o[:], in_=i),
              writes=[nm], tag="ldc")
    for cn, cdt in consts_b + [("id128b", BF16)]:
        p.dma("g", lambda g, o=T["c_" + cn], i=kb.dram["i_c_" + cn].ap():
              g.dma_start(out=o[:], in_=i), writes=["c_" + cn], tag="ldc")
    for cn in ["id128f", "tril2", "selC", "indh1", "sel4"]:
        p.dma("y", lambda y, o=T["c_" + cn], i=kb.dram["i_c_" + cn].ap():
              y.dma_start(out=o[:], in_=i), writes=["c_" + cn], tag="ldc")
    p.op("v", lambda v: v.memset(T["eps1"][:], EPS), writes=["eps1"])
    p.op("v", lambda v: v.memset(T["magicC"][:], MAGIC), writes=["magicC"])
    p.op("v", lambda v: v.memset(T["negS"][:], 0.0), writes=["negS"])
    p.op("v", lambda v: v.memset(T["negS_b"][:], 0.0), writes=["negS_b"])
    p.op("v", lambda v: v.memset(T["mxacc"][:], 0.0), writes=["mxacc"])

    for key, shp, dt in [("qh_fm", [128, D], F32), ("zat", [128, NH], F32),
                         ("scT3", [16, 128], F32), ("sq2n", [128, NH], F32),
                         ("qfm", [128, D], F32),
                         ("alpha", [128, NH], F32), ("loga", [128, NH], F32),
                         ("xlntm", [128, D], F32), ("xlnfm", [128, D], F32),
                         ("mn", [128, 1], F32), ("sum1", [128, 1], F32),
                         ("rstd", [128, 1], F32),
                         ("ssqd", [128, 1], F32), ("xcc", [128, D], F32),
                         ("R", [128, 2048], F32), ("ell", [128, NH], F32),
                         ("glt", [128, NH], F32), ("kt", [128, D], F32),
                         ("nbar", [128, 2048], F32), ("wbar", [128, 1024], F32),
                         ("Xs", [128, 2048], F32), ("gt", [128, 1024], F32),
                         ("y0", [128, D], F32), ("y1", [128, D], F32),
                         ("x2_0", [128, D], F32), ("x2ln_0", [128, D], F32)]:
        kb.dbg_out(key, shp, dt)

    nsteps = 1 if kb.stage in ("A1", "B1", "M1", "S1", "S0", "Sg", "SZ", "SA", "SB") else NSTEP
    for step in range(nsteps):
        emit_attn_step(kb, step, x_d, x2_d, x2ln_d)
        emit_attn_step2(kb, step)
        if kb.stage in ("A",):
            continue
        emit_attn_step3(kb, step)
        if kb.stage in ("M1", "M"):
            continue
        emit_attn_step4(kb, step, x2_d, x2ln_d)
    kb.pop_scope()
    return x2_d, x2ln_d


def _ln_ops(kb, src_res, src_ap_fn, out_res, out_ap, mn_t, r_t, b_t,
            out_dt_note="", src_is_psum=False, tag="ln", scratch="scrD2",
            in_scale=None, in_scale_sq=None):
    p, T = kb.p, kb.T
    sumr = ["sum1"]
    p.op("v", lambda v: v.tensor_reduce(out=T["sum1"][:], in_=src_ap_fn(T),
         axis=AX.X, op=ALU.add), reads=[src_res], writes=["sum1"],
         tag=tag + "_sum", deferred=True)
    p.op("v", lambda v: v.tensor_scalar(out=T[mn_t][:], in0=T["sum1"][:],
         scalar1=-1.0 / D, scalar2=None, op0=ALU.mult),
         reads=["sum1"], writes=[mn_t], tag=tag + "_mn")
    p.op("s", lambda s: s.activation(out=T[scratch][:], in_=src_ap_fn(T),
         func=ACT.Square, bias=T[mn_t][:, 0:1], scale=1.0, accum_out=T["ssq"][:]),
         reads=[src_res], writes=[scratch, "ssq"], scalar_reads=[mn_t],
         tag=tag + "_sq", deferred=True)
    if in_scale is None:
        p.op("s", lambda s: s.activation(out=T["sd"][:], in_=T["ssq"][:],
             func=ACT.Sqrt, bias=T["eps1"][:, 0:1], scale=1.0 / D),
             reads=["ssq", "eps1"], writes=["sd"], tag=tag + "_sd")
    else:
        p.op("s", lambda s: s.activation(out=T["sd"][:], in_=T["ssq"][:],
             func=ACT.Sqrt, bias=T["eps1"][:, 0:1],
             scale=T[in_scale_sq][:, 0:1]),
             reads=["ssq", "eps1"], writes=["sd"],
             scalar_reads=[in_scale_sq], tag=tag + "_sd")
    p.op("v", lambda v: v.reciprocal(out=T[r_t][:], in_=T["sd"][:]),
         reads=["sd"], writes=[r_t], tag=tag + "_r", deferred=True)
    if in_scale is not None:
        p.op("v", lambda v: v.tensor_tensor(out=T[r_t][:], in0=T[r_t][:],
             in1=T[in_scale][:], op=ALU.mult), reads=[r_t, in_scale],
             writes=[r_t], tag=tag + "_rs", deferred=True)
    p.op("v", lambda v: v.tensor_tensor(out=T[b_t][:], in0=T[mn_t][:],
         in1=T[r_t][:], op=ALU.mult), reads=[mn_t, r_t], writes=[b_t],
         tag=tag + "_b", deferred=True)
    p.op("s", lambda s: s.activation(out=out_ap(T), in_=src_ap_fn(T),
         func=ACT.Identity, scale=T[r_t][:, 0:1], bias=T[b_t][:, 0:1]),
         reads=[src_res], writes=[out_res], scalar_reads=[r_t, b_t],
         tag=tag + "_app")


def emit_attn_step(kb, step, x_d, x2_d, x2ln_d):
    p, T = kb.p, kb.T
    r0, r1 = step * C, (step + 1) * C
    dbg = (step == 0)

    for sq in range(NSEQ):
        p.dma("y", lambda y, sq=sq: y.dma_start(
            out=T["xc"][sq * C:(sq + 1) * C, :], in_=x_d[sq, r0:r1, :]),
            writes=["xc"], tag=f"ldx{step}")

    _ln_ops(kb, "xc", lambda T: T["xc"][:], "xln_tm",
            lambda T: T["xln_tm"][:], "mn", "rstd", "bias2", tag=f"ln1_{step}")

    def tr8(t, src="xln_tm"):
        pb = T["psA"][:].bitcast(BF16)
        for ci in range(8):
            inst = t.transpose(pb[:, ci * 128:(ci + 1) * 128],
                               T[src][:, ci * 128:(ci + 1) * 128], T["c_id128b"][:])
        return inst
    p.op("t", lambda t: tr8(t), reads=["xln_tm", "c_id128b"],
         writes=["psA"], tag=f"trx{step}")
    p.op("v", lambda v: v.tensor_copy(out=T["xln_fm"][:],
         in_=T["psA"][:].bitcast(BF16)[:, 0:1024]),
         reads=["psA"], writes=["xln_fm"], tag=f"trxe{step}")

    def proj_fm(t, wname):
        for j in range(8):
            for i in range(8):
                inst = t.matmul(T["psB"][:, j * 128:(j + 1) * 128],
                                T["w_" + wname][:, i * 1024 + j * 128:i * 1024 + (j + 1) * 128],
                                T["xln_fm"][:, i * 128:(i + 1) * 128],
                                start=(i == 0), stop=(i == 7))
        return inst
    for wname, dst in [("Wq", "q_fm"), ("Wk", "k_fm")]:
        p.op("t", lambda t, w=wname: proj_fm(t, w), reads=["w_" + wname, "xln_fm"],
             writes=["psB"], tag=f"pfm{wname}{step}")
        p.op("s", lambda s, d=dst: s.copy(out=T[d][:], in_=T["psB"][:]),
             reads=["psB"], writes=[dst], tag=f"pfme{wname}{step}")

    def proj_tm(t, wname):
        for half in range(2):
            for i in range(8):
                inst = t.matmul(T["psB"][:, half * 512:(half + 1) * 512],
                                T["xln_fm"][:, i * 128:(i + 1) * 128],
                                T["w_" + wname][:, i * 1024 + half * 512:
                                                i * 1024 + (half + 1) * 512],
                                start=(i == 0), stop=(i == 7))
        return inst
    for wname, dst in [("Wq", "q_tm"), ("Wk", "k_tm"), ("Wv", "v_tm")]:
        p.op("t", lambda t, w=wname: proj_tm(t, w), reads=["w_" + wname, "xln_fm"],
             writes=["psB"], tag=f"ptm{wname}{step}")
        p.op("v", lambda v, d=dst: v.tensor_copy(out=T[d][:], in_=T["psB"][:]),
             reads=["psB"], writes=[dst], tag=f"ptme{wname}{step}")
    p.op("t", lambda t: proj_tm(t, "Wg"), reads=["w_Wg", "xln_fm"],
         writes=["psB"], tag=f"ptmWg{step}")
    p.op("s", lambda s: s.activation(out=T["g_tm"][:], in_=T["psB"][:],
         func=ACT.Silu), reads=["psB"], writes=["g_tm"], tag=f"silu{step}")

    def logits(t):
        for i in range(8):
            inst = t.matmul(T["psA"][:, 896:928], T["xln_fm"][:, i * 128:(i + 1) * 128],
                            T["w_ab"][:, i * 32:(i + 1) * 32],
                            start=(i == 0), stop=(i == 7))
        return inst
    p.op("t", logits, reads=["w_ab", "xln_fm"], writes=["psA"], tag=f"lg{step}")
    p.op("v", lambda v: v.tensor_tensor(out=T["zat"][:], in0=T["psA"][:, 896:912],
         in1=T["ba_bc"][:], op=ALU.add), reads=["psA", "ba_bc"], writes=["zat"])
    p.op("v", lambda v: v.tensor_tensor(out=T["zbt"][:], in0=T["psA"][:, 912:928],
         in1=T["bb_bc"][:], op=ALU.add), reads=["psA", "bb_bc"], writes=["zbt"])
    p.op("s", lambda s: s.activation(out=T["alpha"][:], in_=T["zat"][:],
         func=ACT.Sigmoid), reads=["zat"], writes=["alpha"])
    p.op("s", lambda s: s.activation(out=T["beta"][:], in_=T["zbt"][:],
         func=ACT.Sigmoid), reads=["zbt"], writes=["beta"])
    p.op("s", lambda s: s.activation(out=T["loga"][:], in_=T["alpha"][:],
         func=ACT.Ln), reads=["alpha"], writes=["loga"])
    p.op("s", lambda s: s.activation(out=T["logb"][:], in_=T["beta"][:],
         func=ACT.Ln), reads=["beta"], writes=["logb"])

    p.op("t", lambda t: t.matmul(T["psA"][:, 928:944], T["c_tril2"][:],
         T["loga"][:], start=True, stop=True), reads=["c_tril2", "loga"],
         writes=["psA"], tag=f"cum{step}")
    p.op("v", lambda v: v.tensor_copy(out=T["ell"][:], in_=T["psA"][:, 928:944]),
         reads=["psA"], writes=["ell"])
    p.op("t", lambda t: t.matmul(T["psA"][:, 928:944], T["c_selC"][:],
         T["ell"][:], start=True, stop=True), reads=["c_selC", "ell"],
         writes=["psA"], tag=f"selC{step}")
    p.op("v", lambda v: v.tensor_copy(out=T["ellC"][:], in_=T["psA"][:, 928:944]),
         reads=["psA"], writes=["ellC"])
    p.op("v", lambda v: v.tensor_tensor(out=T["ellprev"][:], in0=T["ell"][:],
         in1=T["loga"][:], op=ALU.subtract), reads=["ell", "loga"], writes=["ellprev"])
    p.op("v", lambda v: v.tensor_tensor(out=T["glt"][:], in0=T["ellprev"][:],
         in1=T["logb"][:], op=ALU.add), reads=["ellprev", "logb"], writes=["glt"])
    p.op("v", lambda v: v.tensor_tensor(out=T["d1"][:], in0=T["ellC"][:],
         in1=T["ell"][:], op=ALU.subtract), reads=["ellC", "ell"], writes=["d1"])
    for src, dst in [("ell", "e_l"), ("ellprev", "e_lp"), ("d1", "e_ct"),
                     ("ellC", "e_lC")]:
        p.op("s", lambda s, a=src, b=dst: s.activation(out=T[b][:], in_=T[a][:],
             func=ACT.Exp), reads=[src], writes=[dst])
    def selmm(t):
        for e in range(2):
            for sq in range(2):
                j = e * 2 + sq
                inst = t.matmul(T["psA"][:, 944 + sq * 8:944 + sq * 8 + 8],
                                T["c_sel4"][:, j * 128:(j + 1) * 128],
                                T["ell"][:, e:16:2],
                                start=(e == 0 and sq == 0), stop=(e == 1 and sq == 1))
        return inst
    p.op("t", selmm, reads=["c_sel4", "ell"], writes=["psA"], tag=f"selmm{step}")
    p.op("s", lambda s: s.activation(out=T["e_lC_st"][:], in_=T["psA"][:, 944:960],
         func=ACT.Exp), reads=["psA"], writes=["e_lC_st"], tag=f"elcst{step}")
    if dbg:
        kb.dump("ell", T["ell"][:], ["ell"])
        kb.dump("glt", T["glt"][:], ["glt"])
        kb.dump("zat", T["zat"][:], ["zat"])
        kb.dump("alpha", T["alpha"][:], ["alpha"])
        kb.dump("loga", T["loga"][:], ["loga"])
        kb.dump("xlntm", T["xln_tm"][:], ["xln_tm"])
        kb.dump("mn", T["mn"][:], ["mn"])
        kb.dump("sum1", T["sum1"][:], ["sum1"])
        kb.dump("rstd", T["rstd"][:], ["rstd"])
        kb.dump("ssqd", T["ssq"][:], ["ssq"])
        kb.dump("xcc", T["xc"][:], ["xc"])
        kb.dump("xlnfm", T["xln_fm"][:], ["xln_fm"])


def emit_attn_step2(kb, step):
    p, T = kb.p, kb.T
    dbg = (step == 0)
    for src, ssq in [("q_tm", "ssq_q"), ("k_tm", "ssq_k")]:
        p.op("s", lambda s, a=src: s.activation(out=T["scrD"][:], in_=T[a][:],
             func=ACT.Square), reads=[src], writes=["scrD"], tag=f"l2sq{step}")
        p.op("v", lambda v, o=ssq: v.tensor_reduce(
             out=T[o][:], in_=T["scrD"][:].rearrange("p (h e) -> p h e", h=NH),
             axis=AX.X, op=ALU.add), reads=["scrD"], writes=[ssq],
             tag=f"l2r{step}", deferred=True)
    for ssq, n, rn in [("ssq_q", "nq", "rn_q"), ("ssq_k", "nk", "rn_k")]:
        p.op("s", lambda s, a=ssq, b=n: s.activation(out=T[b][:], in_=T[a][:],
             func=ACT.Sqrt), reads=[ssq], writes=[n])
        p.op("v", lambda v, b=n: v.tensor_scalar_max(out=T[b][:], in0=T[b][:],
             scalar1=1e-12), reads=[n], writes=[n], deferred=True)
        p.op("v", lambda v, b=n, c=rn: v.reciprocal(out=T[c][:], in_=T[b][:]),
             reads=[n], writes=[rn], deferred=True)
    p.op("s", lambda s: s.activation(out=T["lnrnq"][:], in_=T["nq"][:],
         func=ACT.Ln), reads=["nq"], writes=["lnrnq"])
    p.op("s", lambda s: s.activation(out=T["lnrnk"][:], in_=T["nk"][:],
         func=ACT.Ln), reads=["nk"], writes=["lnrnk"])
    p.op("v", lambda v: v.tensor_scalar(out=T["lnrnq"][:], in0=T["lnrnq"][:],
         scalar1=-1.0, scalar2=None, op0=ALU.mult), reads=["lnrnq"], writes=["lnrnq"])
    p.op("v", lambda v: v.tensor_scalar(out=T["lnrnk"][:], in0=T["lnrnk"][:],
         scalar1=-1.0, scalar2=None, op0=ALU.mult), reads=["lnrnk"], writes=["lnrnk"])
    p.op("v", lambda v: v.tensor_tensor(out=T["biasWN"][:], in0=T["lnrnk"][:],
         in1=T["ell"][:], op=ALU.subtract), reads=["lnrnk", "ell"], writes=["biasWN"])
    p.op("v", lambda v: v.tensor_tensor(out=T["nlt"][:], in0=T["glt"][:],
         in1=T["lnrnk"][:], op=ALU.add), reads=["glt", "lnrnk"], writes=["nlt"])
    p.op("v", lambda v: v.tensor_tensor(out=T["wlt"][:], in0=T["ell"][:],
         in1=T["lnrnq"][:], op=ALU.add), reads=["ell", "lnrnq"], writes=["wlt"])
    for srcn, hi, lo in [("wlt", "whi", "wlo"), ("nlt", "ghi", "glo"),
                         ("biasWN", "bhi", "blo")]:
        p.op("v", lambda v, a=srcn: v.tensor_copy(out=T["hilo16"][:], in_=T[a][:]),
             reads=[srcn], writes=["hilo16"], tag="hs")
        p.op("v", lambda v, h=hi: v.tensor_copy(out=T[h][:], in_=T["hilo16"][:]),
             reads=["hilo16"], writes=[hi], tag="hs2")
        p.op("v", lambda v, a=srcn, h=hi, l=lo: v.tensor_tensor(out=T[l][:],
             in0=T[a][:], in1=T[h][:], op=ALU.subtract), reads=[srcn, hi],
             writes=[lo], tag="hs3")
    p.op("v", lambda v: v.tensor_tensor(out=T["sq2"][:], in0=T["rn_q"][:],
         in1=T["e_l"][:], op=ALU.mult), reads=["rn_q", "e_l"], writes=["sq2"])
    p.op("v", lambda v: v.tensor_scalar(out=T["sq2n"][:], in0=T["sq2"][:],
         scalar1=-1.0, scalar2=None, op0=ALU.mult), reads=["sq2"], writes=["sq2n"])
    p.op("v", lambda v: v.memset(T["sc3"][:, 16:64], 0.0), writes=["sc3"])
    p.op("v", lambda v: v.tensor_copy(out=T["sc3"][:, 0:16], in_=T["sq2n"][:]),
         reads=["sq2n"], writes=["sc3"])
    p.op("t", lambda t: t.transpose(T["psA"][0:64, 896:1024], T["sc3"][:],
         T["c_id128f"][:]), reads=["sc3", "c_id128f"], writes=["psA"],
         tag=f"trsc{step}")
    p.op("v", lambda v: v.tensor_copy(out=T["scT3"][:], in_=T["psA"][0:16, 896:1024]),
         reads=["psA"], writes=["scT3"], tag=f"trsce{step}")
    def indmm(t):
        for ci in range(8):
            inst = t.matmul(T["psA"][:, ci * 128:(ci + 1) * 128],
                            T["c_indh1"][:, ci * 128:(ci + 1) * 128],
                            T["scT3"][:], start=True, stop=True)
        return inst
    p.op("t", indmm, reads=["c_indh1", "scT3"], writes=["psA"], tag=f"indmm{step}")
    p.op("v", lambda v: v.tensor_tensor(out=T["qh_fm"][:], in0=T["q_fm"][:],
         in1=T["psA"][:], op=ALU.mult), reads=["q_fm", "psA"], writes=["qh_fm"],
         tag=f"scap{step}")
    p.op("v", lambda v: v.tensor_tensor(out=T["skh"][:], in0=T["beta"][:],
         in1=T["e_lp"][:], op=ALU.mult), reads=["beta", "e_lp"], writes=["skh"])
    p.op("v", lambda v: v.tensor_tensor(out=T["skh2"][:], in0=T["skh"][:],
         in1=T["rn_k"][:], op=ALU.mult), reads=["skh", "rn_k"], writes=["skh2"])
    p.op("v", lambda v: v.tensor_tensor(out=T["sct"][:], in0=T["e_ct"][:],
         in1=T["rn_k"][:], op=ALU.mult), reads=["e_ct", "rn_k"], writes=["sct"])
    R3 = lambda: T["R"][:].rearrange("p (h c) -> p h c", h=NH)
    v3 = lambda nm: T[nm][:].rearrange("p (h e) -> p h e", h=NH)
    bc = lambda nm: T[nm][:].unsqueeze(2).broadcast_to([128, NH, HD])
    p.op("v", lambda v: v.tensor_tensor(out=R3()[:, :, 0:64], in0=v3("v_tm"),
         in1=bc("beta"), op=ALU.mult), reads=["v_tm", "beta"], writes=["R"],
         tag=f"Rv{step}")
    p.op("v", lambda v: v.tensor_tensor(out=R3()[:, :, 64:128], in0=v3("k_tm"),
         in1=bc("skh2"), op=ALU.mult), reads=["k_tm", "skh2"], writes=["R"],
         tag=f"Rk{step}")
    p.op("v", lambda v: v.tensor_tensor(out=v3("kt_tm"), in0=v3("k_tm"),
         in1=bc("sct"), op=ALU.mult), reads=["k_tm", "sct"], writes=["kt_tm"],
         tag=f"kt{step}")
    if dbg:
        kb.dump("qh_fm", T["qh_fm"][:], ["qh_fm"])
        kb.dump("scT3", T["scT3"][:], ["scT3"])
        kb.dump("sq2n", T["sq2n"][:], ["sq2n"])
        kb.dump("qfm", T["q_fm"][:], ["q_fm"])
        kb.dump("R", T["R"][:], ["R"])
        kb.dump("kt", T["kt_tm"][:], ["kt_tm"])


def _halves(fn):
    return [(0, 1024), (1024, 2048)]


def emit_attn_step3(kb, step):
    p, T = kb.p, kb.T
    dbg = (step == 0)
    dv3 = lambda nm, nh: T[nm][:].rearrange("p (h c) -> p h c", h=nh)
    bc16 = lambda nm, w: T[nm][:].unsqueeze(2).broadcast_to([128, NH, w])

    if kb.stage == "SZ":
        return
    for hl in ["whi", "wlo"]:
        p.op("v", lambda v, a=hl: v.tensor_tensor(out=dv3("dW", NH),
             in0=dv3("c_dindW", NH), in1=bc16(a, 64), op=ALU.mult),
             reads=["c_dindW", hl], writes=["dW"], tag=f"dgW{hl}{step}")
        def mmW(t, first=(hl == "whi")):
            for q in range(2):
                inst = t.matmul(T["psB"][:, q * 512:(q + 1) * 512],
                                T["c_ones2"][:], T["dW"][:, q * 512:(q + 1) * 512],
                                start=first, stop=False)
            return inst
        p.op("t", mmW, reads=["c_ones2", "dW"], writes=["psB"], tag=f"mmW{hl}{step}")
        if kb.stage == "SA" and hl == "whi":
            return
    def mmWneg(t):
        for q in range(2):
            inst = t.matmul(T["psB"][:, q * 512:(q + 1) * 512], T["c_id128b"][:],
                            T["c_negW"][:, q * 512:(q + 1) * 512],
                            start=False, stop=True)
        return inst
    p.op("t", mmWneg, reads=["c_id128b", "c_negW"], writes=["psB"],
         tag=f"mmWneg{step}")
    if kb.stage == "SB":
        return
    def expW(s):
        for h in range(NH):
            inst = s.activation(out=T["dW"][:, h * 64:(h + 1) * 64],
                                in_=T["psB"][:, h * 64:(h + 1) * 64], func=ACT.Exp,
                                scale=1.0, bias=T["biasWN"][:, h:h + 1])
        return inst
    p.op("s", expW, reads=["psB"], writes=["dW"], scalar_reads=["biasWN"],
         tag=f"expW{step}")
    if kb.stage == "S0":
        return
    def gramKQ(t):
        inst = None
        for parity in range(2):
            if parity == 1:
                t.drain()
            rlo = 64 * parity
            for h in range(parity, NH, 2):
                for par in range(2):
                    clo = (h // 2) * 128 + 64 * par
                    inst = t.matmul(
                        T["psB"][64 * par:64 * par + 64, h * 64:(h + 1) * 64],
                        T["k_fm"][rlo:rlo + 64, clo:clo + 64],
                        T["q_fm"][rlo:rlo + 64, clo:clo + 64],
                        start=True, stop=True, skip_group_check=True)
        return inst
    p.op("t", gramKQ, reads=["k_fm", "q_fm"], writes=["psB"], tag=f"gKQ{step}")
    p.op("v", lambda v: v.tensor_tensor(out=T["wbar"][:], in0=T["dW"][:],
         in1=T["psB"][:], op=ALU.mult), reads=["dW", "psB"], writes=["wbar"],
         tag=f"wbar{step}")
    if kb.stage == "Sg":
        return

    for hl in ["ghi", "glo"]:
        p.op("v", lambda v, a=hl: v.tensor_tensor(out=dv3("dN2", NH),
             in0=dv3("c_dindN", NH), in1=bc16(a, 128), op=ALU.mult),
             reads=["c_dindN", hl], writes=["dN2"], tag=f"dgN{hl}{step}")
        def mmN(t, first=(hl == "ghi")):
            for q in range(4):
                inst = t.matmul(T["psC"][:, q * 512:(q + 1) * 512], T["c_ones2"][:],
                                T["dN2"][:, q * 512:(q + 1) * 512],
                                start=first, stop=False)
            return inst
        p.op("t", mmN, reads=["c_ones2", "dN2"], writes=["psC"],
             tag=f"mmN{hl}{step}")
    def mmNneg(t):
        for q in range(4):
            inst = t.matmul(T["psC"][:, q * 512:(q + 1) * 512], T["c_id128b"][:],
                            T["c_negN"][:, q * 512:(q + 1) * 512],
                            start=False, stop=True)
        return inst
    p.op("t", mmNneg, reads=["c_id128b", "c_negN"], writes=["psC"],
         tag=f"mmNneg{step}")
    def expN(s):
        for h in range(NH):
            inst = s.activation(out=T["dN"][:, h * 128:(h + 1) * 128],
                                in_=T["psC"][:, h * 128:(h + 1) * 128],
                                func=ACT.Exp, scale=1.0,
                                bias=T["biasWN"][:, h:h + 1])
        return inst
    p.op("s", expN, reads=["psC"], writes=["dN"], scalar_reads=["biasWN"],
         tag=f"expN{step}")

    for hl in ["bhi", "blo"]:
        p.op("v", lambda v, a=hl: v.tensor_tensor(out=dv3("dN2", NH),
             in0=dv3("c_dindN", NH), in1=bc16(a, 128), op=ALU.mult),
             reads=["c_dindN", hl], writes=["dN2"], tag=f"dgT{hl}{step}")
        def mmT(t, first=(hl == "bhi")):
            for q in range(4):
                inst = t.matmul(T["psC"][:, q * 512:(q + 1) * 512], T["c_ones2"][:],
                                T["dN2"][:, q * 512:(q + 1) * 512],
                                start=first, stop=False)
            return inst
        p.op("t", mmT, reads=["c_ones2", "dN2"], writes=["psC"],
             tag=f"mmT{hl}{step}")
    def mmTneg(t):
        for q in range(4):
            inst = t.matmul(T["psC"][:, q * 512:(q + 1) * 512], T["c_id128b"][:],
                            T["c_negNT"][:, q * 512:(q + 1) * 512],
                            start=False, stop=True)
        return inst
    p.op("t", mmTneg, reads=["c_id128b", "c_negNT"], writes=["psC"],
         tag=f"mmTneg{step}")
    def expT(s):
        for h in range(NH):
            inst = s.activation(out=T["dN2"][:, h * 128:(h + 1) * 128],
                                in_=T["psC"][:, h * 128:(h + 1) * 128],
                                func=ACT.Exp, scale=1.0, bias=T["nlt"][:, h:h + 1])
        return inst
    p.op("s", expT, reads=["psC"], writes=["dN2"], scalar_reads=["nlt"],
         tag=f"expT{step}")

    def gramKK(t):
        inst = None
        for parity in range(2):
            if parity == 1:
                t.drain()
            rlo = 64 * parity
            for h in range(parity, NH, 2):
                for par in range(2):
                    clo = (h // 2) * 128 + 64 * par
                    inst = t.matmul(
                        T["psC"][64 * par:64 * par + 64,
                                 h * 128 + 64 * par:h * 128 + 64 * par + 64],
                        T["k_fm"][rlo:rlo + 64, clo:clo + 64],
                        T["k_fm"][rlo:rlo + 64, clo:clo + 64],
                        start=True, stop=True, skip_group_check=True)
        return inst
    p.op("t", gramKK, reads=["k_fm"], writes=["psC"], tag=f"gKK{step}")
    p.op("v", lambda v: v.scalar_tensor_tensor(out=T["nbar"][:], in0=T["dN"][:],
         scalar=-1.0, in1=T["psC"][:], op0=ALU.mult, op1=ALU.mult),
         reads=["dN", "psC"], writes=["nbar"], tag=f"nbar{step}")
    p.op("v", lambda v: v.scalar_tensor_tensor(out=T["pbar"][:], in0=T["dN2"][:],
         scalar=-1.0, in1=T["psC"][:], op0=ALU.mult, op1=ALU.mult),
         reads=["dN2", "psC"], writes=["pbar"], tag=f"pbar{step}")
    if dbg:
        kb.dump("nbar", T["nbar"][:], ["nbar"])
        kb.dump("wbar", T["wbar"][:], ["wbar"])
    if kb.stage in ("S1",):
        return

    Bname, Pname = "nbar", "pbar"
    Balt, Palt = "dN", "dN2"
    Xsrc = "R"
    for lvl in range(6):
        def apply(t, B=Bname, X=Xsrc):
            for h in range(NH):
                sl = slice(h * 128, (h + 1) * 128)
                t.matmul(T["psC"][:, sl], T[B][:, sl], T[X][:, sl],
                         start=(h % 4 == 0), stop=False)
                inst = t.matmul(T["psC"][:, sl], T["c_id128b"][:], T[X][:, sl],
                                start=False, stop=(h % 4 == 3))
            return inst
        p.op("t", lambda t, f=apply: f(t), reads=[Bname, Xsrc, "c_id128b"],
             writes=["psC"], tag=f"ap{lvl}_{step}")
        p.op("v", lambda v: v.tensor_copy(out=T["Xc"][:, 0:1024],
             in_=T["psC"][:, 0:1024]), reads=["psC"], writes=["Xc"],
             tag=f"apev{lvl}_{step}")
        p.op("s", lambda s: s.copy(out=T["Xc"][:, 1024:2048],
             in_=T["psC"][:, 1024:2048]), reads=["psC"], writes=["Xc"],
             tag=f"apes{lvl}_{step}")
        Xsrc = "Xc"
        if lvl < 5:
            def sqB(t, B=Bname, P=Pname):
                for h in range(NH):
                    sl = slice(h * 128, (h + 1) * 128)
                    inst = t.matmul(T["psC"][:, sl], T[P][:, sl], T[B][:, sl],
                                    start=(h % 4 == 0), stop=(h % 4 == 3))
                return inst
            p.op("t", lambda t, f=sqB: f(t), reads=[Bname, Pname],
                 writes=["psC"], tag=f"sqB{lvl}_{step}")
            p.op("v", lambda v, d=Balt: v.tensor_copy(out=T[d][:], in_=T["psC"][:]),
                 reads=["psC"], writes=[Balt], tag=f"sqBe{lvl}_{step}")
            if lvl < 4:
                def sqP(t, B=Bname, P=Pname):
                    for h in range(NH):
                        sl = slice(h * 128, (h + 1) * 128)
                        inst = t.matmul(T["psC"][:, sl], T[B][:, sl], T[P][:, sl],
                                        start=(h % 4 == 0), stop=(h % 4 == 3))
                    return inst
                p.op("t", lambda t, f=sqP: f(t), reads=[Bname, Pname],
                     writes=["psC"], tag=f"sqP{lvl}_{step}")
                p.op("s", lambda s, d=Palt: s.copy(out=T[d][:], in_=T["psC"][:]),
                     reads=["psC"], writes=[Palt], tag=f"sqPe{lvl}_{step}")
                Pname, Palt = Palt, Pname
            Bname, Balt = Balt, Bname
    if dbg:
        kb.dump("Xs", T["Xc"][:], ["Xc"])


def emit_attn_step4(kb, step, x2_d, x2ln_d):
    p, T = kb.p, kb.T
    dbg = (step == 0)
    r0, r1 = step * C, (step + 1) * C

    def xkt(t):
        pb = T["psA"][:].bitcast(BF16)
        for h in range(NH):
            rlo = 64 * (h % 2)
            inst = t.transpose(pb[rlo:rlo + 64, (h // 2) * 128:(h // 2) * 128 + 128],
                               T["Xc"][:, h * 128 + 64:(h + 1) * 128],
                               T["c_id128b"][:], tile_position=(0, rlo))
        return inst
    p.op("t", xkt, reads=["Xc", "c_id128b"], writes=["psA"], tag=f"xkt{step}")
    p.op("v", lambda v: v.tensor_copy(out=T["xkT"][:],
         in_=T["psA"][:].bitcast(BF16)[:, 0:1024]),
         reads=["psA"], writes=["xkT"], tag=f"xkte{step}")

    def gmm(t):
        inst = None
        for h in range(NH):
            cl = slice(h * 64, (h + 1) * 64)
            for par in range(2):
                rlo = 64 * (h % 2)
                t.matmul(
                    T["psB"][64 * par:64 * par + 64, cl],
                    T["xkT"][rlo:rlo + 64, (h // 2) * 128 + 64 * par:
                             (h // 2) * 128 + 64 * par + 64],
                    T["negS_b"][rlo:rlo + 64,
                                (par * 8 + h // 2) * 64:(par * 8 + h // 2) * 64 + 64],
                    start=True, stop=False, skip_group_check=True)
            inst = t.matmul(T["psB"][:, cl], T["c_id128b"][:],
                            T["Xc"][:, h * 128:h * 128 + 64],
                            start=False, stop=True, skip_group_check=True)
        return inst
    p.op("t", gmm, reads=["Xc", "xkT", "negS_b", "c_id128b"], writes=["psB"],
         tag=f"gmm{step}")
    p.op("v", lambda v: v.tensor_copy(out=T["gt"][:], in_=T["psB"][:]),
         reads=["psB"], writes=["gt"], tag=f"gte{step}")
    if dbg:
        kb.dump("gt", T["gt"][:], ["gt"])

    def omm1(t):
        inst = None
        for parity in range(2):
            if parity == 1:
                t.drain()
            rlo = 64 * parity
            for h in range(parity, NH, 2):
                cl = slice(h * 64, (h + 1) * 64)
                for par in range(2):
                    inst = t.matmul(
                        T["psC"][64 * par:64 * par + 64, cl],
                        T["qh_fm"][rlo:rlo + 64, (h // 2) * 128 + 64 * par:
                                   (h // 2) * 128 + 64 * par + 64],
                        T["negS_b"][rlo:rlo + 64, (par * 8 + h // 2) * 64:
                                    (par * 8 + h // 2) * 64 + 64],
                        start=True, stop=True, skip_group_check=True)
        return inst

    def omm2(t):
        inst = None
        for par in (1, 0):
            if par == 0:
                t.drain()
            for h in range(NH):
                cl = slice(h * 64, (h + 1) * 64)
                inst = t.matmul(
                    T["psB"][64 * par:64 * par + 64, cl],
                    T["wbar"][64 * par:64 * par + 64, cl],
                    T["gt"][64 * par:64 * par + 64, cl],
                    start=True, stop=True, skip_group_check=True)
        return inst
    p.op("t", omm1, reads=["qh_fm", "negS_b"], writes=["psC"],
         tag=f"omm1{step}")
    p.op("t", omm2, reads=["wbar", "gt"], writes=["psB"],
         tag=f"omm2{step}")
    p.op("s", lambda s: s.copy(out=T["y_tm"][:], in_=T["psC"][:, 0:1024]),
         reads=["psC"], writes=["y_tm"], tag=f"ye1{step}")
    p.op("v", lambda v: v.tensor_tensor(out=T["y_tm"][:], in0=T["y_tm"][:],
         in1=T["psB"][:, 0:1024], op=ALU.add), reads=["y_tm", "psB"],
         writes=["y_tm"], tag=f"ye2{step}")
    if dbg:
        kb.dump("y0", T["y_tm"][:], ["y_tm"])
    if step == 1 and "y1" in kb.dbg_outs:
        kb.dump("y1", T["y_tm"][:], ["y_tm"])

    def smm(t):
        inst = None
        for par in range(2):
            if par == 1:
                t.drain()
            for h in range(NH):
                rlo = 64 * (h % 2)
                inst = t.matmul(
                    T["psA"][rlo:rlo + 64,
                             (par * 8 + h // 2) * 64:(par * 8 + h // 2) * 64 + 64],
                    T["kt_tm"][64 * par:64 * par + 64, h * 64:(h + 1) * 64],
                    T["gt"][64 * par:64 * par + 64, h * 64:(h + 1) * 64],
                    start=True, stop=True, skip_group_check=True)
        return inst
    p.op("t", smm, reads=["kt_tm", "gt"], writes=["psA"], tag=f"smm{step}")
    st3 = lambda: T["negS"][:].rearrange("p (c e) -> p c e", c=16)
    ebc = lambda: T["e_lC_st"][:].unsqueeze(2).broadcast_to([128, 16, HD])
    p.op("v", lambda v: v.tensor_tensor(out=T["scrD"][:].rearrange(
         "p (c e) -> p c e", c=16), in0=st3(), in1=ebc(), op=ALU.mult),
         reads=["negS", "e_lC_st"], writes=["scrD"], tag=f"sdec{step}")
    p.op("v", lambda v: v.tensor_tensor(out=T["negS"][:], in0=T["scrD"][:],
         in1=T["psA"][:], op=ALU.subtract), reads=["scrD", "psA"],
         writes=["negS"], tag=f"supd{step}")
    p.op("s", lambda s: s.copy(out=T["negS_b"][:], in_=T["negS"][:]),
         reads=["negS"], writes=["negS_b"], tag=f"supb{step}")

    p.op("s", lambda s: s.activation(out=T["scrD2"][:], in_=T["y_tm"][:],
         func=ACT.Square), reads=["y_tm"], writes=["scrD2"], tag=f"ysq{step}")
    p.op("v", lambda v: v.tensor_reduce(out=T["ssq"][:], in_=T["scrD2"][:],
         axis=AX.X, op=ALU.add), reads=["scrD2"], writes=["ssq"],
         tag=f"yssq{step}", deferred=True)
    p.op("s", lambda s: s.activation(out=T["sd"][:], in_=T["ssq"][:],
         func=ACT.Sqrt, bias=T["eps1"][:, 0:1], scale=1.0 / D),
         reads=["ssq", "eps1"], writes=["sd"], tag=f"ysd{step}")
    p.op("v", lambda v: v.reciprocal(out=T["rstd"][:], in_=T["sd"][:]),
         reads=["sd"], writes=["rstd"], tag=f"yr{step}", deferred=True)
    p.op("s", lambda s: s.activation(out=T["scrD2"][:], in_=T["y_tm"][:],
         func=ACT.Identity, scale=T["rstd"][:, 0:1]), reads=["y_tm"],
         writes=["scrD2"], scalar_reads=["rstd"], tag=f"yn{step}")
    p.op("v", lambda v: v.tensor_tensor(out=T["scrD2"][:], in0=T["scrD2"][:],
         in1=T["gnorm_bc"][:], op=ALU.mult), reads=["scrD2", "gnorm_bc"],
         writes=["scrD2"], tag=f"yg{step}")
    p.op("v", lambda v: v.tensor_tensor(out=T["xln_tm"][:], in0=T["scrD2"][:],
         in1=T["g_tm"][:], op=ALU.mult), reads=["scrD2", "g_tm"],
         writes=["xln_tm"], tag=f"ygg{step}")
    def try8(t):
        pb = T["psA"][:].bitcast(BF16)
        for ci in range(8):
            inst = t.transpose(pb[:, ci * 128:(ci + 1) * 128],
                               T["xln_tm"][:, ci * 128:(ci + 1) * 128],
                               T["c_id128b"][:])
        return inst
    p.op("t", try8, reads=["xln_tm", "c_id128b"], writes=["psA"], tag=f"try{step}")
    p.op("v", lambda v: v.tensor_copy(out=T["yf_fm"][:],
         in_=T["psA"][:].bitcast(BF16)[:, 0:1024]),
         reads=["psA"], writes=["yf_fm"], tag=f"trye{step}")
    def womm(t):
        for half in range(2):
            for i in range(8):
                inst = t.matmul(T["psB"][:, half * 512:(half + 1) * 512],
                                T["yf_fm"][:, i * 128:(i + 1) * 128],
                                T["w_Wo"][:, i * 1024 + half * 512:
                                          i * 1024 + (half + 1) * 512],
                                start=(i == 0), stop=(i == 7))
        return inst
    p.op("t", womm, reads=["yf_fm", "w_Wo"], writes=["psB"], tag=f"womm{step}")

    _ln_ops(kb, "psB", lambda T: T["psB"][:], "scrD",
            lambda T: T["scrD"][:], "m2", "r2", "b2", tag=f"ln2_{step}")
    p.op("v", lambda v: v.tensor_tensor(out=T["xc"][:], in0=T["xc"][:],
         in1=T["scrD"][:], op=ALU.add), reads=["xc", "scrD"], writes=["xc"],
         tag=f"res1{step}")
    for sq in range(NSEQ):
        p.dma("g", lambda g, sq=sq: g.dma_start(
            out=x2_d[sq, r0:r1, :], in_=T["xc"][sq * C:(sq + 1) * C, :]),
            reads=["xc"], tag=f"stx2_{step}")
    if dbg:
        kb.dump("x2_0", T["xc"][:], ["xc"])

    _ln_ops(kb, "xc", lambda T: T["xc"][:], "x2ln_c",
            lambda T: T["x2ln_c"][:], "m3", "r3", "b3", tag=f"ln3_{step}")
    for sq in range(NSEQ):
        p.dma("g", lambda g, sq=sq: g.dma_start(
            out=x2ln_d[sq, r0:r1, :], in_=T["x2ln_c"][sq * C:(sq + 1) * C, :]),
            reads=["x2ln_c"], tag=f"stxln_{step}")
    p.op("v", lambda v: v.tensor_reduce(out=T["mx1"][:], in_=T["x2ln_c"][:],
         axis=AX.X, op=ALU.max, apply_absolute_value=True),
         reads=["x2ln_c"], writes=["mx1"], tag=f"mx{step}", deferred=True)
    p.op("v", lambda v: v.tensor_tensor(out=T["mxacc"][:], in0=T["mxacc"][:],
         in1=T["mx1"][:], op=ALU.max), reads=["mxacc", "mx1"], writes=["mxacc"],
         tag=f"mxa{step}")
    if dbg:
        kb.dump("x2ln_0", T["x2ln_c"][:], ["x2ln_c"])


def lower(kb):
    nc, p = kb.nc, kb.p
    T = kb.T
    es = kb.es
    sems = {}
    dsems = {}
    for e in ("t", "v", "s", "g", "y"):
        sems[e] = es.enter_context(nc.semaphore("sem_" + e))
    for e in ("g", "y", "s", "v"):
        dsems[e] = [es.enter_context(nc.semaphore(f"dsem_{e}{i}")) for i in range(8)]
    block = es.enter_context(nc.Block())
    p.finalize(nc, None, sems, dsems)

    @block.tensor
    def _(t):
        p.emit("t", t, sems, dsems)

    @block.vector
    def _(v):
        p.emit("v", v, sems, dsems)

    @block.scalar
    def _(s):
        p.emit("s", s, sems, dsems)

    @block.gpsimd
    def _(g):
        p.emit("g", g, sems, dsems)

    @block.sync
    def _(y):
        p.emit("y", y, sems, dsems)

    kb.es.close()
    return nc


def psum_init(kb):
    p, T = kb.p, kb.T
    for ps, res in [("psA", "psA"), ("psB", "psB"), ("psC", "psC")]:
        p.op("v", lambda v, a=ps: v.memset(T[a][:], 0.0), writes=[res],
             tag="psz")


def final_barrier(kb):
    p = kb.p
    for e in ("g", "y", "s", "v"):
        n = p.ndma[e]
        if n:
            def mk(eng_obj, e=e, n=n):
                return None
            o = p.op(e, lambda eng: eng.nop(), tag="final_wait")
            last = [op for op in p.ops[e] if op.is_dma]
            if last:
                o.deps.append(last[-1])


def _gmax_collective(kb, srcs, dsts, tagn):
    p, T, nc = kb.p, kb.T, kb.nc
    n = len(srcs)
    cin = kb.dtmp(f"cc_in_{tagn}", [128, n])
    cout = kb.dtmp(f"cc_out_{tagn}", [128, n])
    sd = kb.dtmp(f"cc_s_{tagn}", [n])
    for i, s in enumerate(srcs):
        p.dma("g", lambda g, i=i, s=s: g.dma_start(
            out=cin.ap()[:, i:i + 1], in_=T[s][:],
            allow_slow_non_contiguous=True), reads=[s],
            writes=[f"ccin_{tagn}"], tag=f"cc{tagn}a")
    p.op("g", lambda g: g.collective_compute(
        "AllReduce", ALU.max, replica_groups=[list(range(8))],
        ins=[cin.ap()], outs=[cout.ap()]),
        reads=[f"ccin_{tagn}"], writes=[f"ccout_{tagn}"], tag=f"cc{tagn}b",
        is_cc=True)
    p.dma("g", lambda g: g.dma_start(out=T["ccrow"][0:128, 0:n], in_=cout.ap()),
          reads=[f"ccout_{tagn}"], writes=["ccrow"], tag=f"cc{tagn}c")
    p.op("t", lambda t: t.transpose(T["psA"][0:n, 896:1024],
         T["ccrow"][:, 0:n], T["c_id128f"][:]), reads=["ccrow", "c_id128f"],
         writes=["psA"], tag=f"cc{tagn}t")
    p.op("v", lambda v: v.tensor_reduce(out=T["ccred"][0:n, 0:1],
         in_=T["psA"][0:n, 896:1024], axis=AX.X, op=ALU.max),
         reads=["psA"], writes=["ccred"], tag=f"cc{tagn}r")
    p.dma("g", lambda g: g.dma_start(out=sd.ap(), in_=T["ccred"][0:n, 0:1]),
          reads=["ccred"], writes=[f"ccs_{tagn}"], tag=f"cc{tagn}d")
    for i, d in enumerate(dsts):
        p.dma("y", lambda y, i=i, d=d: y.dma_start(
            out=T[d][:], in_=sd.ap()[i:i + 1].rearrange("(a b) -> a b", a=1)
            .broadcast_to([128, 1])), reads=[f"ccs_{tagn}"], writes=[d],
            tag=f"cc{tagn}e")


def _fix_cc_deps(kb, tagn):
    pass


def emit_ffn_head(kb, x2_d, x2ln_d):
    p, T, nc = kb.p, kb.T, kb.nc
    NB = IFF // 32
    kb.din("w_gate_d", [IFF, D]); kb.din("w_up_d", [IFF, D])
    kb.din("w_down_d", [D, IFF])
    uraw_d = kb.dtmp("uraw_d", [IFF, NTOK]).ap()
    graw_d = kb.dtmp("graw_d", [IFF, NTOK]).ap()
    kb.sb("ccrow", [128, 8], F32)
    kb.sb("ccred", [8, 1], F32)
    kb.sb("sgx", [128, 1], F32)
    kb.sb("rsgx", [128, 1], F32)
    kb.sb("Mxg", [128, 1], F32)
    kb.sb("xt1", [128, D], F32)
    kb.sb("xq_tm", [128, D], BF16)
    kb.sb("mxu", [128, 1], F32); kb.sb("mxg", [128, 1], F32)
    kb.sb("mxp", [128, 1], F32); kb.sb("mxo", [128, 1], F32)
    kb.sb("Mug", [128, 1], F32); kb.sb("Mgg", [128, 1], F32)
    kb.sb("Mpg", [128, 1], F32); kb.sb("Mog", [128, 1], F32)
    kb.sb("cu", [128, 1], F32); kb.sb("cg", [128, 1], F32)
    kb.sb("sgu", [128, 1], F32); kb.sb("sgg", [128, 1], F32)
    kb.sb("c2", [128, 1], F32); kb.sb("co", [128, 1], F32)
    kb.sb("sgo", [128, 1], F32)
    kb.push_scope()
    kb.sb("wu_l", [128, 8 * IFF], BF16)
    kb.sb("wg_l", [128, 8 * IFF], BF16)
    kb.sb("xq_fm", [128, 8 * NTOK], BF16)
    kb.sb("wst", [128, IFF], F32)
    kb.sb("bmax", [128, 96], F32)
    kb.sb("bsc", [128, 96], F32)
    kb.sb("brs", [128, 96], F32)
    kb.sb("dq1", [128, IFF], F32)
    kb.sb("dqo", [128, IFF], BF16)
    kb.sb("u_st", [128, 1024], F32)
    kb.sb("g_st", [128, 1024], F32)

    _gmax_collective(kb, ["mxacc"], ["Mxg"], "x")
    p.op("v", lambda v: v.tensor_scalar(out=T["sgx"][:], in0=T["Mxg"][:],
         scalar1=1.0 / QMAX, scalar2=1e-12, op0=ALU.mult, op1=ALU.max),
         reads=["Mxg"], writes=["sgx"], tag="sgx")
    p.op("v", lambda v: v.reciprocal(out=T["rsgx"][:], in_=T["sgx"][:]),
         reads=["sgx"], writes=["rsgx"], tag="rsgx")

    for wsrc, wdst in [("w_up_d", "wu_l"), ("w_gate_d", "wg_l")]:
        for ot in range(24):
            wd = kb.dram[wsrc].ap()
            p.dma("y", lambda y, ot=ot, wd=wd: y.dma_start(
                out=T["wst"][:, 0:1024], in_=wd[ot * 128:(ot + 1) * 128, :]),
                writes=["wst"], tag="wld")
            emit_dequant_tile(kb, 1024, wsrc + str(ot))
            def wtr(t, ot=ot):
                pb = T["psA"][:].bitcast(BF16)
                for ci in range(8):
                    inst = t.transpose(pb[:, ci * 128:(ci + 1) * 128],
                                       T["dqo"][:, ci * 128:(ci + 1) * 128],
                                       T["c_id128b"][:])
                return inst
            p.op("t", wtr, reads=["dqo", "c_id128b"], writes=["psA"], tag="wtr")
            p.op("v", lambda v, wdst=wdst, ot=ot: v.tensor_copy(
                out=T[wdst][:].rearrange("p (c o) -> p c o", c=8)[:, :, ot * 128:(ot + 1) * 128],
                in_=T["psA"][:].bitcast(BF16)[:, 0:1024].rearrange(
                    "p (c o) -> p c o", c=8)),
                reads=["psA"], writes=[wdst], tag="wtre")

    for ti in range(NTOK // 128):
        sq, r0 = divmod(ti * 128, S)
        p.dma("y", lambda y, sq=sq, r0=r0: y.dma_start(
            out=T["xt1"][:], in_=x2ln_d[sq, r0:r0 + 128, :]), writes=["xt1"],
            tag=f"xqld{ti}")
        p.op("s", lambda s: s.activation(out=T["xt1"][:], in_=T["xt1"][:],
             func=ACT.Identity, scale=T["rsgx"][:, 0:1], bias=T["magicC"][:, 0:1]),
             reads=["xt1"], writes=["xt1"], scalar_reads=["rsgx", "magicC"],
             tag=f"xqr{ti}")
        p.op("v", lambda v: v.tensor_scalar(out=T["xq_tm"][:], in0=T["xt1"][:],
             scalar1=MAGIC, scalar2=None, op0=ALU.subtract),
             reads=["xt1"], writes=["xq_tm"], tag=f"xqs{ti}")
        def xtr(t):
            pb = T["psA"][:].bitcast(BF16)
            for ci in range(8):
                inst = t.transpose(pb[:, ci * 128:(ci + 1) * 128],
                                   T["xq_tm"][:, ci * 128:(ci + 1) * 128],
                                   T["c_id128b"][:])
            return inst
        p.op("t", xtr, reads=["xq_tm", "c_id128b"], writes=["psA"], tag=f"xqt{ti}")
        p.op("v", lambda v, ti=ti: v.tensor_copy(
             out=T["xq_fm"][:].rearrange("p (c t) -> p c t", c=8)[:, :, ti * 128:(ti + 1) * 128],
             in_=T["psA"][:].bitcast(BF16)[:, 0:1024].rearrange("p (c t) -> p c t", c=8)),
             reads=["psA"], writes=["xq_fm"], tag=f"xqe{ti}")
    return uraw_d, graw_d


def emit_dequant_tile(kb, width, tagn, sfx=""):
    p, T = kb.p, kb.T
    nb = width // 32
    W, BM, BS, BR, D1, DQ = ("wst" + sfx, "bmax" + sfx, "bsc" + sfx,
                             "brs" + sfx, "dq1" + sfx, "dqo" + sfx)
    v3 = lambda nm, w=width: T[nm][:, 0:w].rearrange("p (b k) -> p b k", k=32)
    bcb = lambda nm, c=nb: T[nm][:, 0:c].unsqueeze(2).broadcast_to([128, c, 32])
    p.op("v", lambda v: v.tensor_reduce(out=T[BM][:, 0:nb], in_=v3(W),
         axis=AX.X, op=ALU.max, apply_absolute_value=True),
         reads=[W], writes=[BM], tag="dq_bm", deferred=True)
    p.op("v", lambda v: v.tensor_scalar(out=T[BS][:, 0:nb], in0=T[BM][:, 0:nb],
         scalar1=1.0 / 7.0, scalar2=1e-12, op0=ALU.mult, op1=ALU.max),
         reads=[BM], writes=[BS], tag="dq_bs", deferred=True)
    p.op("v", lambda v: v.reciprocal(out=T[BR][:, 0:nb], in_=T[BS][:, 0:nb]),
         reads=[BS], writes=[BR], tag="dq_rs", deferred=True)
    p.op("v", lambda v: v.tensor_tensor(out=v3(D1), in0=v3(W),
         in1=bcb(BR), op=ALU.mult), reads=[W, BR], writes=[D1],
         tag="dq_m1")
    p.op("v", lambda v: v.tensor_scalar(out=T[D1][:, 0:width], in0=T[D1][:, 0:width],
         scalar1=MAGIC, scalar2=MAGIC, op0=ALU.add, op1=ALU.subtract),
         reads=[D1], writes=[D1], tag="dq_rnd")
    p.op("v", lambda v: v.tensor_tensor(out=v3(DQ), in0=v3(D1),
         in1=bcb(BS), op=ALU.mult), reads=[D1, BS], writes=[DQ],
         tag="dq_m2")


def emit_ffn_f1(kb, uraw_d, graw_d):
    p, T = kb.p, kb.T
    NSLAB = NTOK // 512
    p.op("v", lambda v: v.memset(T["mxu"][:], 0.0), writes=["mxu"])
    p.op("v", lambda v: v.memset(T["mxg"][:], 0.0), writes=["mxg"])
    xq3 = lambda: T["xq_fm"][:].rearrange("p (c t) -> p c t", c=8)
    for sl in range(NSLAB):
        for wnm, stg, mx, raw, ps in [("wu_l", "u_st", "mxu", uraw_d, "psA"),
                                      ("wg_l", "g_st", "mxg", graw_d, "psB")]:
            for oc in range(24):
                half = oc % 2
                def mm(t, wnm=wnm, oc=oc, sl=sl, half=half, ps=ps):
                    for i in range(8):
                        inst = t.matmul(
                            T[ps][:, half * 512:(half + 1) * 512],
                            T[wnm][:].rearrange("p (c o) -> p c o", c=8)[:, i, oc * 128:(oc + 1) * 128],
                            xq3()[:, i, sl * 512:(sl + 1) * 512],
                            start=(i == 0), stop=(i == 7))
                    return inst
                p.op("t", mm, reads=[wnm, "xq_fm"], writes=[ps], tag=f"f1mm{sl}_{oc}")
                p.op("s", lambda s, stg=stg, half=half, ps=ps: s.copy(
                     out=T[stg][:, half * 512:(half + 1) * 512],
                     in_=T[ps][:, half * 512:(half + 1) * 512]),
                     reads=[ps], writes=[stg], tag=f"f1ev{sl}_{oc}")
                p.op("v", lambda v, stg=stg, half=half: v.tensor_reduce(
                     out=T["mx1"][:], in_=T[stg][:, half * 512:(half + 1) * 512],
                     axis=AX.X, op=ALU.max, apply_absolute_value=True),
                     reads=[stg], writes=["mx1"], tag=f"f1mx{sl}_{oc}", deferred=True)
                p.op("v", lambda v, mx=mx: v.tensor_tensor(out=T[mx][:],
                     in0=T[mx][:], in1=T["mx1"][:], op=ALU.max),
                     reads=[mx, "mx1"], writes=[mx], tag=f"f1mxa{sl}_{oc}")
                p.dma("g", lambda g, raw=raw, oc=oc, sl=sl, stg=stg, half=half:
                      g.dma_start(out=raw[oc * 128:(oc + 1) * 128,
                                          sl * 512:(sl + 1) * 512],
                                  in_=T[stg][:, half * 512:(half + 1) * 512]),
                      reads=[stg], tag=f"f1st{sl}_{oc}")


def _scal(kb, fn, reads, writes, tag):
    kb.p.op("v", fn, reads=reads, writes=writes, tag=tag, deferred=True)


def emit_ffn_rest(kb, x2_d, x2ln_d, out_d, uraw_d, graw_d):
    p, T, nc = kb.p, kb.T, kb.nc
    _gmax_collective(kb, ["mxu", "mxg"], ["Mug", "Mgg"], "ug")
    for Mg, sg, cc in [("Mug", "sgu", "cu"), ("Mgg", "sgg", "cg")]:
        _scal(kb, lambda v, Mg=Mg, sg=sg: v.tensor_tensor(out=T[sg][:],
              in0=T[Mg][:], in1=T["sgx"][:], op=ALU.mult), [Mg, "sgx"], [sg], "a2a")
        _scal(kb, lambda v, sg=sg: v.tensor_scalar(out=T[sg][:], in0=T[sg][:],
              scalar1=1.0 / QMAX, scalar2=1e-12, op0=ALU.mult, op1=ALU.max),
              [sg], [sg], "a2b")
        _scal(kb, lambda v, sg=sg, cc=cc: v.reciprocal(out=T[cc][:], in_=T[sg][:]),
              [sg], [cc], "a2c")
        _scal(kb, lambda v, cc=cc: v.tensor_tensor(out=T[cc][:], in0=T[cc][:],
              in1=T["sgx"][:], op=ALU.mult), [cc, "sgx"], [cc], "a2d")

    kb.pop_scope()
    kb.barrier()
    kb.push_scope()
    kb.sb("wst2", [128, IFF], F32)
    kb.sb("bmax2", [128, 96], F32); kb.sb("bsc2", [128, 96], F32)
    kb.sb("brs2", [128, 96], F32)
    kb.sb("dq12", [128, IFF], F32); kb.sb("dqo2", [128, IFF], BF16)
    kb.sb("swg", [128, 4096], F32)
    kb.sb("swu", [128, 4096], F32)
    kb.sb("sw1", [128, 4096], F32)
    kb.sb("sw2", [128, 4096], F32)
    kb.sb("oq_sl", [128, 24 * 512], BF16)
    kb.sb("wd_c0", [128, 1024], BF16); kb.sb("wd_c1", [128, 1024], BF16)
    kb.sb("xr", [128, D], F32)
    kb.sb("of", [128, D], F32)
    kb.sb("scrF", [128, D], F32)
    wdT_d = kb.dtmp("wdT_d", [IFF, 1024], BF16).ap()

    wd = kb.dram["w_down_d"].ap()
    for ot in range(8):
        p.dma("y", lambda y, ot=ot: y.dma_start(
            out=T["wst2"][:], in_=wd[ot * 128:(ot + 1) * 128, :]),
            writes=["wst2"], tag="wdld")
        emit_dequant_tile(kb, IFF, "wd" + str(ot), sfx="2")
        for grp in range(3):
            def wtr(t, grp=grp):
                pb = T["psA"][:].bitcast(BF16)
                for ci in range(8):
                    inst = t.transpose(pb[:, ci * 128:(ci + 1) * 128],
                                       T["dqo2"][:, (grp * 8 + ci) * 128:(grp * 8 + ci + 1) * 128],
                                       T["c_id128b"][:])
                return inst
            p.op("t", wtr, reads=["dqo2", "c_id128b"], writes=["psA"], tag="wdtr")
            p.op("v", lambda v: v.tensor_copy(
                 out=T["dq12"][:, 0:512].bitcast(BF16),
                 in_=T["psA"][:].bitcast(BF16)[:, 0:1024]), reads=["psA"],
                 writes=["dq12"], tag="wdtre")
            for ci in range(8):
                p.dma("g", lambda g, ot=ot, grp=grp, ci=ci: g.dma_start(
                    out=wdT_d[(grp * 8 + ci) * 128:(grp * 8 + ci + 1) * 128,
                              ot * 128:(ot + 1) * 128],
                    in_=T["dq12"][:, 0:512].bitcast(BF16)[:, ci * 128:(ci + 1) * 128]),
                    reads=["dq12"], tag="wdst")

    p_d = kb.dtmp("p_d", [IFF, NTOK]).ap()
    p.op("v", lambda v: v.memset(T["mxp"][:], 0.0), writes=["mxp"])
    for ot in range(24):
        buf = "swg" if ot % 2 == 0 else "swu"
        p.dma("y", lambda y, ot=ot, buf=buf: y.dma_start(out=T[buf][:],
              in_=graw_d[ot * 128:(ot + 1) * 128, :]), writes=[buf], tag="mpld")
        emit_gq_chain(kb, buf, "sw1", "sw2", p_out=True)
        p.dma("g", lambda g, ot=ot: g.dma_start(
            out=p_d[ot * 128:(ot + 1) * 128, :], in_=T["sw2"][:]),
            reads=["sw2"], writes=[f"p_d{ot}"], tag="pst")
        p.op("v", lambda v: v.tensor_reduce(out=T["mx1"][:], in_=T["sw2"][:],
             axis=AX.X, op=ALU.max, apply_absolute_value=True),
             reads=["sw2"], writes=["mx1"], tag="mpmx", deferred=True)
        p.op("v", lambda v: v.tensor_tensor(out=T["mxp"][:], in0=T["mxp"][:],
             in1=T["mx1"][:], op=ALU.max), reads=["mxp", "mx1"], writes=["mxp"],
             tag="mpmxa")
    _gmax_collective(kb, ["mxp"], ["Mpg"], "p")
    _scal(kb, lambda v: v.tensor_tensor(out=T["c2"][:], in0=T["Mpg"][:],
          in1=T["sgg"][:], op=ALU.mult), ["Mpg", "sgg"], ["c2"], "a3a")
    _scal(kb, lambda v: v.tensor_scalar(out=T["c2"][:], in0=T["c2"][:],
          scalar1=1.0 / (65536.0 * QMAX), scalar2=1e-12, op0=ALU.mult,
          op1=ALU.max), ["c2"], ["c2"], "a3b")
    _scal(kb, lambda v: v.reciprocal(out=T["co"][:], in_=T["c2"][:]),
          ["c2"], ["co"], "a3c")
    _scal(kb, lambda v: v.tensor_tensor(out=T["co"][:], in0=T["co"][:],
          in1=T["sgg"][:], op=ALU.mult), ["co", "sgg"], ["co"], "a3d")
    _scal(kb, lambda v: v.tensor_scalar(out=T["co"][:], in0=T["co"][:],
          scalar1=1.0 / 65536.0, scalar2=None, op0=ALU.mult), ["co"], ["co"],
          "a3e")
    o_d = kb.dtmp("o_d", [IFF, NTOK]).ap()
    p.op("v", lambda v: v.memset(T["mxo"][:], 0.0), writes=["mxo"])
    for ot in range(24):
        p.dma("y", lambda y, ot=ot: y.dma_start(out=T["sw2"][:],
              in_=p_d[ot * 128:(ot + 1) * 128, :]),
              reads=[f"p_d{ot}"], writes=["sw2"], tag="mold")
        p.dma("y", lambda y, ot=ot: y.dma_start(out=T["swu"][:],
              in_=uraw_d[ot * 128:(ot + 1) * 128, :]), writes=["swu"], tag="mold2")
        emit_o_chain(kb, 4096, from_p=True)
        p.dma("g", lambda g, ot=ot: g.dma_start(
            out=o_d[ot * 128:(ot + 1) * 128, :], in_=T["sw2"][:]),
            reads=["sw2"], writes=[f"o_d{ot}"], tag="ost")
        p.op("v", lambda v: v.tensor_reduce(out=T["mx1"][:], in_=T["sw2"][:],
             axis=AX.X, op=ALU.max, apply_absolute_value=True),
             reads=["sw2"], writes=["mx1"], tag="momx", deferred=True)
        p.op("v", lambda v: v.tensor_tensor(out=T["mxo"][:], in0=T["mxo"][:],
             in1=T["mx1"][:], op=ALU.max), reads=["mxo", "mx1"], writes=["mxo"],
             tag="momxa")
    _gmax_collective(kb, ["mxo"], ["Mog"], "o")
    _scal(kb, lambda v: v.tensor_tensor(out=T["sgo"][:], in0=T["c2"][:],
          in1=T["sgu"][:], op=ALU.mult), ["c2", "sgu"], ["sgo"], "a4a")
    _scal(kb, lambda v: v.tensor_tensor(out=T["Mog"][:], in0=T["Mog"][:],
          in1=T["sgo"][:], op=ALU.mult), ["Mog", "sgo"], ["Mog"], "a4b")
    _scal(kb, lambda v: v.tensor_scalar(out=T["Mog"][:], in0=T["Mog"][:],
          scalar1=1.0 / QMAX, scalar2=1e-12, op0=ALU.mult, op1=ALU.max),
          ["Mog"], ["Mog"], "a4c")
    _scal(kb, lambda v: v.reciprocal(out=T["mx1"][:], in_=T["Mog"][:]),
          ["Mog"], ["mx1"], "a4d")
    _scal(kb, lambda v: v.tensor_tensor(out=T["sgo"][:], in0=T["sgo"][:],
          in1=T["mx1"][:], op=ALU.mult), ["sgo", "mx1"], ["sgo"], "a4e")
    kb.sb("co2", [128, 1], F32)
    _scal(kb, lambda v: v.tensor_tensor(out=T["co2"][:], in0=T["Mog"][:],
          in1=T["Mog"][:], op=ALU.mult), ["Mog"], ["co2"], "a4f")
    _scal(kb, lambda v: v.tensor_scalar(out=T["co2"][:], in0=T["co2"][:],
          scalar1=1.0 / D, scalar2=None, op0=ALU.mult), ["co2"], ["co2"], "a4g")

    emit_ffn_final(kb, x2_d, out_d, uraw_d, graw_d, wdT_d, o_d)
    kb.pop_scope()


def emit_gq_chain(kb, src, t1, t2, p_out, width=4096):
    p, T = kb.p, kb.T
    w = width
    p.op("s", lambda s: s.activation(out=T[t1][:, 0:w], in_=T[src][:, 0:w],
         func=ACT.Identity, scale=T["cg"][:, 0:1], bias=T["magicC"][:, 0:1]),
         reads=[src], writes=[t1], scalar_reads=["cg", "magicC"], tag="gq1")
    p.op("v", lambda v: v.tensor_scalar(out=T[t1][:, 0:w], in0=T[t1][:, 0:w],
         scalar1=MAGIC, scalar2=None, op0=ALU.subtract), reads=[t1],
         writes=[t1], tag="gq2")
    p.op("s", lambda s: s.activation(out=T[t2][:, 0:w], in_=T[t1][:, 0:w],
         func=ACT.Sigmoid, scale=T["sgg"][:, 0:1]), reads=[t1], writes=[t2],
         scalar_reads=["sgg"], tag="gq3")
    p.op("s", lambda s: s.activation(out=T[t2][:, 0:w], in_=T[t2][:, 0:w],
         func=ACT.Identity, scale=65536.0, bias=T["magicC"][:, 0:1]),
         reads=[t2], writes=[t2], scalar_reads=["magicC"], tag="gq4")
    p.op("v", lambda v: v.tensor_scalar(out=T[t2][:, 0:w], in0=T[t2][:, 0:w],
         scalar1=MAGIC, scalar2=65535.0, op0=ALU.subtract, op1=ALU.min),
         reads=[t2], writes=[t2], tag="gq5")
    if p_out:
        p.op("v", lambda v: v.tensor_tensor(out=T[t2][:, 0:w], in0=T[t1][:, 0:w],
             in1=T[t2][:, 0:w], op=ALU.mult), reads=[t1, t2], writes=[t2],
             tag="gq6")


def emit_o_chain(kb, width, from_p=False):
    p, T = kb.p, kb.T
    w = width
    if not from_p:
        emit_gq_chain(kb, "swg", "sw1", "sw2", p_out=True, width=w)
    p.op("s", lambda s: s.activation(out=T["sw2"][:, 0:w], in_=T["sw2"][:, 0:w],
         func=ACT.Identity, scale=T["co"][:, 0:1], bias=T["magicC"][:, 0:1]),
         reads=["sw2"], writes=["sw2"], scalar_reads=["co", "magicC"], tag="oc1")
    p.op("v", lambda v: v.tensor_scalar(out=T["sw2"][:, 0:w], in0=T["sw2"][:, 0:w],
         scalar1=MAGIC, scalar2=None, op0=ALU.subtract), reads=["sw2"],
         writes=["sw2"], tag="oc2")
    p.op("s", lambda s: s.activation(out=T["sw1"][:, 0:w], in_=T["swu"][:, 0:w],
         func=ACT.Identity, scale=T["cu"][:, 0:1], bias=T["magicC"][:, 0:1]),
         reads=["swu"], writes=["sw1"], scalar_reads=["cu", "magicC"], tag="oc3")
    p.op("v", lambda v: v.tensor_scalar(out=T["sw1"][:, 0:w], in0=T["sw1"][:, 0:w],
         scalar1=MAGIC, scalar2=None, op0=ALU.subtract), reads=["sw1"],
         writes=["sw1"], tag="oc4")
    p.op("v", lambda v: v.tensor_tensor(out=T["sw2"][:, 0:w], in0=T["sw2"][:, 0:w],
         in1=T["sw1"][:, 0:w], op=ALU.mult), reads=["sw1", "sw2"], writes=["sw2"],
         tag="oc5")


def emit_ffn_final(kb, x2_d, out_d, uraw_d, graw_d, wdT_d, o_d):
    p, T = kb.p, kb.T
    NSLAB = NTOK // 512
    for sl in range(NSLAB):
        for ic in range(24):
            qb = "sw2" if ic % 2 == 0 else "sw1"
            p.dma("y", lambda y, ic=ic, sl=sl, qb=qb: y.dma_start(
                out=T[qb][:, 0:512],
                in_=o_d[ic * 128:(ic + 1) * 128, sl * 512:(sl + 1) * 512]),
                reads=[f"o_d{ic}"], writes=[qb], tag=f"f4g{sl}_{ic}")
            p.op("s", lambda s, qb=qb: s.activation(out=T[qb][:, 0:512],
                 in_=T[qb][:, 0:512], func=ACT.Identity,
                 scale=T["sgo"][:, 0:1], bias=T["magicC"][:, 0:1]),
                 reads=[qb], writes=[qb], scalar_reads=["sgo", "magicC"],
                 tag=f"f4r{sl}_{ic}")
            p.op("v", lambda v, ic=ic, qb=qb: v.tensor_scalar(
                 out=T["oq_sl"][:, ic * 512:(ic + 1) * 512], in0=T[qb][:, 0:512],
                 scalar1=MAGIC, scalar2=None, op0=ALU.subtract),
                 reads=[qb], writes=["oq_sl"], tag=f"f4q{sl}_{ic}")
        pmap = [("psB", 0), ("psC", 0), ("psC", 1024), ("psA", 0)]
        for ic in range(24):
            buf = "wd_c0" if ic % 2 == 0 else "wd_c1"
            p.dma("g", lambda g, ic=ic, buf=buf: g.dma_start(
                out=T[buf][:], in_=wdT_d[ic * 128:(ic + 1) * 128, :]),
                writes=[buf], tag=f"f4w{sl}_{ic}")
            def dmm(t, ic=ic, buf=buf):
                for tt in range(4):
                    ps, off = pmap[tt]
                    for q in range(2):
                        inst = t.matmul(
                            T[ps][:, off + q * 512:off + (q + 1) * 512],
                            T["oq_sl"][:, ic * 512 + tt * 128:ic * 512 + (tt + 1) * 128],
                            T[buf][:, q * 512:(q + 1) * 512],
                            start=(ic == 0), stop=(ic == 23))
                return inst
            p.op("t", dmm, reads=["oq_sl", buf], writes=["psA", "psB", "psC"],
                 tag=f"f4mm{sl}_{ic}")
        for tt in range(4):
            ps, off = pmap[tt]
            ti = sl * 4 + tt
            sq, r0 = divmod(ti * 128, S)
            p.dma("y", lambda y, sq=sq, r0=r0: y.dma_start(
                out=T["xr"][:], in_=x2_d[sq, r0:r0 + 128, :]), writes=["xr"],
                tag=f"f4x{sl}_{tt}")
            _ln_ops(kb, ps, lambda T, ps=ps, off=off: T[ps][:, off:off + 1024],
                    "of", lambda T: T["of"][:], "m2", "r2", "b2",
                    tag=f"ln4_{sl}_{tt}", scratch="scrF",
                    in_scale="Mog", in_scale_sq="co2")
            p.op("v", lambda v: v.tensor_tensor(out=T["of"][:], in0=T["of"][:],
                 in1=T["xr"][:], op=ALU.add), reads=["of", "xr"], writes=["of"],
                 tag=f"f4res{sl}_{tt}")
            p.dma("g", lambda g, sq=sq, r0=r0: g.dma_start(
                out=out_d[sq, r0:r0 + 128, :], in_=T["of"][:]), reads=["of"],
                tag=f"f4o{sl}_{tt}")


def build_full(kb):
    x2_d, x2ln_d = build_attn(kb)
    out_d = kb.dout("out", [NSEQ, S, D]).ap()
    if kb.stage in ("A", "A1", "B", "B1", "M", "M1", "S0", "S1", "SZ", "SA",
                    "SB", "Sg"):
        final_barrier(kb)
        return
    kb.barrier()
    uraw_d, graw_d = emit_ffn_head(kb, x2_d, x2ln_d)
    emit_ffn_f1(kb, uraw_d, graw_d)
    emit_ffn_rest(kb, x2_d, x2ln_d, out_d, uraw_d, graw_d)
    final_barrier(kb)


_CACHE = {}


def _build():
    if "nc" not in _CACHE:
        nc = bass.Bass("TRN2", target_bir_lowering=False, debug=False)
        kb = KB(nc, stage="full", dbg=[])
        build_full(kb)
        lower(kb)
        _CACHE["nc"] = nc
    return _CACHE["nc"]


def kernel(L0_attn_x, L0_attn_Wq, L0_attn_Wk, L0_attn_Wv, L0_attn_Wa,
           L0_attn_ba, L0_attn_Wb, L0_attn_bb, L0_attn_Wg, L0_attn_Wo,
           L0_attn_g_norm, L0_ffn_w_gate, L0_ffn_w_up, L0_ffn_w_down):
    nc = _build()
    cs = host_constants()
    x = np.ascontiguousarray(np.asarray(L0_attn_x, dtype=np.float32))
    common = {
        "Wq": np.ascontiguousarray(L0_attn_Wq, dtype=np.float32),
        "Wk": np.ascontiguousarray(L0_attn_Wk, dtype=np.float32),
        "Wv": np.ascontiguousarray(L0_attn_Wv, dtype=np.float32),
        "Wg": np.ascontiguousarray(L0_attn_Wg, dtype=np.float32),
        "Wo": np.ascontiguousarray(L0_attn_Wo, dtype=np.float32),
        "Wa": np.ascontiguousarray(L0_attn_Wa, dtype=np.float32),
        "Wb": np.ascontiguousarray(L0_attn_Wb, dtype=np.float32),
        "i_ba_bc": np.ascontiguousarray(
            np.broadcast_to(np.asarray(L0_attn_ba, np.float32), (128, NH))),
        "i_bb_bc": np.ascontiguousarray(
            np.broadcast_to(np.asarray(L0_attn_bb, np.float32), (128, NH))),
        "i_gnorm_bc": np.ascontiguousarray(
            np.broadcast_to(np.asarray(L0_attn_g_norm, np.float32), (128, D))),
        "w_gate_d": np.ascontiguousarray(L0_ffn_w_gate, dtype=np.float32),
        "w_up_d": np.ascontiguousarray(L0_ffn_w_up, dtype=np.float32),
        "w_down_d": np.ascontiguousarray(L0_ffn_w_down, dtype=np.float32),
    }
    for k, v in cs.items():
        if k != "c_indh":
            common["i_" + k] = v
    in_maps = [dict(common, x=np.ascontiguousarray(x[2 * c:2 * c + 2]))
               for c in range(8)]
    res = run_bass_kernel_spmd(nc, in_maps, core_ids=list(range(8)))
    out = np.concatenate(
        [np.asarray(res.results[c]["out"], dtype=np.float32).reshape(NSEQ, S, D)
         for c in range(8)], axis=0)
    return out
